# version 01
# baseline (speedup 1.0000x reference)
import numpy as np
import ml_dtypes
from contextlib import ExitStack
import concourse.bass as bass
import concourse.mybir as mybir
from concourse.bass_utils import run_bass_kernel_spmd

F32 = mybir.dt.float32
BF16 = mybir.dt.bfloat16
AF = mybir.ActivationFunctionType
ALU = mybir.AluOpType
AX = mybir.AxisListType

T = 2048
C = 2048
NH = 16
HD = 128
NL = 64
NE = 8
RK = 16
DH = 5632
NHC = DH // 128
NT = T // 128
TS = 512
NSUB = T // TS
NCORE = 8

SCALE = float(1.0 / np.sqrt(128.0))
NEGM = float(-1e9 / SCALE)
ALPHA = 2.0
EPS = 1e-5
BIGI = 1.0e6

_DEBUG = []


class Sched:
    ENGINES = ("sync", "gpsimd", "tensor", "vector", "scalar")

    def __init__(self, nc, ctx):
        self.nc = nc
        self.ctx = ctx
        self.ops = {e: [] for e in self.ENGINES}
        self.sems = {}
        self.cnt = {}

    def _sem(self, name):
        if name not in self.sems:
            self.sems[name] = self.ctx.enter_context(self.nc.semaphore(name))
            self.cnt[name] = 0
        return self.sems[name]

    def emit(self, engine, fn, waits=(), inc=None, inc_by=1):
        import inspect
        tok = None
        if inc is not None:
            self._sem(inc)
            self.cnt[inc] += inc_by
            tok = (inc, self.cnt[inc])
        ws = tuple(w for w in waits if w is not None)
        for s, _ in ws:
            self._sem(s)
        lineno = inspect.currentframe().f_back.f_lineno
        self.ops[engine].append((ws, fn, inc, inc_by, lineno))
        return tok

    def build(self, block):
        def mk(engine):
            def body(eng):
                last_wait = {}
                for idx, (ws, fn, inc, inc_by, lineno) in enumerate(self.ops[engine]):
                    for (s, v) in ws:
                        if last_wait.get(s, -1) >= v:
                            continue
                        eng.wait_ge(self.sems[s], v)
                        last_wait[s] = v
                    try:
                        inst = fn(eng)
                    except Exception as ex:
                        raise RuntimeError(
                            f"op failed on {engine} idx {idx} (line {lineno}): {ex}"
                        ) from ex
                    if inc is not None:
                        assert inst is not None
                        inst.then_inc(self.sems[inc], inc_by)
            return body
        block.sync(mk("sync"))
        block.gpsimd(mk("gpsimd"))
        block.tensor(mk("tensor"))
        block.vector(mk("vector"))
        block.scalar(mk("scalar"))


def build_program(debug=(), stop_after="E", nsub=NSUB, nhc=NHC, stop_e=None):
    nc = bass.Bass("TRN2", target_bir_lowering=False, debug=False)
    dbg = set(debug)

    d = {}
    def din(name, shape):
        d[name] = nc.dram_tensor(name, list(shape), F32, kind="ExternalInput").ap()
    din("x", (T, C))
    din("wk", (C, C))
    din("wv", (C, C))
    din("wg", (C, NH * NL))
    din("wp", (C, C))
    din("lq", (NH, HD, NL))
    din("gwT", (C, NE))
    din("wupT", (C, DH))
    din("wgT", (C, DH))
    din("wdT", (DH, C))
    din("aup1", (C, 128)); din("aup2", (C, 128))
    din("ag1", (C, 128)); din("ag2", (C, 128))
    din("bup1", (128, DH)); din("bup2", (128, DH))
    din("bg1", (128, DH)); din("bg2", (128, DH))
    din("dAT", (DH, 128))
    din("dBA", (128, C)); din("dBB", (128, C))
    din("idn", (128, 128))
    din("maskp", (128, 128))
    din("cosT", (128, NL)); din("sinT", (128, NL))
    din("iota8", (128, 128)); din("iotaB", (128, 128))
    din("zeros16", (RK, 128))

    out_d = nc.dram_tensor("out", [T, C], F32, kind="ExternalOutput").ap()

    xsp_d = nc.dram_tensor("xspill", [128, NT, C], BF16, kind="Internal").ap()
    bscr_d = [nc.dram_tensor(f"bscr{i}", [128, DH], BF16, kind="Internal").ap()
              for i in range(4)]
    cT_d = nc.dram_tensor("cTd", [128, 128], BF16, kind="Internal").ap()
    st_d = nc.dram_tensor("std", [1, 8], F32, kind="Internal").ap()

    dbg_tensors = {}
    def ddbg(name, shape, dtype=F32):
        if name in dbg:
            dbg_tensors[name] = nc.dram_tensor(
                "dbg_" + name, list(shape), dtype, kind="ExternalOutput").ap()
    ddbg("xn0", (128, C))
    ddbg("glog", (128, 128))
    ddbg("csel", (128, 128))
    ddbg("kf", (128, C))
    ddbg("krope", (128, C), BF16)
    ddbg("vbf", (128, C), BF16)
    ddbg("attn", (128, 8 * 128), BF16)
    ddbg("gfm", (128, 8 * T), BF16)
    ddbg("wtd", (128, NH * TS), BF16)
    ddbg("eg15", (128, NH * NL), BF16)
    ddbg("wgsb", (128, 2048), BF16)
    ddbg("xa0", (128, TS), BF16)
    ddbg("mix0", (128, NHC * TS), BF16)
    ddbg("ya0", (128, 2 * TS), BF16)

    with ExitStack() as ctx:
        E = ctx.enter_context
        S = Sched(nc, ctx)
        GD, SD, TM, VV, SS = "gd", "sd", "tm", "vv", "ss"

        id_bf = E(nc.sbuf_tensor("id_bf", [128, 128], BF16))
        mask_t = E(nc.sbuf_tensor("mask_t", [128, 128], F32))
        cos_t = E(nc.sbuf_tensor("cos_t", [128, NL], F32))
        sin_t = E(nc.sbuf_tensor("sin_t", [128, NL], F32))
        iota_t = E(nc.sbuf_tensor("iota_t", [128, 128], F32))
        iotaB_t = E(nc.sbuf_tensor("iotaB_t", [128, 128], F32))
        z16_t = E(nc.sbuf_tensor("z16_t", [RK, 128], BF16))
        lq_t = E(nc.sbuf_tensor("lq_t", [128, NH * NL], BF16))
        gwT_t = E(nc.sbuf_tensor("gwT_t", [128, 16 * NE], BF16))
        attn_t = E(nc.sbuf_tensor("attn_t", [128, 8 * 128], BF16))
        glog_t = E(nc.sbuf_tensor("glog_t", [128, 128], F32))
        csel_t = E(nc.sbuf_tensor("csel_t", [128, 128], F32))
        cm_t = E(nc.sbuf_tensor("cm_t", [128, 96], F32))
        stt_t = E(nc.sbuf_tensor("stt_t", [128, 16], F32))
        mxb_t = E(nc.sbuf_tensor("mxb_t", [128, 8], F32))

        pA = E(nc.psum_tensor("pA", [128, 1024], F32))
        pB = E(nc.psum_tensor("pB", [128, 1024], F32))
        pC = E(nc.psum_tensor("pC", [128, 1024], F32))
        pT = E(nc.psum_tensor("pT", [128, 2048], BF16))

        blk = E(nc.Block())

        def g_dma(out, in_, waits=()):
            return S.emit("gpsimd", lambda e, o=out, i=in_: e.dma_start(out=o, in_=i),
                          waits=waits, inc=GD, inc_by=16)

        def s_dma(out, in_, waits=()):
            return S.emit("sync", lambda e, o=out, i=in_: e.dma_start(out=o, in_=i),
                          waits=waits, inc=SD, inc_by=16)

        tk_id = g_dma(id_bf[:, :], d["idn"][:, :])
        tk_mask = s_dma(mask_t[:, :], d["maskp"][:, :])
        tk_cos = s_dma(cos_t[:, :], d["cosT"][:, :])
        tk_sin = s_dma(sin_t[:, :], d["sinT"][:, :])
        tk_iota = s_dma(iota_t[:, :], d["iota8"][:, :])
        tk_iotaB = s_dma(iotaB_t[:, :], d["iotaB"][:, :])
        tk_z16 = g_dma(z16_t[:, :], d["zeros16"][:, :])
        tk_lq = g_dma(lq_t.ap().rearrange("p (h l) -> p h l", l=NL),
                      d["lq"].rearrange("h p l -> p h l"))
        tk_gwT = g_dma(gwT_t.ap().rearrange("p (c e) -> p c e", e=NE),
                       d["gwT"].rearrange("(c p) e -> p c e", p=128))

        NEG1 = (32, 64, 96)
        NEG2 = (0, 32, 64, 96)
        with nc.sbuf_tensor("bstg", [128, DH], BF16) as bstg, \
             nc.sbuf_tensor("bneg", [128, DH], BF16) as bneg:
            tok_bscr = None
            tok_bneg = None
            for i, name in enumerate(("bup1", "bup2", "bg1", "bg2")):
                tld = g_dma(bstg[:, :], d[name][:, :],
                            waits=(tok_bscr,) if tok_bscr else ())
                tok_bneg = S.emit("vector",
                                  lambda e: e.tensor_scalar_mul(bneg[:, :], bstg[:, :], -1.0),
                                  waits=(tld,), inc=VV)
                negs = NEG1 if (i % 2 == 0) else NEG2
                for lo in range(0, 128, 16):
                    src = bneg if lo in negs else bstg
                    tok_bscr = s_dma(bscr_d[i][lo:lo + 16, :], src[lo:lo + 16, :],
                                     waits=(tok_bneg,))

        es_gfm = ExitStack()
        Gfm = es_gfm.enter_context(nc.sbuf_tensor("Gfm", [128, 8 * T], BF16))
        es_x = ExitStack()
        X_sb = es_x.enter_context(nc.sbuf_tensor("X_sb", [128, NT * C], BF16))
        with nc.sbuf_tensor("wg_sb", [128, 16 * NH * NL], BF16) as wg_sb, \
             nc.sbuf_tensor("xt0", [128, C], F32) as xt0, \
             nc.sbuf_tensor("xt1", [128, C], F32) as xt1, \
             nc.sbuf_tensor("xn_bf", [128, C], BF16) as xn_bf, \
             nc.sbuf_tensor("E_bf", [128, NH * NL], BF16) as E_bf, \
             nc.sbuf_tensor("Gn_bf", [128, NH * NL], BF16) as Gn_bf, \
             nc.sbuf_tensor("lnsc", [128, 8], F32) as lnsc, \
             nc.sbuf_tensor("sqd", [128, C], F32) as sqd, \
             nc.sbuf_tensor("grec", [128, NH], F32) as grec:

            tok_wg = None
            for c in range(16):
                tok_wg = g_dma(wg_sb[:, c * NH * NL:(c + 1) * NH * NL],
                               d["wg"][c * 128:(c + 1) * 128, :],
                               waits=(tok_bscr, tok_bneg) if c == 0 else ())

            X3 = X_sb.ap().rearrange("p (c t) -> p c t", t=T)
            pT3 = pT.ap().rearrange("p (c t) -> p c t", t=128)
            Gfm3 = Gfm.ap().rearrange("p (h t) -> p h t", t=T)

            tok_x = [None, None]
            tok_xn = [None, None]
            tok_xev = None
            tok_gev = None
            tok_exp = None
            tok_glev = None
            tok_gtr = None
            tok_gn = None

            for it in range(NT):
                pb = it % 2
                xt = (xt0, xt1)[pb]
                tok_x[pb] = s_dma(xt[:, :], d["x"][it * 128:(it + 1) * 128, :],
                                  waits=(tok_xn[pb],))
                S.emit("vector", lambda e, x=xt: e.reduce_sum(
                    lnsc[:, 0:1], x[:, :], axis=AX.X), waits=(tok_x[pb],))
                S.emit("vector", lambda e: e.drain())
                t2 = S.emit("vector", lambda e: e.tensor_scalar_mul(
                    lnsc[:, 1:2], lnsc[:, 0:1], -1.0 / C), inc=VV)
                t3 = S.emit("scalar", lambda e, x=xt: e.activation(
                    sqd[:, :], x[:, :], AF.Square, bias=lnsc[:, 1:2],
                    accum_out=lnsc[:, 2:3]), waits=(t2,), inc=SS)
                S.emit("vector", lambda e: e.tensor_scalar(
                    lnsc[:, 3:4], lnsc[:, 2:3], 1.0 / C, EPS,
                    op0=ALU.mult, op1=ALU.add), waits=(t3,))
                S.emit("vector", lambda e: e.drain())
                t5 = S.emit("vector", lambda e: e.reciprocal(
                    lnsc[:, 4:5], lnsc[:, 3:4]), inc=VV)
                t6 = S.emit("scalar", lambda e: e.activation(
                    lnsc[:, 5:6], lnsc[:, 4:5], AF.Sqrt), waits=(t5,), inc=SS)
                tok_xn[pb] = S.emit("vector", lambda e, x=xt: e.tensor_scalar(
                    xn_bf[:, :], x[:, :], lnsc[:, 1:2], lnsc[:, 5:6],
                    op0=ALU.add, op1=ALU.mult), waits=(t6,), inc=VV)
                if it == 0 and "xn0" in dbg:
                    tdbg = S.emit("vector", lambda e: e.tensor_copy(
                        sqd[:, :], xn_bf[:, :]), waits=(tok_xn[pb],), inc=VV)
                    s_dma(dbg_tensors["xn0"][:, :], sqd[:, :], waits=(tdbg,))
                tt = None
                for c in range(16):
                    tt = S.emit("tensor", lambda e, c=c: e.transpose(
                        pT3[:, c, :], xn_bf[:, c * 128:(c + 1) * 128], id_bf[:, :]),
                        waits=(tok_xn[pb], tk_id, tok_gev) if c == 0 else (),
                        inc=TM if c == 15 else None)
                tok_xev = S.emit("scalar", lambda e, it=it: e.copy(
                    X3[:, :, it * 128:(it + 1) * 128], pT3[:, :, :]),
                    waits=(tt,), inc=SS)
                tg = None
                for c in range(16):
                    tg = S.emit("tensor", lambda e, c=c, it=it: e.matmul(
                        pB[:, 512:520],
                        X_sb[:, c * C + it * 128: c * C + it * 128 + 128],
                        gwT_t[:, c * NE:(c + 1) * NE],
                        start=(c == 0), stop=(c == 15), skip_group_check=True),
                        waits=(tok_xev, tok_glev, tk_gwT) if c == 0 else (),
                        inc=TM if c == 15 else None)
                tok_glev = S.emit("vector", lambda e, it=it: e.tensor_copy(
                    glog_t[:, it * NE:(it + 1) * NE], pB[:, 512:520]),
                    waits=(tg,), inc=VV)
                tG = None
                for c in range(16):
                    for nh in range(2):
                        tG = S.emit("tensor", lambda e, c=c, it=it, nh=nh: e.matmul(
                            pA[:, nh * 512:(nh + 1) * 512],
                            X_sb[:, c * C + it * 128: c * C + it * 128 + 128],
                            wg_sb[:, c * NH * NL + nh * 512: c * NH * NL + (nh + 1) * 512],
                            start=(c == 0), stop=(c == 15), skip_group_check=True),
                            waits=(tok_xev, tok_exp, tok_wg) if (c == 0 and nh == 0) else (),
                            inc=TM if (c == 15 and nh == 1) else None)
                tok_exp = S.emit("scalar", lambda e: e.activation(
                    E_bf[:, :], pA[:, :], AF.Exp, scale=SCALE),
                    waits=(tG, tok_gn), inc=SS)
                S.emit("vector", lambda e: e.reduce_sum(
                    grec.ap().rearrange("p (h o) -> p h o", o=1),
                    E_bf.ap().rearrange("p (h l) -> p h l", l=NL), axis=AX.X),
                    waits=(tok_exp,))
                S.emit("vector", lambda e: e.drain())
                S.emit("vector", lambda e: e.reciprocal(grec[:, :], grec[:, :]), inc=VV)
                S.emit("vector", lambda e: e.drain())
                tok_gn = S.emit("vector", lambda e: e.tensor_tensor(
                    Gn_bf.ap().rearrange("p (h l) -> p h l", l=NL),
                    E_bf.ap().rearrange("p (h l) -> p h l", l=NL),
                    grec.ap().rearrange("p (h o) -> p h o", o=1).to_broadcast([128, NH, NL]),
                    ALU.mult), waits=(tok_gtr,), inc=VV)
                tok_gtr = None
                for pr in range(8):
                    tok_gtr = S.emit("tensor", lambda e, pr=pr: e.transpose(
                        pT3[:, pr, :], Gn_bf[:, pr * 128:(pr + 1) * 128], id_bf[:, :]),
                        waits=(tok_gn, tok_xev) if pr == 0 else (),
                        inc=TM if pr == 7 else None)
                tok_gev = S.emit("scalar", lambda e, it=it: e.copy(
                    Gfm3[:, :, it * 128:(it + 1) * 128], pT3[:, 0:8, :]),
                    waits=(tok_gtr,), inc=SS)

            if "wgsb" in dbg:
                s_dma(dbg_tensors["wgsb"][:, :], wg_sb[:, 0:2048], waits=(tok_wg,))
            if "eg15" in dbg:
                s_dma(dbg_tensors["eg15"][:, :], E_bf[:, :], waits=(tok_gn,))
            tok_A_gev = tok_gev
            tok_A_glev = tok_glev
            tok_A_exp = tok_exp
            tok_A_xev = tok_xev
            tok_A_gtr = tok_gtr
            tok_A_gn = tok_gn
            bA = (tok_A_gev, tok_A_glev, tok_A_exp, tok_A_xev, tok_A_gtr, tok_A_gn)
            if "glog" in dbg:
                s_dma(dbg_tensors["glog"][:, :], glog_t[:, :], waits=(tok_glev,))

        with nc.sbuf_tensor("wkv0", [128, C], BF16) as wkv0, \
             nc.sbuf_tensor("wkv1", [128, C], BF16) as wkv1, \
             nc.sbuf_tensor("kf", [128, C], F32) as kf, \
             nc.sbuf_tensor("k_bf", [128, C], BF16) as k_bf, \
             nc.sbuf_tensor("v_bf", [128, C], BF16) as v_bf, \
             nc.sbuf_tensor("kfm", [128, C], BF16) as kfm, \
             nc.sbuf_tensor("rta", [128, NH * NL], F32) as rta, \
             nc.sbuf_tensor("rtb", [128, NH * NL], F32) as rtb, \
             nc.sbuf_tensor("wsc", [128, 128], F32) as wsc, \
             nc.sbuf_tensor("we", [128, 128], F32) as we, \
             nc.sbuf_tensor("wn", [128, 128], BF16) as wn, \
             nc.sbuf_tensor("wT", [128, 128], BF16) as wT, \
             nc.sbuf_tensor("scs", [128, 8], F32) as scs:

            tok_kv = {}
            for which, wname in (("k", "wk"), ("v", "wv")):
                tok_w = [None, None]
                tchunk = {}
                tmm = None
                for c in range(16):
                    wb = (wkv0, wkv1)[c % 2]
                    extra = (tok_kv["k"],) if (which == "v" and c < 2) else ()
                    tok_w[c % 2] = g_dma(
                        wb[:, :], d[wname][c * 128:(c + 1) * 128, :],
                        waits=(tchunk.get(c - 2),) + extra +
                        (bA if (which == "k" and c < 2) else ()))
                    for nq in range(4):
                        ps = (pA, pA, pB, pB)[nq]
                        sl = (slice(0, 512), slice(512, 1024))[nq % 2]
                        tmm = S.emit("tensor", lambda e, c=c, wb=wb, ps=ps, sl=sl, nq=nq: e.matmul(
                            ps[:, sl], X_sb[:, c * C: c * C + 128],
                            wb[:, nq * 512:(nq + 1) * 512],
                            start=(c == 0), stop=(c == 15), skip_group_check=True),
                            waits=(tok_w[c % 2], tok_A_exp, tok_A_glev,
                                   tok_kv.get(which)) if nq == 0 else (),
                            inc=TM if nq == 3 else None)
                    tchunk[c] = tmm
                if which == "k":
                    S.emit("vector", lambda e: e.tensor_copy(kf[:, 0:1024], pA[:, :]),
                           waits=(tmm,))
                    tok_kv["k"] = S.emit("vector", lambda e: e.tensor_copy(
                        kf[:, 1024:2048], pB[:, :]), inc=VV)
                else:
                    S.emit("vector", lambda e: e.tensor_copy(v_bf[:, 0:1024], pA[:, :]),
                           waits=(tmm,))
                    tok_kv["v"] = S.emit("vector", lambda e: e.tensor_copy(
                        v_bf[:, 1024:2048], pB[:, :]), inc=VV)

            kf3 = kf.ap().rearrange("p (h d) -> p h d", d=HD)
            kb3 = k_bf.ap().rearrange("p (h d) -> p h d", d=HD)
            cosb = cos_t.ap().rearrange("p (o j) -> p o j", o=1).to_broadcast([128, NH, NL])
            sinb = sin_t.ap().rearrange("p (o j) -> p o j", o=1).to_broadcast([128, NH, NL])
            rta3 = rta.ap().rearrange("p (h j) -> p h j", j=NL)
            rtb3 = rtb.ap().rearrange("p (h j) -> p h j", j=NL)
            S.emit("vector", lambda e: e.tensor_tensor(
                rta3[:, :, :], kf3[:, :, 0:NL], cosb, ALU.mult),
                waits=(tok_kv["k"], tk_cos, tk_sin))
            S.emit("vector", lambda e: e.tensor_tensor(
                rtb3[:, :, :], kf3[:, :, NL:HD], sinb, ALU.mult))
            S.emit("vector", lambda e: e.tensor_tensor(
                kb3[:, :, 0:NL], rta3[:, :, :], rtb3[:, :, :], ALU.subtract), inc=VV)
            S.emit("vector", lambda e: e.tensor_tensor(
                rta3[:, :, :], kf3[:, :, 0:NL], sinb, ALU.mult))
            S.emit("vector", lambda e: e.tensor_tensor(
                rtb3[:, :, :], kf3[:, :, NL:HD], cosb, ALU.mult))
            tok_rope = S.emit("vector", lambda e: e.tensor_tensor(
                kb3[:, :, NL:HD], rta3[:, :, :], rtb3[:, :, :], ALU.add), inc=VV)
            if "kf" in dbg:
                s_dma(dbg_tensors["kf"][:, :], kf[:, :], waits=(tok_kv["k"],))
            if "krope" in dbg:
                s_dma(dbg_tensors["krope"][:, :], k_bf[:, :], waits=(tok_rope,))
            if "vbf" in dbg:
                s_dma(dbg_tensors["vbf"][:, :], v_bf[:, :], waits=(tok_kv["v"],))

            pT3b = pT.ap().rearrange("p (h t) -> p h t", t=128)
            tt = None
            for h in range(NH):
                tt = S.emit("tensor", lambda e, h=h: e.transpose(
                    pT3b[:, h, :], k_bf[:, h * 128:(h + 1) * 128], id_bf[:, :]),
                    waits=(tok_rope, tok_A_gev) if h == 0 else (),
                    inc=TM if h == 15 else None)
            tok_kfm = S.emit("scalar", lambda e: e.copy(kfm[:, :], pT[:, :]),
                             waits=(tt,), inc=SS)

            tok_attn_ev = None
            tok_twe = None
            tok_tm1 = None
            for pr in range(8):
                tsc = None
                for sub in range(2):
                    h = pr * 2 + sub
                    tsc = S.emit("tensor", lambda e, h=h, sub=sub: e.matmul(
                        pC[sub * 64:(sub + 1) * 64, 0:128],
                        lq_t[:, h * NL:(h + 1) * NL],
                        kfm[:, h * 128:(h + 1) * 128],
                        start=True, stop=True, skip_group_check=True),
                        waits=(tok_kfm, tk_lq, tok_tm1) if sub == 0 else (),
                        inc=TM if sub == 1 else None)
                tok_tm1 = S.emit("vector", lambda e: e.tensor_add(
                    wsc[:, :], pC[:, 0:128], mask_t[:, :]),
                    waits=(tsc, tk_mask), inc=VV)
                S.emit("vector", lambda e: e.drain())
                tm2 = S.emit("vector", lambda e: e.reduce_max(
                    scs[:, 0:1], wsc[:, :], axis=AX.X, negate=True), inc=VV)
                S.emit("scalar", lambda e: e.mul(scs[:, 1:2], scs[:, 0:1], SCALE),
                       waits=(tm2,))
                S.emit("scalar", lambda e: e.drain())
                tm4 = S.emit("scalar", lambda e: e.activation(
                    we[:, :], wsc[:, :], AF.Exp, bias=scs[:, 1:2], scale=SCALE,
                    accum_out=scs[:, 2:3]), inc=SS)
                S.emit("vector", lambda e: e.reciprocal(scs[:, 3:4], scs[:, 2:3]),
                       waits=(tm4,))
                S.emit("vector", lambda e: e.drain())
                tm6 = S.emit("vector", lambda e: e.tensor_scalar_mul(
                    wn[:, :], we[:, :], scs[:, 3:4]), inc=VV)
                tw = S.emit("tensor", lambda e: e.transpose(
                    pT3b[:, 0, :], wn[:, :], id_bf[:, :]),
                    waits=(tm6, tok_kfm, tok_twe), inc=TM)
                tok_twe = S.emit("scalar", lambda e: e.copy(wT[:, :], pT3b[:, 0, :]),
                                 waits=(tw,), inc=SS)
                ta = None
                for sub in range(2):
                    h = pr * 2 + sub
                    ta = S.emit("tensor", lambda e, h=h, sub=sub: e.matmul(
                        pC[sub * 64:(sub + 1) * 64, 512:640],
                        wT[:, sub * 64:(sub + 1) * 64],
                        v_bf[:, h * 128:(h + 1) * 128],
                        start=True, stop=True, skip_group_check=True),
                        waits=(tok_twe, tok_kv["v"], tok_attn_ev) if sub == 0 else (),
                        inc=TM if sub == 1 else None)
                tok_attn_ev = S.emit("scalar", lambda e, pr=pr: e.copy(
                    attn_t[:, pr * 128:(pr + 1) * 128], pC[:, 512:640]),
                    waits=(ta,), inc=SS)
            if "attn" in dbg:
                s_dma(dbg_tensors["attn"][:, :], attn_t[:, :], waits=(tok_attn_ev,))
            if "gfm" in dbg:
                s_dma(dbg_tensors["gfm"][:, :], Gfm[:, :], waits=(tok_A_gev,))
            tok_B_kv_v = tok_kv["v"]
            tok_B_kfm = tok_kfm
            tok_B_twe = tok_twe

        tok_xspill = s_dma(xsp_d[:, :, :],
                           X_sb.ap().rearrange("p (c t) -> p c t", t=T),
                           waits=(tok_A_xev,))

        g3 = glog_t.ap().rearrange("p (a b) -> p a b", b=NE)
        i3 = iota_t.ap().rearrange("p (a b) -> p a b", b=NE)
        def cmb(lo, hi):
            return cm_t.ap()[:, lo:hi].rearrange(
                "p (a b) -> p a b", b=1).to_broadcast([128, 16, NE])
        def cm3(lo, hi):
            return cm_t.ap()[:, lo:hi].rearrange("p (a b) -> p a b", b=1)

        with nc.sbuf_tensor("eqw", [128, 128], F32) as eqw, \
             nc.sbuf_tensor("ind0", [128, 128], F32) as ind0, \
             nc.sbuf_tensor("ind1", [128, 128], F32) as ind1, \
             nc.sbuf_tensor("mskg", [128, 128], F32) as mskg, \
             nc.sbuf_tensor("sttb", [128, 16], BF16) as sttb, \
             nc.sbuf_tensor("rowsc", [16, 128], F32) as rowsc, \
             nc.sbuf_tensor("cselb", [128, 128], BF16) as cselb:
            eq3 = eqw.ap().rearrange("p (a b) -> p a b", b=NE)
            i03 = ind0.ap().rearrange("p (a b) -> p a b", b=NE)
            i13 = ind1.ap().rearrange("p (a b) -> p a b", b=NE)
            mg3 = mskg.ap().rearrange("p (a b) -> p a b", b=NE)

            S.emit("vector", lambda e: e.tensor_reduce(
                cm3(0, 16), g3[:, :, :], axis=AX.X, op=ALU.max),
                waits=(tok_A_glev, tok_attn_ev))
            S.emit("vector", lambda e: e.drain())
            S.emit("vector", lambda e: e.tensor_tensor(
                eq3[:, :, :], g3[:, :, :], cmb(0, 16), ALU.is_equal))
            S.emit("vector", lambda e: e.scalar_tensor_tensor(
                mskg[:, :], eqw[:, :], -BIGI, iotaB_t[:, :], ALU.mult, ALU.add),
                waits=(tk_iotaB,))
            S.emit("vector", lambda e: e.drain())
            S.emit("vector", lambda e: e.tensor_reduce(
                cm3(64, 80), mg3[:, :, :], axis=AX.X, op=ALU.min))
            S.emit("vector", lambda e: e.drain())
            S.emit("vector", lambda e: e.tensor_tensor(
                i03[:, :, :], i3[:, :, :], cmb(64, 80), ALU.is_equal),
                waits=(tk_iota,))
            S.emit("vector", lambda e: e.scalar_tensor_tensor(
                mskg[:, :], ind0[:, :], -1.0e5, glog_t[:, :], ALU.mult, ALU.add))
            S.emit("vector", lambda e: e.drain())
            S.emit("vector", lambda e: e.tensor_reduce(
                cm3(16, 32), mg3[:, :, :], axis=AX.X, op=ALU.max))
            S.emit("vector", lambda e: e.drain())
            S.emit("vector", lambda e: e.tensor_tensor(
                eq3[:, :, :], mg3[:, :, :], cmb(16, 32), ALU.is_equal))
            S.emit("vector", lambda e: e.scalar_tensor_tensor(
                mskg[:, :], eqw[:, :], -BIGI, iotaB_t[:, :], ALU.mult, ALU.add))
            S.emit("vector", lambda e: e.drain())
            S.emit("vector", lambda e: e.tensor_reduce(
                cm3(64, 80), mg3[:, :, :], axis=AX.X, op=ALU.min))
            S.emit("vector", lambda e: e.drain())
            tok_ind1 = S.emit("vector", lambda e: e.tensor_tensor(
                i13[:, :, :], i3[:, :, :], cmb(64, 80), ALU.is_equal), inc=VV)

            S.emit("vector", lambda e: e.reduce_max(
                stt_t[:, 0:1], cm_t[:, 0:16], axis=AX.X))
            tok_pm = S.emit("vector", lambda e: e.reduce_max(
                stt_t[:, 1:2], cm_t[:, 16:32], axis=AX.X), inc=VV)
            S.emit("vector", lambda e: e.drain())
            tc1 = S.emit("vector", lambda e: e.tensor_copy(
                sttb[:, :], stt_t[:, 0:16]), inc=VV)
            tc2 = S.emit("tensor", lambda e: e.transpose(
                pT.ap()[0:16, 0:128], sttb[:, :], id_bf[:, :]),
                waits=(tc1, tok_B_twe), inc=TM)
            tc3 = S.emit("scalar", lambda e: e.copy(
                rowsc[:, :], pT.ap()[0:16, 0:128]), waits=(tc2,), inc=SS)
            tc4 = S.emit("vector", lambda e: e.reduce_max(
                stt_t[0:2, 2:3], rowsc[0:2, :], axis=AX.X, negate=True),
                waits=(tc3,), inc=VV)
            tc5 = s_dma(st_d[0:1, 0:2].rearrange("o p -> p o"),
                        stt_t.ap()[0:2, 2:3], waits=(tc4,))
            tc6 = s_dma(mxb_t[:, 0:2], st_d.partition_broadcast(128)[:, 0, 0:2],
                        waits=(tc5,))
            tc7 = S.emit("scalar", lambda e: e.activation(
                cm_t[:, 32:48], cm_t[:, 0:16], AF.Exp, bias=mxb_t[:, 0:1]),
                waits=(tc6,))
            tc8 = S.emit("scalar", lambda e: e.activation(
                cm_t[:, 48:64], cm_t[:, 16:32], AF.Exp, bias=mxb_t[:, 1:2]), inc=SS)
            S.emit("vector", lambda e: e.reduce_sum(
                stt_t[:, 4:5], cm_t[:, 32:48], axis=AX.X), waits=(tc8,))
            tok_ps = S.emit("vector", lambda e: e.reduce_sum(
                stt_t[:, 5:6], cm_t[:, 48:64], axis=AX.X), inc=VV)
            S.emit("vector", lambda e: e.drain())
            tc11 = S.emit("vector", lambda e: e.tensor_copy(
                sttb[:, 0:2], stt_t[:, 4:6]), inc=VV)
            tc12 = S.emit("tensor", lambda e: e.transpose(
                pT.ap()[0:16, 128:256], sttb[:, :], id_bf[:, :]),
                waits=(tc11, tc3), inc=TM)
            tc13 = S.emit("scalar", lambda e: e.copy(
                rowsc[:, :], pT.ap()[0:16, 128:256]), waits=(tc12,), inc=SS)
            S.emit("vector", lambda e: e.reduce_sum(
                stt_t[0:2, 6:7], rowsc[0:2, :], axis=AX.X), waits=(tc13,))
            S.emit("vector", lambda e: e.drain())
            tc15 = S.emit("vector", lambda e: e.reciprocal(
                stt_t[0:2, 7:8], stt_t[0:2, 6:7]), inc=VV)
            tc16 = s_dma(st_d[0:1, 2:4].rearrange("o p -> p o"),
                         stt_t.ap()[0:2, 7:8], waits=(tc15,))
            tc17 = s_dma(mxb_t[:, 2:4], st_d.partition_broadcast(128)[:, 0, 2:4],
                         waits=(tc16,))
            S.emit("vector", lambda e: e.tensor_tensor(
                mg3[:, :, :], i03[:, :, :], cmb(32, 48), ALU.mult),
                waits=(tc17, tok_ind1))
            S.emit("vector", lambda e: e.tensor_scalar_mul(
                ind0[:, :], mskg[:, :], mxb_t[:, 2:3]))
            S.emit("vector", lambda e: e.tensor_tensor(
                eq3[:, :, :], i13[:, :, :], cmb(48, 64), ALU.mult))
            S.emit("vector", lambda e: e.tensor_scalar_mul(
                eqw[:, :], eqw[:, :], mxb_t[:, 3:4]))
            u5 = S.emit("vector", lambda e: e.tensor_add(
                csel_t[:, :], ind0[:, :], eqw[:, :]), inc=VV)
            if "csel" in dbg:
                s_dma(dbg_tensors["csel"][:, :], csel_t[:, :], waits=(u5,))
            u6 = S.emit("vector", lambda e: e.tensor_copy(
                cselb[:, :], csel_t[:, :]), waits=(u5,), inc=VV)
            u7 = S.emit("tensor", lambda e: e.transpose(
                pT.ap()[:, 256:384], cselb[:, :], id_bf[:, :]),
                waits=(u6, tc12), inc=TM)
            u8 = S.emit("scalar", lambda e: e.copy(
                cselb[:, :], pT.ap()[:, 256:384]), waits=(u7,), inc=SS)
            tok_cT = s_dma(cT_d[:, :], cselb[:, :], waits=(u8,))
            tok_C_pT = u8
            tok_C_vv = u6
        es_x.close()
        bC = (tok_cT, u8, u7, tok_C_vv, tok_attn_ev, tok_B_kv_v, tok_B_twe)

        if stop_after in ("A", "B", "C"):
            S.emit("sync", lambda e: None,
                   waits=((GD, S.cnt[GD]), (SD, S.cnt[SD]), (VV, S.cnt[VV]),
                          (SS, S.cnt[SS]), (TM, S.cnt[TM])))
            S.build(blk)
            es_gfm.close()
            return nc, sorted(dbg_tensors.keys())

        with nc.sbuf_tensor("wp_sb", [128, 16 * C], BF16) as wp_sb, \
             nc.sbuf_tensor("wtd", [128, NH * TS], BF16) as wtd, \
             nc.sbuf_tensor("xt0d", [128, C], F32) as xt0d, \
             nc.sbuf_tensor("xt1d", [128, C], F32) as xt1d, \
             nc.sbuf_tensor("y1a", [128, C], F32) as y1a, \
             nc.sbuf_tensor("y1b", [128, C], F32) as y1b:
            tok_wp = None
            for c in range(16):
                tok_wp = g_dma(wp_sb[:, c * C:(c + 1) * C],
                               d["wp"][c * 128:(c + 1) * 128, :],
                               waits=(bC + ((GD, S.cnt[GD]),)) if c == 0 else ())
            tok_xd = [None, None]
            tok_y1 = [None, None]
            wtd_ev = {0: None, 1: None}
            tok_wtd_last = None
            tok_D_pBC = None
            for sd in range(NSUB):
                tok_wtd_ev = None
                for h in range(NH):
                    hb = h % 2
                    sl = slice(hb * 512, (hb + 1) * 512)
                    pr, sub = h // 2, h % 2
                    tw = S.emit("tensor", lambda e, pr=pr, sub=sub, sd=sd, sl=sl: e.matmul(
                        pA[:, sl],
                        attn_t[sub * 64:(sub + 1) * 64, pr * 128:(pr + 1) * 128],
                        Gfm[sub * 64:(sub + 1) * 64, pr * T + sd * TS: pr * T + (sd + 1) * TS],
                        start=True, stop=True, skip_group_check=True),
                        waits=bC + (tok_A_gev, wtd_ev[hb],
                                    tok_wtd_last if h <= 1 else None),
                        inc=TM)
                    ev = S.emit("scalar", lambda e, h=h, sl=sl: e.copy(
                        wtd[:, h * TS:(h + 1) * TS], pA[:, sl]), waits=(tw,), inc=SS)
                    wtd_ev[hb] = ev
                    tok_wtd_ev = ev
                if "wtd" in dbg and sd == 0:
                    s_dma(dbg_tensors["wtd"][:, :], wtd[:, :], waits=(tok_wtd_ev,))
                for tc in range(4):
                    it = sd * 4 + tc
                    pbx = it % 2
                    xt = (xt0d, xt1d)[pbx]
                    y1 = (y1a, y1b)[pbx]
                    tok_xd[pbx] = s_dma(xt[:, :], d["x"][it * 128:(it + 1) * 128, :],
                                        waits=(tok_y1[pbx],) + (bC if it < 2 else ()))
                    tmm = None
                    for c in range(16):
                        for dc in range(4):
                            ps = (pB, pB, pC, pC)[dc]
                            sl = (slice(0, 512), slice(512, 1024))[dc % 2]
                            tmm = S.emit("tensor", lambda e, c=c, dc=dc, tc=tc, ps=ps, sl=sl: e.matmul(
                                ps[:, sl],
                                wtd[:, c * TS + tc * 128: c * TS + tc * 128 + 128],
                                wp_sb[:, c * C + dc * 512: c * C + (dc + 1) * 512],
                                start=(c == 0), stop=(c == 15), skip_group_check=True),
                                waits=(tok_wtd_ev, tok_wp, tok_B_kv_v, tok_attn_ev,
                                       tok_D_pBC) if (c == 0 and dc == 0) else (),
                                inc=TM if (c == 15 and dc == 3) else None)
                    if tc == 3:
                        tok_wtd_last = tmm
                    S.emit("vector", lambda e, y1=y1, xt=xt: e.tensor_add(
                        y1[:, 0:1024], pB[:, :], xt[:, 0:1024]),
                        waits=(tmm, tok_xd[pbx]))
                    tok_D_pBC = S.emit("vector", lambda e, y1=y1, xt=xt: e.tensor_add(
                        y1[:, 1024:2048], pC[:, :], xt[:, 1024:2048]), inc=VV)
                    tok_y1[pbx] = s_dma(out_d[it * 128:(it + 1) * 128, :], y1[:, :],
                                        waits=(tok_D_pBC,))
            tok_D_y1_last = tok_y1[1] if tok_y1[1] else tok_y1[0]
            tok_D_pA = wtd_ev
        es_gfm.close()
        bD = (tok_D_y1_last, tok_D_pBC, tok_wtd_last, tok_D_pA[0], tok_D_pA[1],
              tok_wp)

        if stop_after == "D":
            S.emit("sync", lambda e: None,
                   waits=((GD, S.cnt[GD]), (SD, S.cnt[SD]), (VV, S.cnt[VV]),
                          (SS, S.cnt[SS]), (TM, S.cnt[TM])))
            S.build(blk)
            return nc, sorted(dbg_tensors.keys())

        XsE = E(nc.sbuf_tensor("XsE", [128, 16 * TS], BF16))
        mixed = E(nc.sbuf_tensor("mixed", [128, NHC * TS], BF16))
        cbc = E(nc.sbuf_tensor("cbc", [128, NE * TS], BF16))
        xa_sb = [E(nc.sbuf_tensor(f"xa{j}", [128, TS], BF16)) for j in range(4)]
        A_sb = [E(nc.sbuf_tensor(f"A_sb{j}", [128, 16 * 128], BF16)) for j in range(4)]
        wup_sl = [E(nc.sbuf_tensor(f"wup{j}", [128, 16 * 128], BF16)) for j in range(2)]
        wg_sl = [E(nc.sbuf_tensor(f"wgs{j}", [128, 16 * 128], BF16)) for j in range(2)]
        b_sl = [E(nc.sbuf_tensor(f"bsl{j}", [128, 4 * 128], BF16)) for j in range(2)]
        dA_sl = [E(nc.sbuf_tensor(f"dAs{j}", [128, 128], BF16)) for j in range(2)]
        wdn_sl = [E(nc.sbuf_tensor(f"wdn{j}", [128, NHC * 128], BF16)) for j in range(2)]
        su_bf = E(nc.sbuf_tensor("su_bf", [128, TS], BF16))
        h_bf = E(nc.sbuf_tensor("h_bf", [128, TS], BF16))
        ch_bf = [E(nc.sbuf_tensor(f"ch{j}", [128, TS], BF16)) for j in range(2)]
        macc = [E(nc.sbuf_tensor(f"macc{j}", [128, TS], F32)) for j in range(2)]
        ya_sb = E(nc.sbuf_tensor("ya_sb", [128, 2 * TS], BF16))
        dB_sb = [E(nc.sbuf_tensor(f"dB{j}", [128, C], BF16)) for j in range(2)]
        dout = E(nc.sbuf_tensor("dout", [128, 16 * TS], BF16))
        y1e = [E(nc.sbuf_tensor(f"y1e{j}", [128, C], F32)) for j in range(2)]
        oste = [E(nc.sbuf_tensor(f"ost{j}", [128, C], F32)) for j in range(2)]

        tok_A_ld = []
        for j, nm in enumerate(("aup1", "aup2", "ag1", "ag2")):
            ta = None
            for c in range(16):
                ta = g_dma(A_sb[j][:, c * 128:(c + 1) * 128],
                           d[nm][c * 128:(c + 1) * 128, :],
                           waits=bD if (j == 0 and c == 0) else ())
            tok_A_ld.append(ta)
        tok_dB = [g_dma(dB_sb[0][:, :], d["dBA"][:, :]),
                  g_dma(dB_sb[1][:, :], d["dBB"][:, :])]

        TRANS = [(0, 0, 16), (0, 32, 32), (0, 64, 32), (0, 96, 32),
                 (1, 0, 32), (1, 32, 32), (1, 64, 32), (1, 96, 32)]

        tok_Xs = None
        tok_base_last = {0: None, 1: None}
        tok_xa_rd = None
        tok_cb_rd = None
        tok_ya_ev = None
        tok_mix_rd = None
        tok_out = []
        tok_w_used = {0: None, 1: None}
        tok_b_used = {0: None, 1: None}
        tok_dA_used = {0: None, 1: None}
        tok_wdn_used = {0: None, 1: None}
        tok_final_add = {0: None, 1: None}
        tok_tfa_prev = None
        prev_dn_ev = {0: None, 1: None}

        for s in range(nsub):
            tok_Xs = None
            for c in range(16):
                tok_Xs = s_dma(XsE[:, c * TS:(c + 1) * TS],
                               xsp_d[:, c, s * TS:(s + 1) * TS],
                               waits=((tok_xspill, tok_base_last[0],
                                       tok_base_last[1], tok_xa_rd) +
                                      (bD if s == 0 else ())) if c == 0 else ())
            tcb = None
            for e8 in range(NE):
                for q in range(4):
                    row = (4 * s + q) * NE + e8
                    tcb = s_dma(cbc[:, e8 * TS + q * 128: e8 * TS + (q + 1) * 128],
                                cT_d.partition_broadcast(128)[:, row, :],
                                waits=((tok_cT, tok_cb_rd) + (bD if s == 0 else ()))
                                if (e8 == 0 and q == 0) else ())
            tok_cbc = tcb
            tok_xa_ev = []
            for j in range(4):
                half = j % 2
                tj = None
                for c in range(16):
                    tj = S.emit("tensor", lambda e, j=j, c=c, half=half: e.matmul(
                        pA[:, half * 512:(half + 1) * 512],
                        A_sb[j][:, c * 128:(c + 1) * 128],
                        XsE[:, c * TS:(c + 1) * TS],
                        start=(c == 0), stop=(c == 15), skip_group_check=True),
                        waits=(tok_Xs, tok_A_ld[j],
                               tok_xa_ev[j - 2] if j >= 2 else
                               (tok_D_pA[half] if s == 0 else None)) if c == 0 else (),
                        inc=TM if c == 15 else None)
                ev = S.emit("scalar", lambda e, j=j, half=half: e.activation(
                    xa_sb[j][:, :], pA[:, half * 512:(half + 1) * 512],
                    AF.Copy, scale=ALPHA), waits=(tj,), inc=SS)
                tok_xa_ev.append(ev)
            if "xa0" in dbg and s == 0:
                s_dma(dbg_tensors["xa0"][:, :], xa_sb[0][:, :], waits=(tok_xa_ev[0],))


            if stop_e == "xa":
                S.emit("sync", lambda e: None,
                       waits=((GD, S.cnt[GD]), (SD, S.cnt[SD]), (VV, S.cnt[VV]),
                              (SS, S.cnt[SS]), (TM, S.cnt[TM])))
                S.build(blk)
                return nc, sorted(dbg_tensors.keys())
            S.emit("tensor", lambda e: e.matmul(
                pC[:, 0:512], z16_t[:, :], xa_sb[0][0:16, :],
                start=True, stop=True, skip_group_check=True),
                waits=(tok_xa_ev[0], tk_z16, tok_ya_ev,
                       tok_D_pBC if s == 0 else None, tok_attn_ev if s == 0 else None))
            tok_zya = S.emit("tensor", lambda e: e.matmul(
                pC[:, 512:1024], z16_t[:, :], xa_sb[0][0:16, :],
                start=True, stop=True, skip_group_check=True), inc=TM)


            if stop_e == "zero":
                S.emit("sync", lambda e: None,
                       waits=((GD, S.cnt[GD]), (SD, S.cnt[SD]), (VV, S.cnt[VV]),
                              (SS, S.cnt[SS]), (TM, S.cnt[TM])))
                S.build(blk)
                return nc, sorted(dbg_tensors.keys())
            tok_silu = {}
            tok_h = {}
            tok_ya_par = {0: None, 1: None}
            tok_ya = None
            tok_trans_g_last = None
            tok_mix_last = None
            for hc in range(nhc):
                pb = hc % 2
                usl = slice(pb * 512, (pb + 1) * 512)
                tok_w = g_dma(
                    wup_sl[pb].ap().rearrange("p (c m) -> p c m", m=128),
                    d["wupT"].rearrange("(c p) h -> p c h", p=128)[:, :, hc * 128:(hc + 1) * 128],
                    waits=(tok_w_used[pb],) + (bD if (s == 0 and hc < 2) else ()))
                tok_w2 = g_dma(
                    wg_sl[pb].ap().rearrange("p (c m) -> p c m", m=128),
                    d["wgT"].rearrange("(c p) h -> p c h", p=128)[:, :, hc * 128:(hc + 1) * 128])
                tb = None
                for i in range(4):
                    tb = s_dma(b_sl[pb][:, i * 128:(i + 1) * 128],
                               bscr_d[i][:, hc * 128:(hc + 1) * 128],
                               waits=((tok_bscr, tok_b_used[pb]) +
                                      (bD if (s == 0 and hc < 2) else ()))
                               if i == 0 else ())
                tok_b = tb
                tok_dA = g_dma(dA_sl[pb][:, :], d["dAT"][hc * 128:(hc + 1) * 128, :],
                               waits=(tok_dA_used[pb],) +
                               (bD if (s == 0 and hc < 2) else ()))
                for c in range(16):
                    S.emit("tensor", lambda e, c=c, usl=usl, pb=pb: e.matmul(
                        pA[:, usl], wup_sl[pb][:, c * 128:(c + 1) * 128],
                        XsE[:, c * TS:(c + 1) * TS],
                        start=(c == 0), stop=False, skip_group_check=True),
                        waits=(tok_w, tok_Xs,
                               tok_silu.get((hc - 2, 7)) if hc >= 2 else
                               tok_xa_ev[2 + pb]) if c == 0 else ())
                tb2 = None
                for c in range(16):
                    tb2 = S.emit("tensor", lambda e, c=c, usl=usl, pb=pb: e.matmul(
                        pB[:, usl], wg_sl[pb][:, c * 128:(c + 1) * 128],
                        XsE[:, c * TS:(c + 1) * TS],
                        start=(c == 0), stop=False, skip_group_check=True),
                        waits=(tok_w2,
                               tok_h.get((hc - 2, 7)) if hc >= 2 else None,
                               prev_dn_ev[pb] if (s > 0 and hc < 2) else None,
                               tok_D_pBC if (s == 0 and hc < 2) else None) if c == 0 else (),
                        inc=TM if c == 15 else None)
                tok_base_last[pb] = tb2

                for e8 in range(NE):
                    ti, rlo, nr = TRANS[e8]
                    kw = {"tile_position": (96, 0)} if rlo == 96 else {}
                    tup = S.emit("tensor", lambda e, ti=ti, rlo=rlo, nr=nr, pb=pb,
                                 usl=usl, kw=dict(kw): e.matmul(
                        pA[:, usl],
                        b_sl[pb][rlo:rlo + nr, ti * 128:(ti + 1) * 128],
                        xa_sb[ti][rlo:rlo + nr, :],
                        start=False, stop=False, skip_group_check=True, **kw),
                        waits=(tok_b, tok_xa_ev[ti],
                               tok_silu.get((hc, e8 - 1)) if e8 > 0 else None),
                        inc=TM)
                    tgt = S.emit("tensor", lambda e, ti=ti, rlo=rlo, nr=nr, pb=pb,
                                 usl=usl, kw=dict(kw): e.matmul(
                        pB[:, usl],
                        b_sl[pb][rlo:rlo + nr, (2 + ti) * 128:(3 + ti) * 128],
                        xa_sb[2 + ti][rlo:rlo + nr, :],
                        start=False, stop=False, skip_group_check=True, **kw),
                        waits=(tok_xa_ev[2 + ti],
                               tok_h.get((hc, e8 - 1)) if e8 > 0 else None),
                        inc=TM)
                    if e8 == 7:
                        tok_trans_g_last = tgt
                    ts1 = S.emit("scalar", lambda e, usl=usl: e.activation(
                        su_bf[:, :], pA[:, usl], AF.Silu),
                        waits=(tup, tok_h.get((hc, e8 - 1)) if e8 > 0 else
                               tok_h.get((hc - 1, 7))), inc=SS)
                    tok_silu[(hc, e8)] = ts1
                    th1 = S.emit("vector", lambda e, usl=usl: e.tensor_mul(
                        h_bf[:, :], su_bf[:, :], pB[:, usl]),
                        waits=(ts1, tgt), inc=VV)
                    tok_h[(hc, e8)] = th1
                    chb = ch_bf[e8 % 2]
                    tch = S.emit("vector", lambda e, e8=e8, chb=chb: e.tensor_mul(
                        chb[:, :], h_bf[:, :], cbc[:, e8 * TS:(e8 + 1) * TS]),
                        waits=(tok_cbc, tok_ya_par[e8 % 2]), inc=VV)
                    if e8 == 0:
                        S.emit("vector", lambda e, chb=chb: e.tensor_copy(
                            macc[0][:, :], chb[:, :]))
                    elif e8 < 7:
                        S.emit("vector", lambda e, e8=e8, chb=chb: e.tensor_add(
                            macc[e8 % 2][:, :], macc[(e8 - 1) % 2][:, :], chb[:, :]))
                    else:
                        tok_mix_last = S.emit(
                            "vector", lambda e, hc=hc, chb=chb: e.tensor_add(
                                mixed[:, hc * TS:(hc + 1) * TS], macc[0][:, :], chb[:, :]),
                            waits=(tok_mix_rd if (s > 0 and hc == 0) else None,),
                            inc=VV)
                    grp, off = e8 // 4, (e8 % 4) * 32
                    kw2 = {"tile_position": (0, 96)} if off == 96 else {}
                    tok_ya = S.emit("tensor", lambda e, grp=grp, off=off, e8=e8, pb=pb,
                                    kw2=dict(kw2): e.matmul(
                        pC[:, grp * 512:(grp + 1) * 512][off:off + 16, :],
                        dA_sl[pb][:, e8 * 16:(e8 + 1) * 16],
                        ch_bf[e8 % 2][:, :],
                        start=False, stop=(hc == nhc - 1 and e8 == 7),
                        skip_group_check=True, **kw2),
                        waits=(tch, tok_dA, tok_zya), inc=TM)
                    tok_ya_par[e8 % 2] = tok_ya
                tok_w_used[pb] = tok_base_last[pb]
                tok_b_used[pb] = tok_trans_g_last
                tok_dA_used[pb] = tok_ya
            tok_xa_rd = tok_trans_g_last
            tok_cb_rd = tok_h.get((NHC - 1, 7))
            tok_cb_rd = None
            tok_cb_rd = tch
            if "mix0" in dbg and s == 0:
                s_dma(dbg_tensors["mix0"][:, :], mixed[:, :], waits=(tok_mix_last,))

            S.emit("scalar", lambda e: e.activation(
                ya_sb[:, 0:TS], pC[:, 0:512], AF.Copy, scale=ALPHA),
                waits=(tok_ya,))
            tok_ya_ev = S.emit("scalar", lambda e: e.activation(
                ya_sb[:, TS:2 * TS], pC[:, 512:1024], AF.Copy, scale=ALPHA), inc=SS)
            if "ya0" in dbg and s == 0:
                s_dma(dbg_tensors["ya0"][:, :], ya_sb[:, :], waits=(tok_ya_ev,))


            if stop_e == "hloop":
                S.emit("sync", lambda e: None,
                       waits=((GD, S.cnt[GD]), (SD, S.cnt[SD]), (VV, S.cnt[VV]),
                              (SS, S.cnt[SS]), (TM, S.cnt[TM])))
                S.build(blk)
                return nc, sorted(dbg_tensors.keys())
            tok_dn_ev = {}
            for dc in range(16):
                dpb = dc % 2
                dsl = slice(dpb * 512, (dpb + 1) * 512)
                tok_wdn = g_dma(
                    wdn_sl[dpb].ap().rearrange("p (c m) -> p c m", m=128),
                    d["wdT"].rearrange("(c p) n -> p c n", p=128)[:, :, dc * 128:(dc + 1) * 128],
                    waits=(tok_wdn_used[dpb],) +
                    (bD if (s == 0 and dc < 2) else ()))
                for hc in range(nhc):
                    S.emit("tensor", lambda e, hc=hc, dpb=dpb, dsl=dsl: e.matmul(
                        pB[:, dsl], wdn_sl[dpb][:, hc * 128:(hc + 1) * 128],
                        mixed[:, hc * TS:(hc + 1) * TS],
                        start=(hc == 0), stop=False, skip_group_check=True),
                        waits=(tok_wdn, tok_mix_last,
                               tok_h.get((nhc - 1, 7)) if dc == 0 else None,
                               tok_dn_ev.get(dc - 2)) if hc == 0 else ())
                tdn = None
                for gi in range(2):
                    tdn = S.emit("tensor", lambda e, gi=gi, dc=dc, dsl=dsl: e.matmul(
                        pB[:, dsl], dB_sb[gi][:, dc * 128:(dc + 1) * 128],
                        ya_sb[:, gi * TS:(gi + 1) * TS],
                        start=False, stop=(gi == 1), skip_group_check=True),
                        waits=(tok_ya_ev, tok_dB[gi]) if gi == 0 else (),
                        inc=TM if gi == 1 else None)
                tok_wdn_used[dpb] = tdn
                tok_dn_ev[dc] = S.emit("scalar", lambda e, dc=dc, dsl=dsl: e.copy(
                    dout[:, dc * TS:(dc + 1) * TS], pB[:, dsl]), waits=(tdn,), inc=SS)
            tok_mix_rd = tdn
            prev_dn_ev = {0: tok_dn_ev[14], 1: tok_dn_ev[15]}


            if stop_e == "down":
                S.emit("sync", lambda e: None,
                       waits=((GD, S.cnt[GD]), (SD, S.cnt[SD]), (VV, S.cnt[VV]),
                              (SS, S.cnt[SS]), (TM, S.cnt[TM])))
                S.build(blk)
                return nc, sorted(dbg_tensors.keys())
            pTd = pT.ap().rearrange("p (c t) -> p c t", t=128)
            for tc in range(4):
                fp = tc % 2
                it = s * 4 + tc
                ty = s_dma(y1e[fp][:, :], out_d[it * 128:(it + 1) * 128, :],
                           waits=(tok_D_y1_last,
                                  tok_out[-2] if len(tok_out) >= 2 else None) +
                           (bD if (s == 0 and tc < 2) else ()))
                ttr = None
                for dc in range(16):
                    ttr = S.emit("tensor", lambda e, dc=dc, tc=tc: e.transpose(
                        pTd[:, dc, :], dout[:, dc * TS + tc * 128: dc * TS + tc * 128 + 128],
                        id_bf[:, :]),
                        waits=(tok_dn_ev[15], tok_tfa_prev,
                               tok_C_pT if (s == 0 and tc == 0) else None)
                        if dc == 0 else (),
                        inc=TM if dc == 15 else None)
                if stop_e == "fin1":
                    S.emit("sync", lambda e: None,
                           waits=((GD, S.cnt[GD]), (SD, S.cnt[SD]), (VV, S.cnt[VV]),
                                  (SS, S.cnt[SS]), (TM, S.cnt[TM])))
                    S.build(blk)
                    return nc, sorted(dbg_tensors.keys())
                tfa = S.emit("vector", lambda e, fp=fp: e.tensor_add(
                    oste[fp][:, :], y1e[fp][:, :], pT[:, :]),
                    waits=(ttr, ty), inc=VV)
                tok_tfa_prev = tfa
                if stop_e == "fin2":
                    S.emit("sync", lambda e: None,
                           waits=((GD, S.cnt[GD]), (SD, S.cnt[SD]), (VV, S.cnt[VV]),
                                  (SS, S.cnt[SS]), (TM, S.cnt[TM])))
                    S.build(blk)
                    return nc, sorted(dbg_tensors.keys())
                tok_final_add[fp] = tfa
                tok_out.append(s_dma(out_d[it * 128:(it + 1) * 128, :], oste[fp][:, :],
                                     waits=(tfa,)))

        S.emit("sync", lambda e: None,
               waits=tuple(tok_out[-2:]) + ((GD, S.cnt[GD]), (VV, S.cnt[VV]),
                                            (SS, S.cnt[SS]), (TM, S.cnt[TM])))

        S.build(blk)

    return nc, sorted(dbg_tensors.keys())


def _prep_inputs(inp):
    f32 = np.float32
    Wk = np.asarray(inp["L0_attn_Wk"], f32)
    Wv = np.asarray(inp["L0_attn_Wv"], f32)
    Wg = np.asarray(inp["L0_attn_Wg"], f32)
    Wp = np.asarray(inp["L0_attn_Wp"], f32)
    lq = np.asarray(inp["L0_attn_latent_queries"], f32)
    gw = np.asarray(inp["L0_ffn_gate_w"], f32)
    w_up = np.asarray(inp["L0_ffn_w_up"], f32)
    w_gate = np.asarray(inp["L0_ffn_w_gate"], f32)
    w_down = np.asarray(inp["L0_ffn_w_down"], f32)
    up_A = np.asarray(inp["L0_ffn_up_A"], f32)
    up_B = np.asarray(inp["L0_ffn_up_B"], f32)
    gate_A = np.asarray(inp["L0_ffn_gate_A"], f32)
    gate_B = np.asarray(inp["L0_ffn_gate_B"], f32)
    down_A = np.asarray(inp["L0_ffn_down_A"], f32)
    down_B = np.asarray(inp["L0_ffn_down_B"], f32)

    permd = np.concatenate([np.arange(0, HD, 2), np.arange(1, HD, 2)])
    perm = np.concatenate([h * HD + permd for h in range(NH)])
    wk_p = np.ascontiguousarray(Wk[:, perm])
    lq_p = np.ascontiguousarray(lq[0].transpose(1, 2, 0)[:, permd, :])

    def a_dup(A):
        T1 = np.zeros((C, 128), f32)
        T2 = np.zeros((C, 128), f32)
        T1[:, 0:16] = A[0].T
        T1[:, 32:48] = A[0].T; T1[:, 48:64] = A[1].T
        T1[:, 64:80] = A[1].T; T1[:, 80:96] = A[2].T
        T1[:, 96:112] = A[2].T; T1[:, 112:128] = A[3].T
        T2[:, 0:16] = A[3].T; T2[:, 16:32] = A[4].T
        T2[:, 32:48] = A[4].T; T2[:, 48:64] = A[5].T
        T2[:, 64:80] = A[5].T; T2[:, 80:96] = A[6].T
        T2[:, 96:112] = A[6].T; T2[:, 112:128] = A[7].T
        return np.ascontiguousarray(T1), np.ascontiguousarray(T2)

    def b_dup(B):
        Bt = B.transpose(0, 2, 1)
        T1 = np.zeros((128, DH), f32)
        T2 = np.zeros((128, DH), f32)
        T1[0:16] = Bt[0]
        T1[32:48] = Bt[0]; T1[48:64] = Bt[1]
        T1[64:80] = Bt[1]; T1[80:96] = Bt[2]
        T1[96:112] = Bt[2]; T1[112:128] = Bt[3]
        T2[0:16] = Bt[3]; T2[16:32] = Bt[4]
        T2[32:48] = Bt[4]; T2[48:64] = Bt[5]
        T2[64:80] = Bt[5]; T2[80:96] = Bt[6]
        T2[96:112] = Bt[6]; T2[112:128] = Bt[7]
        return np.ascontiguousarray(T1), np.ascontiguousarray(T2)

    aup1, aup2 = a_dup(up_A)
    ag1, ag2 = a_dup(gate_A)
    bup1, bup2 = b_dup(up_B)
    bg1, bg2 = b_dup(gate_B)
    dAT = np.ascontiguousarray(down_A.transpose(2, 0, 1).reshape(DH, 128))
    dBA = np.zeros((128, C), f32)
    dBB = np.zeros((128, C), f32)
    for e in range(4):
        dBA[32 * e:32 * e + 16] = down_B[e].T
        dBB[32 * e:32 * e + 16] = down_B[4 + e].T

    idn = np.eye(128, dtype=f32)
    lrow = np.arange(128, dtype=np.int64) % 64
    trow = np.arange(128, dtype=np.int64)
    maskp = np.where(trow[None, :] <= lrow[:, None], 0.0, NEGM).astype(f32)
    inv_freq = (1.0 / (10000.0 ** (np.arange(0, NL).astype(f32) * 2.0 / HD))).astype(f32)
    ang = (np.arange(128).astype(f32)[:, None] * inv_freq[None, :]).astype(f32)
    cosT = np.cos(ang).astype(f32)
    sinT = np.sin(ang).astype(f32)
    iota8 = np.ascontiguousarray(
        np.tile(np.arange(NE, dtype=f32), (128, 16)).reshape(128, 128))
    iotaB = (iota8 + BIGI).astype(f32)
    zeros16 = np.zeros((RK, 128), f32)

    return dict(
        wk=wk_p, wv=np.ascontiguousarray(Wv), wg=np.ascontiguousarray(Wg),
        wp=np.ascontiguousarray(Wp), lq=lq_p,
        gwT=np.ascontiguousarray(gw.T),
        wupT=np.ascontiguousarray(w_up.T), wgT=np.ascontiguousarray(w_gate.T),
        wdT=np.ascontiguousarray(w_down.T),
        aup1=aup1, aup2=aup2, ag1=ag1, ag2=ag2,
        bup1=bup1, bup2=bup2, bg1=bg1, bg2=bg2,
        dAT=dAT, dBA=dBA, dBB=dBB,
        idn=idn, maskp=maskp, cosT=cosT, sinT=sinT,
        iota8=iota8, iotaB=iotaB, zeros16=zeros16,
    )


def kernel(**inputs):
    x = np.asarray(inputs["L0_attn_x"], np.float32)
    shared = _prep_inputs(inputs)
    nc, dbg_names = build_program(debug=_DEBUG)
    in_maps = []
    for i in range(NCORE):
        m = dict(shared)
        m["x"] = np.ascontiguousarray(x[i])
        in_maps.append(m)
    res = run_bass_kernel_spmd(nc, in_maps, core_ids=list(range(NCORE)))
    out = np.stack([res.results[i]["out"] for i in range(NCORE)], axis=0)
    if _DEBUG:
        kernel._dbg = [{n: res.results[i]["dbg_" + n] for n in dbg_names}
                       for i in range(NCORE)]
    return out


# version 02
# speedup vs baseline: 1.3082x; 1.3082x over previous
import numpy as np
import ml_dtypes
from contextlib import ExitStack
import concourse.bass as bass
import concourse.mybir as mybir
from concourse.bass_utils import run_bass_kernel_spmd

F32 = mybir.dt.float32
BF16 = mybir.dt.bfloat16
AF = mybir.ActivationFunctionType
ALU = mybir.AluOpType
AX = mybir.AxisListType

T = 2048
C = 2048
NH = 16
HD = 128
NL = 64
NE = 8
RK = 16
DH = 5632
NHC = DH // 128
NT = T // 128
TS = 512
NSUB = T // TS
NCORE = 8

SCALE = float(1.0 / np.sqrt(128.0))
NEGM = float(-1e9 / SCALE)
ALPHA = 2.0
EPS = 1e-5
BIGI = 1.0e6

_DEBUG = []


class Sched:
    ENGINES = ("sync", "gpsimd", "tensor", "vector", "scalar")

    def __init__(self, nc, ctx):
        self.nc = nc
        self.ctx = ctx
        self.ops = {e: [] for e in self.ENGINES}
        self.sems = {}
        self.cnt = {}

    def _sem(self, name):
        if name not in self.sems:
            self.sems[name] = self.ctx.enter_context(self.nc.semaphore(name))
            self.cnt[name] = 0
        return self.sems[name]

    def emit(self, engine, fn, waits=(), inc=None, inc_by=1):
        import inspect
        tok = None
        if inc is not None:
            self._sem(inc)
            self.cnt[inc] += inc_by
            tok = (inc, self.cnt[inc])
        ws = tuple(w for w in waits if w is not None)
        for s, _ in ws:
            self._sem(s)
        lineno = inspect.currentframe().f_back.f_lineno
        self.ops[engine].append((ws, fn, inc, inc_by, lineno))
        return tok

    def build(self, block):
        def mk(engine):
            def body(eng):
                last_wait = {}
                for idx, (ws, fn, inc, inc_by, lineno) in enumerate(self.ops[engine]):
                    for (s, v) in ws:
                        if last_wait.get(s, -1) >= v:
                            continue
                        eng.wait_ge(self.sems[s], v)
                        last_wait[s] = v
                    try:
                        inst = fn(eng)
                    except Exception as ex:
                        raise RuntimeError(
                            f"op failed on {engine} idx {idx} (line {lineno}): {ex}"
                        ) from ex
                    if inc is not None:
                        assert inst is not None
                        inst.then_inc(self.sems[inc], inc_by)
            return body
        block.sync(mk("sync"))
        block.gpsimd(mk("gpsimd"))
        block.tensor(mk("tensor"))
        block.vector(mk("vector"))
        block.scalar(mk("scalar"))


def build_program(debug=(), stop_after="E", nsub=NSUB, nhc=NHC, stop_e=None):
    nc = bass.Bass("TRN2", target_bir_lowering=False, debug=False)
    dbg = set(debug)

    d = {}
    def din(name, shape):
        d[name] = nc.dram_tensor(name, list(shape), F32, kind="ExternalInput").ap()
    din("x", (T, C))
    din("wk", (C, C))
    din("wv", (C, C))
    din("wg", (C, NH * NL))
    din("wp", (C, C))
    din("lq", (NH, HD, NL))
    din("gwT", (C, NE))
    din("wupT", (C, DH))
    din("wgT", (C, DH))
    din("wdT", (DH, C))
    din("aup1", (C, 128)); din("aup2", (C, 128))
    din("ag1", (C, 128)); din("ag2", (C, 128))
    din("bup1", (128, DH)); din("bup2", (128, DH))
    din("bg1", (128, DH)); din("bg2", (128, DH))
    din("dAT", (DH, 128))
    din("dBA", (128, C)); din("dBB", (128, C))
    din("idn", (128, 128))
    din("maskp", (128, 128))
    din("cosT", (128, NL)); din("sinT", (128, NL))
    din("iota8", (128, 128)); din("iotaB", (128, 128))
    din("zeros16", (RK, 128))

    out_d = nc.dram_tensor("out", [T, C], F32, kind="ExternalOutput").ap()

    xsp_d = nc.dram_tensor("xspill", [128, NT, C], BF16, kind="Internal").ap()
    bscr_d = [nc.dram_tensor(f"bscr{i}", [128, DH], BF16, kind="Internal").ap()
              for i in range(4)]
    cT_d = nc.dram_tensor("cTd", [128, 128], BF16, kind="Internal").ap()
    st_d = nc.dram_tensor("std", [1, 8], F32, kind="Internal").ap()

    dbg_tensors = {}
    def ddbg(name, shape, dtype=F32):
        if name in dbg:
            dbg_tensors[name] = nc.dram_tensor(
                "dbg_" + name, list(shape), dtype, kind="ExternalOutput").ap()
    ddbg("xn0", (128, C))
    ddbg("glog", (128, 128))
    ddbg("csel", (128, 128))
    ddbg("kf", (128, C))
    ddbg("krope", (128, C), BF16)
    ddbg("vbf", (128, C), BF16)
    ddbg("attn", (128, 8 * 128), BF16)
    ddbg("gfm", (128, 8 * T), BF16)
    ddbg("wtd", (128, NH * TS), BF16)
    ddbg("eg15", (128, NH * NL), BF16)
    ddbg("wgsb", (128, 2048), BF16)
    ddbg("xa0", (128, TS), BF16)
    ddbg("mix0", (128, NHC * TS), BF16)
    ddbg("ya0", (128, 2 * TS), BF16)

    with ExitStack() as ctx:
        E = ctx.enter_context
        S = Sched(nc, ctx)
        GD, SD, TM, VV, SS = "gd", "sd", "tm", "vv", "ss"

        id_bf = E(nc.sbuf_tensor("id_bf", [128, 128], BF16))
        mask_t = E(nc.sbuf_tensor("mask_t", [128, 128], F32))
        cos_t = E(nc.sbuf_tensor("cos_t", [128, NL], F32))
        sin_t = E(nc.sbuf_tensor("sin_t", [128, NL], F32))
        iota_t = E(nc.sbuf_tensor("iota_t", [128, 128], F32))
        iotaB_t = E(nc.sbuf_tensor("iotaB_t", [128, 128], F32))
        z16_t = E(nc.sbuf_tensor("z16_t", [RK, 128], BF16))
        lq_t = E(nc.sbuf_tensor("lq_t", [128, NH * NL], BF16))
        gwT_t = E(nc.sbuf_tensor("gwT_t", [128, 16 * NE], BF16))
        attn_t = E(nc.sbuf_tensor("attn_t", [128, 8 * 128], BF16))
        glog_t = E(nc.sbuf_tensor("glog_t", [128, 128], F32))
        csel_t = E(nc.sbuf_tensor("csel_t", [128, 128], F32))
        cm_t = E(nc.sbuf_tensor("cm_t", [128, 96], F32))
        stt_t = E(nc.sbuf_tensor("stt_t", [128, 16], F32))
        mxb_t = E(nc.sbuf_tensor("mxb_t", [128, 8], F32))

        pA = E(nc.psum_tensor("pA", [128, 1024], F32))
        pB = E(nc.psum_tensor("pB", [128, 1024], F32))
        pC = E(nc.psum_tensor("pC", [128, 1024], F32))
        pT = E(nc.psum_tensor("pT", [128, 2048], BF16))

        blk = E(nc.Block())

        def g_dma(out, in_, waits=()):
            return S.emit("gpsimd", lambda e, o=out, i=in_: e.dma_start(out=o, in_=i),
                          waits=waits, inc=GD, inc_by=16)

        def s_dma(out, in_, waits=()):
            return S.emit("sync", lambda e, o=out, i=in_: e.dma_start(out=o, in_=i),
                          waits=waits, inc=SD, inc_by=16)

        tk_id = g_dma(id_bf[:, :], d["idn"][:, :])
        tk_mask = s_dma(mask_t[:, :], d["maskp"][:, :])
        tk_cos = s_dma(cos_t[:, :], d["cosT"][:, :])
        tk_sin = s_dma(sin_t[:, :], d["sinT"][:, :])
        tk_iota = s_dma(iota_t[:, :], d["iota8"][:, :])
        tk_iotaB = s_dma(iotaB_t[:, :], d["iotaB"][:, :])
        tk_z16 = g_dma(z16_t[:, :], d["zeros16"][:, :])
        tk_lq = g_dma(lq_t.ap().rearrange("p (h l) -> p h l", l=NL),
                      d["lq"].rearrange("h p l -> p h l"))
        tk_gwT = g_dma(gwT_t.ap().rearrange("p (c e) -> p c e", e=NE),
                       d["gwT"].rearrange("(c p) e -> p c e", p=128))

        NEG1 = (32, 64, 96)
        NEG2 = (0, 32, 64, 96)
        with nc.sbuf_tensor("bstg", [128, DH], BF16) as bstg, \
             nc.sbuf_tensor("bneg", [128, DH], BF16) as bneg:
            tok_bscr = None
            tok_bneg = None
            for i, name in enumerate(("bup1", "bup2", "bg1", "bg2")):
                tld = g_dma(bstg[:, :], d[name][:, :],
                            waits=(tok_bscr,) if tok_bscr else ())
                tok_bneg = S.emit("vector",
                                  lambda e: e.tensor_scalar_mul(bneg[:, :], bstg[:, :], -1.0),
                                  waits=(tld,), inc=VV)
                negs = NEG1 if (i % 2 == 0) else NEG2
                for lo in range(0, 128, 16):
                    src = bneg if lo in negs else bstg
                    tok_bscr = s_dma(bscr_d[i][lo:lo + 16, :], src[lo:lo + 16, :],
                                     waits=(tok_bneg,))

        es_gfm = ExitStack()
        Gfm = es_gfm.enter_context(nc.sbuf_tensor("Gfm", [128, 8 * T], BF16))
        es_x = ExitStack()
        X_sb = es_x.enter_context(nc.sbuf_tensor("X_sb", [128, NT * C], BF16))
        with nc.sbuf_tensor("wg_sb", [128, 16 * NH * NL], BF16) as wg_sb, \
             nc.sbuf_tensor("xt0", [128, C], F32) as xt0, \
             nc.sbuf_tensor("xt1", [128, C], F32) as xt1, \
             nc.sbuf_tensor("xn_bf", [128, C], BF16) as xn_bf, \
             nc.sbuf_tensor("E_bf", [128, NH * NL], BF16) as E_bf, \
             nc.sbuf_tensor("Gn_bf", [128, NH * NL], BF16) as Gn_bf, \
             nc.sbuf_tensor("lnsc", [128, 8], F32) as lnsc, \
             nc.sbuf_tensor("sqd", [128, C], F32) as sqd, \
             nc.sbuf_tensor("grec", [128, NH], F32) as grec:

            tok_wg = None
            for c in range(16):
                tok_wg = g_dma(wg_sb[:, c * NH * NL:(c + 1) * NH * NL],
                               d["wg"][c * 128:(c + 1) * 128, :],
                               waits=(tok_bscr, tok_bneg) if c == 0 else ())

            X3 = X_sb.ap().rearrange("p (c t) -> p c t", t=T)
            pT3 = pT.ap().rearrange("p (c t) -> p c t", t=128)
            Gfm3 = Gfm.ap().rearrange("p (h t) -> p h t", t=T)

            tok_x = [None, None]
            tok_xn = [None, None]
            tok_xev = None
            tok_gev = None
            tok_exp = None
            tok_glev = None
            tok_gtr = None
            tok_gn = None

            for it in range(NT):
                pb = it % 2
                xt = (xt0, xt1)[pb]
                tok_x[pb] = s_dma(xt[:, :], d["x"][it * 128:(it + 1) * 128, :],
                                  waits=(tok_xn[pb],))
                S.emit("vector", lambda e, x=xt: e.reduce_sum(
                    lnsc[:, 0:1], x[:, :], axis=AX.X), waits=(tok_x[pb],))
                S.emit("vector", lambda e: e.drain())
                t2 = S.emit("vector", lambda e: e.tensor_scalar_mul(
                    lnsc[:, 1:2], lnsc[:, 0:1], -1.0 / C), inc=VV)
                t3 = S.emit("scalar", lambda e, x=xt: e.activation(
                    sqd[:, :], x[:, :], AF.Square, bias=lnsc[:, 1:2],
                    accum_out=lnsc[:, 2:3]), waits=(t2,), inc=SS)
                S.emit("vector", lambda e: e.tensor_scalar(
                    lnsc[:, 3:4], lnsc[:, 2:3], 1.0 / C, EPS,
                    op0=ALU.mult, op1=ALU.add), waits=(t3,))
                S.emit("vector", lambda e: e.drain())
                t5 = S.emit("vector", lambda e: e.reciprocal(
                    lnsc[:, 4:5], lnsc[:, 3:4]), inc=VV)
                t6 = S.emit("scalar", lambda e: e.activation(
                    lnsc[:, 5:6], lnsc[:, 4:5], AF.Sqrt), waits=(t5,), inc=SS)
                tok_xn[pb] = S.emit("vector", lambda e, x=xt: e.tensor_scalar(
                    xn_bf[:, :], x[:, :], lnsc[:, 1:2], lnsc[:, 5:6],
                    op0=ALU.add, op1=ALU.mult), waits=(t6,), inc=VV)
                if it == 0 and "xn0" in dbg:
                    tdbg = S.emit("vector", lambda e: e.tensor_copy(
                        sqd[:, :], xn_bf[:, :]), waits=(tok_xn[pb],), inc=VV)
                    s_dma(dbg_tensors["xn0"][:, :], sqd[:, :], waits=(tdbg,))
                tt = None
                for c in range(16):
                    tt = S.emit("tensor", lambda e, c=c: e.transpose(
                        pT3[:, c, :], xn_bf[:, c * 128:(c + 1) * 128], id_bf[:, :]),
                        waits=(tok_xn[pb], tk_id, tok_gev) if c == 0 else (),
                        inc=TM if c == 15 else None)
                tok_xev = S.emit("scalar", lambda e, it=it: e.copy(
                    X3[:, :, it * 128:(it + 1) * 128], pT3[:, :, :]),
                    waits=(tt,), inc=SS)
                tg = None
                for c in range(16):
                    tg = S.emit("tensor", lambda e, c=c, it=it: e.matmul(
                        pB[:, 512:520],
                        X_sb[:, c * C + it * 128: c * C + it * 128 + 128],
                        gwT_t[:, c * NE:(c + 1) * NE],
                        start=(c == 0), stop=(c == 15), skip_group_check=True),
                        waits=(tok_xev, tok_glev, tk_gwT) if c == 0 else (),
                        inc=TM if c == 15 else None)
                tok_glev = S.emit("vector", lambda e, it=it: e.tensor_copy(
                    glog_t[:, it * NE:(it + 1) * NE], pB[:, 512:520]),
                    waits=(tg,), inc=VV)
                tG = None
                for c in range(16):
                    for nh in range(2):
                        tG = S.emit("tensor", lambda e, c=c, it=it, nh=nh: e.matmul(
                            pA[:, nh * 512:(nh + 1) * 512],
                            X_sb[:, c * C + it * 128: c * C + it * 128 + 128],
                            wg_sb[:, c * NH * NL + nh * 512: c * NH * NL + (nh + 1) * 512],
                            start=(c == 0), stop=(c == 15), skip_group_check=True),
                            waits=(tok_xev, tok_exp, tok_wg) if (c == 0 and nh == 0) else (),
                            inc=TM if (c == 15 and nh == 1) else None)
                tok_exp = S.emit("scalar", lambda e: e.activation(
                    E_bf[:, :], pA[:, :], AF.Exp, scale=SCALE),
                    waits=(tG, tok_gn), inc=SS)
                S.emit("vector", lambda e: e.reduce_sum(
                    grec.ap().rearrange("p (h o) -> p h o", o=1),
                    E_bf.ap().rearrange("p (h l) -> p h l", l=NL), axis=AX.X),
                    waits=(tok_exp,))
                S.emit("vector", lambda e: e.drain())
                S.emit("vector", lambda e: e.reciprocal(grec[:, :], grec[:, :]), inc=VV)
                S.emit("vector", lambda e: e.drain())
                tok_gn = S.emit("vector", lambda e: e.tensor_tensor(
                    Gn_bf.ap().rearrange("p (h l) -> p h l", l=NL),
                    E_bf.ap().rearrange("p (h l) -> p h l", l=NL),
                    grec.ap().rearrange("p (h o) -> p h o", o=1).to_broadcast([128, NH, NL]),
                    ALU.mult), waits=(tok_gtr,), inc=VV)
                tok_gtr = None
                for pr in range(8):
                    tok_gtr = S.emit("tensor", lambda e, pr=pr: e.transpose(
                        pT3[:, pr, :], Gn_bf[:, pr * 128:(pr + 1) * 128], id_bf[:, :]),
                        waits=(tok_gn, tok_xev) if pr == 0 else (),
                        inc=TM if pr == 7 else None)
                tok_gev = S.emit("scalar", lambda e, it=it: e.copy(
                    Gfm3[:, :, it * 128:(it + 1) * 128], pT3[:, 0:8, :]),
                    waits=(tok_gtr,), inc=SS)

            if "wgsb" in dbg:
                s_dma(dbg_tensors["wgsb"][:, :], wg_sb[:, 0:2048], waits=(tok_wg,))
            if "eg15" in dbg:
                s_dma(dbg_tensors["eg15"][:, :], E_bf[:, :], waits=(tok_gn,))
            tok_A_gev = tok_gev
            tok_A_glev = tok_glev
            tok_A_exp = tok_exp
            tok_A_xev = tok_xev
            tok_A_gtr = tok_gtr
            tok_A_gn = tok_gn
            bA = (tok_A_gev, tok_A_glev, tok_A_exp, tok_A_xev, tok_A_gtr, tok_A_gn)
            if "glog" in dbg:
                s_dma(dbg_tensors["glog"][:, :], glog_t[:, :], waits=(tok_glev,))

        with nc.sbuf_tensor("wkv0", [128, C], BF16) as wkv0, \
             nc.sbuf_tensor("wkv1", [128, C], BF16) as wkv1, \
             nc.sbuf_tensor("kf", [128, C], F32) as kf, \
             nc.sbuf_tensor("k_bf", [128, C], BF16) as k_bf, \
             nc.sbuf_tensor("v_bf", [128, C], BF16) as v_bf, \
             nc.sbuf_tensor("kfm", [128, C], BF16) as kfm, \
             nc.sbuf_tensor("rta", [128, NH * NL], F32) as rta, \
             nc.sbuf_tensor("rtb", [128, NH * NL], F32) as rtb, \
             nc.sbuf_tensor("wsc", [128, 128], F32) as wsc, \
             nc.sbuf_tensor("we", [128, 128], F32) as we, \
             nc.sbuf_tensor("wn", [128, 128], BF16) as wn, \
             nc.sbuf_tensor("wT", [128, 128], BF16) as wT, \
             nc.sbuf_tensor("scs", [128, 8], F32) as scs:

            tok_kv = {}
            for which, wname in (("k", "wk"), ("v", "wv")):
                tok_w = [None, None]
                tchunk = {}
                tmm = None
                for c in range(16):
                    wb = (wkv0, wkv1)[c % 2]
                    extra = (tok_kv["k"],) if (which == "v" and c < 2) else ()
                    tok_w[c % 2] = g_dma(
                        wb[:, :], d[wname][c * 128:(c + 1) * 128, :],
                        waits=(tchunk.get(c - 2),) + extra +
                        (bA if (which == "k" and c < 2) else ()))
                    for nq in range(4):
                        ps = (pA, pA, pB, pB)[nq]
                        sl = (slice(0, 512), slice(512, 1024))[nq % 2]
                        tmm = S.emit("tensor", lambda e, c=c, wb=wb, ps=ps, sl=sl, nq=nq: e.matmul(
                            ps[:, sl], X_sb[:, c * C: c * C + 128],
                            wb[:, nq * 512:(nq + 1) * 512],
                            start=(c == 0), stop=(c == 15), skip_group_check=True),
                            waits=(tok_w[c % 2], tok_A_exp, tok_A_glev,
                                   tok_kv.get(which)) if nq == 0 else (),
                            inc=TM if nq == 3 else None)
                    tchunk[c] = tmm
                if which == "k":
                    S.emit("vector", lambda e: e.tensor_copy(kf[:, 0:1024], pA[:, :]),
                           waits=(tmm,))
                    tok_kv["k"] = S.emit("vector", lambda e: e.tensor_copy(
                        kf[:, 1024:2048], pB[:, :]), inc=VV)
                else:
                    S.emit("vector", lambda e: e.tensor_copy(v_bf[:, 0:1024], pA[:, :]),
                           waits=(tmm,))
                    tok_kv["v"] = S.emit("vector", lambda e: e.tensor_copy(
                        v_bf[:, 1024:2048], pB[:, :]), inc=VV)

            kf3 = kf.ap().rearrange("p (h d) -> p h d", d=HD)
            kb3 = k_bf.ap().rearrange("p (h d) -> p h d", d=HD)
            cosb = cos_t.ap().rearrange("p (o j) -> p o j", o=1).to_broadcast([128, NH, NL])
            sinb = sin_t.ap().rearrange("p (o j) -> p o j", o=1).to_broadcast([128, NH, NL])
            rta3 = rta.ap().rearrange("p (h j) -> p h j", j=NL)
            rtb3 = rtb.ap().rearrange("p (h j) -> p h j", j=NL)
            S.emit("vector", lambda e: e.tensor_tensor(
                rta3[:, :, :], kf3[:, :, 0:NL], cosb, ALU.mult),
                waits=(tok_kv["k"], tk_cos, tk_sin))
            S.emit("vector", lambda e: e.tensor_tensor(
                rtb3[:, :, :], kf3[:, :, NL:HD], sinb, ALU.mult))
            S.emit("vector", lambda e: e.tensor_tensor(
                kb3[:, :, 0:NL], rta3[:, :, :], rtb3[:, :, :], ALU.subtract), inc=VV)
            S.emit("vector", lambda e: e.tensor_tensor(
                rta3[:, :, :], kf3[:, :, 0:NL], sinb, ALU.mult))
            S.emit("vector", lambda e: e.tensor_tensor(
                rtb3[:, :, :], kf3[:, :, NL:HD], cosb, ALU.mult))
            tok_rope = S.emit("vector", lambda e: e.tensor_tensor(
                kb3[:, :, NL:HD], rta3[:, :, :], rtb3[:, :, :], ALU.add), inc=VV)
            if "kf" in dbg:
                s_dma(dbg_tensors["kf"][:, :], kf[:, :], waits=(tok_kv["k"],))
            if "krope" in dbg:
                s_dma(dbg_tensors["krope"][:, :], k_bf[:, :], waits=(tok_rope,))
            if "vbf" in dbg:
                s_dma(dbg_tensors["vbf"][:, :], v_bf[:, :], waits=(tok_kv["v"],))

            pT3b = pT.ap().rearrange("p (h t) -> p h t", t=128)
            tt = None
            for h in range(NH):
                tt = S.emit("tensor", lambda e, h=h: e.transpose(
                    pT3b[:, h, :], k_bf[:, h * 128:(h + 1) * 128], id_bf[:, :]),
                    waits=(tok_rope, tok_A_gev) if h == 0 else (),
                    inc=TM if h == 15 else None)
            tok_kfm = S.emit("scalar", lambda e: e.copy(kfm[:, :], pT[:, :]),
                             waits=(tt,), inc=SS)

            tok_attn_ev = None
            tok_twe = None
            tok_tm1 = None
            for pr in range(8):
                tsc = None
                for sub in range(2):
                    h = pr * 2 + sub
                    tsc = S.emit("tensor", lambda e, h=h, sub=sub: e.matmul(
                        pC[sub * 64:(sub + 1) * 64, 0:128],
                        lq_t[:, h * NL:(h + 1) * NL],
                        kfm[:, h * 128:(h + 1) * 128],
                        start=True, stop=True, skip_group_check=True),
                        waits=(tok_kfm, tk_lq, tok_tm1) if sub == 0 else (),
                        inc=TM if sub == 1 else None)
                tok_tm1 = S.emit("vector", lambda e: e.tensor_add(
                    wsc[:, :], pC[:, 0:128], mask_t[:, :]),
                    waits=(tsc, tk_mask), inc=VV)
                S.emit("vector", lambda e: e.drain())
                tm2 = S.emit("vector", lambda e: e.reduce_max(
                    scs[:, 0:1], wsc[:, :], axis=AX.X, negate=True), inc=VV)
                S.emit("scalar", lambda e: e.mul(scs[:, 1:2], scs[:, 0:1], SCALE),
                       waits=(tm2,))
                S.emit("scalar", lambda e: e.drain())
                tm4 = S.emit("scalar", lambda e: e.activation(
                    we[:, :], wsc[:, :], AF.Exp, bias=scs[:, 1:2], scale=SCALE,
                    accum_out=scs[:, 2:3]), inc=SS)
                S.emit("vector", lambda e: e.reciprocal(scs[:, 3:4], scs[:, 2:3]),
                       waits=(tm4,))
                S.emit("vector", lambda e: e.drain())
                tm6 = S.emit("vector", lambda e: e.tensor_scalar_mul(
                    wn[:, :], we[:, :], scs[:, 3:4]), inc=VV)
                tw = S.emit("tensor", lambda e: e.transpose(
                    pT3b[:, 0, :], wn[:, :], id_bf[:, :]),
                    waits=(tm6, tok_kfm, tok_twe), inc=TM)
                tok_twe = S.emit("scalar", lambda e: e.copy(wT[:, :], pT3b[:, 0, :]),
                                 waits=(tw,), inc=SS)
                ta = None
                for sub in range(2):
                    h = pr * 2 + sub
                    ta = S.emit("tensor", lambda e, h=h, sub=sub: e.matmul(
                        pC[sub * 64:(sub + 1) * 64, 512:640],
                        wT[:, sub * 64:(sub + 1) * 64],
                        v_bf[:, h * 128:(h + 1) * 128],
                        start=True, stop=True, skip_group_check=True),
                        waits=(tok_twe, tok_kv["v"], tok_attn_ev) if sub == 0 else (),
                        inc=TM if sub == 1 else None)
                tok_attn_ev = S.emit("scalar", lambda e, pr=pr: e.copy(
                    attn_t[:, pr * 128:(pr + 1) * 128], pC[:, 512:640]),
                    waits=(ta,), inc=SS)
            if "attn" in dbg:
                s_dma(dbg_tensors["attn"][:, :], attn_t[:, :], waits=(tok_attn_ev,))
            if "gfm" in dbg:
                s_dma(dbg_tensors["gfm"][:, :], Gfm[:, :], waits=(tok_A_gev,))
            tok_B_kv_v = tok_kv["v"]
            tok_B_kfm = tok_kfm
            tok_B_twe = tok_twe

        tok_xspill = s_dma(xsp_d[:, :, :],
                           X_sb.ap().rearrange("p (c t) -> p c t", t=T),
                           waits=(tok_A_xev,))

        g3 = glog_t.ap().rearrange("p (a b) -> p a b", b=NE)
        i3 = iota_t.ap().rearrange("p (a b) -> p a b", b=NE)
        def cmb(lo, hi):
            return cm_t.ap()[:, lo:hi].rearrange(
                "p (a b) -> p a b", b=1).to_broadcast([128, 16, NE])
        def cm3(lo, hi):
            return cm_t.ap()[:, lo:hi].rearrange("p (a b) -> p a b", b=1)

        with nc.sbuf_tensor("eqw", [128, 128], F32) as eqw, \
             nc.sbuf_tensor("ind0", [128, 128], F32) as ind0, \
             nc.sbuf_tensor("ind1", [128, 128], F32) as ind1, \
             nc.sbuf_tensor("mskg", [128, 128], F32) as mskg, \
             nc.sbuf_tensor("sttb", [128, 16], BF16) as sttb, \
             nc.sbuf_tensor("rowsc", [16, 128], F32) as rowsc, \
             nc.sbuf_tensor("cselb", [128, 128], BF16) as cselb:
            eq3 = eqw.ap().rearrange("p (a b) -> p a b", b=NE)
            i03 = ind0.ap().rearrange("p (a b) -> p a b", b=NE)
            i13 = ind1.ap().rearrange("p (a b) -> p a b", b=NE)
            mg3 = mskg.ap().rearrange("p (a b) -> p a b", b=NE)

            S.emit("vector", lambda e: e.tensor_reduce(
                cm3(0, 16), g3[:, :, :], axis=AX.X, op=ALU.max),
                waits=(tok_A_glev, tok_attn_ev))
            S.emit("vector", lambda e: e.drain())
            S.emit("vector", lambda e: e.tensor_tensor(
                eq3[:, :, :], g3[:, :, :], cmb(0, 16), ALU.is_equal))
            S.emit("vector", lambda e: e.scalar_tensor_tensor(
                mskg[:, :], eqw[:, :], -BIGI, iotaB_t[:, :], ALU.mult, ALU.add),
                waits=(tk_iotaB,))
            S.emit("vector", lambda e: e.drain())
            S.emit("vector", lambda e: e.tensor_reduce(
                cm3(64, 80), mg3[:, :, :], axis=AX.X, op=ALU.min))
            S.emit("vector", lambda e: e.drain())
            S.emit("vector", lambda e: e.tensor_tensor(
                i03[:, :, :], i3[:, :, :], cmb(64, 80), ALU.is_equal),
                waits=(tk_iota,))
            S.emit("vector", lambda e: e.scalar_tensor_tensor(
                mskg[:, :], ind0[:, :], -1.0e5, glog_t[:, :], ALU.mult, ALU.add))
            S.emit("vector", lambda e: e.drain())
            S.emit("vector", lambda e: e.tensor_reduce(
                cm3(16, 32), mg3[:, :, :], axis=AX.X, op=ALU.max))
            S.emit("vector", lambda e: e.drain())
            S.emit("vector", lambda e: e.tensor_tensor(
                eq3[:, :, :], mg3[:, :, :], cmb(16, 32), ALU.is_equal))
            S.emit("vector", lambda e: e.scalar_tensor_tensor(
                mskg[:, :], eqw[:, :], -BIGI, iotaB_t[:, :], ALU.mult, ALU.add))
            S.emit("vector", lambda e: e.drain())
            S.emit("vector", lambda e: e.tensor_reduce(
                cm3(64, 80), mg3[:, :, :], axis=AX.X, op=ALU.min))
            S.emit("vector", lambda e: e.drain())
            tok_ind1 = S.emit("vector", lambda e: e.tensor_tensor(
                i13[:, :, :], i3[:, :, :], cmb(64, 80), ALU.is_equal), inc=VV)

            S.emit("vector", lambda e: e.reduce_max(
                stt_t[:, 0:1], cm_t[:, 0:16], axis=AX.X))
            tok_pm = S.emit("vector", lambda e: e.reduce_max(
                stt_t[:, 1:2], cm_t[:, 16:32], axis=AX.X), inc=VV)
            S.emit("vector", lambda e: e.drain())
            tc1 = S.emit("vector", lambda e: e.tensor_copy(
                sttb[:, :], stt_t[:, 0:16]), inc=VV)
            tc2 = S.emit("tensor", lambda e: e.transpose(
                pT.ap()[0:16, 0:128], sttb[:, :], id_bf[:, :]),
                waits=(tc1, tok_B_twe), inc=TM)
            tc3 = S.emit("scalar", lambda e: e.copy(
                rowsc[:, :], pT.ap()[0:16, 0:128]), waits=(tc2,), inc=SS)
            tc4 = S.emit("vector", lambda e: e.reduce_max(
                stt_t[0:2, 2:3], rowsc[0:2, :], axis=AX.X, negate=True),
                waits=(tc3,), inc=VV)
            tc5 = s_dma(st_d[0:1, 0:2].rearrange("o p -> p o"),
                        stt_t.ap()[0:2, 2:3], waits=(tc4,))
            tc6 = s_dma(mxb_t[:, 0:2], st_d.partition_broadcast(128)[:, 0, 0:2],
                        waits=(tc5,))
            tc7 = S.emit("scalar", lambda e: e.activation(
                cm_t[:, 32:48], cm_t[:, 0:16], AF.Exp, bias=mxb_t[:, 0:1]),
                waits=(tc6,))
            tc8 = S.emit("scalar", lambda e: e.activation(
                cm_t[:, 48:64], cm_t[:, 16:32], AF.Exp, bias=mxb_t[:, 1:2]), inc=SS)
            S.emit("vector", lambda e: e.reduce_sum(
                stt_t[:, 4:5], cm_t[:, 32:48], axis=AX.X), waits=(tc8,))
            tok_ps = S.emit("vector", lambda e: e.reduce_sum(
                stt_t[:, 5:6], cm_t[:, 48:64], axis=AX.X), inc=VV)
            S.emit("vector", lambda e: e.drain())
            tc11 = S.emit("vector", lambda e: e.tensor_copy(
                sttb[:, 0:2], stt_t[:, 4:6]), inc=VV)
            tc12 = S.emit("tensor", lambda e: e.transpose(
                pT.ap()[0:16, 128:256], sttb[:, :], id_bf[:, :]),
                waits=(tc11, tc3), inc=TM)
            tc13 = S.emit("scalar", lambda e: e.copy(
                rowsc[:, :], pT.ap()[0:16, 128:256]), waits=(tc12,), inc=SS)
            S.emit("vector", lambda e: e.reduce_sum(
                stt_t[0:2, 6:7], rowsc[0:2, :], axis=AX.X), waits=(tc13,))
            S.emit("vector", lambda e: e.drain())
            tc15 = S.emit("vector", lambda e: e.reciprocal(
                stt_t[0:2, 7:8], stt_t[0:2, 6:7]), inc=VV)
            tc16 = s_dma(st_d[0:1, 2:4].rearrange("o p -> p o"),
                         stt_t.ap()[0:2, 7:8], waits=(tc15,))
            tc17 = s_dma(mxb_t[:, 2:4], st_d.partition_broadcast(128)[:, 0, 2:4],
                         waits=(tc16,))
            S.emit("vector", lambda e: e.tensor_tensor(
                mg3[:, :, :], i03[:, :, :], cmb(32, 48), ALU.mult),
                waits=(tc17, tok_ind1))
            S.emit("vector", lambda e: e.tensor_scalar_mul(
                ind0[:, :], mskg[:, :], mxb_t[:, 2:3]))
            S.emit("vector", lambda e: e.tensor_tensor(
                eq3[:, :, :], i13[:, :, :], cmb(48, 64), ALU.mult))
            S.emit("vector", lambda e: e.tensor_scalar_mul(
                eqw[:, :], eqw[:, :], mxb_t[:, 3:4]))
            u5 = S.emit("vector", lambda e: e.tensor_add(
                csel_t[:, :], ind0[:, :], eqw[:, :]), inc=VV)
            if "csel" in dbg:
                s_dma(dbg_tensors["csel"][:, :], csel_t[:, :], waits=(u5,))
            u6 = S.emit("vector", lambda e: e.tensor_copy(
                cselb[:, :], csel_t[:, :]), waits=(u5,), inc=VV)
            u7 = S.emit("tensor", lambda e: e.transpose(
                pT.ap()[:, 256:384], cselb[:, :], id_bf[:, :]),
                waits=(u6, tc12), inc=TM)
            u8 = S.emit("scalar", lambda e: e.copy(
                cselb[:, :], pT.ap()[:, 256:384]), waits=(u7,), inc=SS)
            tok_cT = s_dma(cT_d[:, :], cselb[:, :], waits=(u8,))
            tok_C_pT = u8
            tok_C_vv = u6
        es_x.close()
        bC = (tok_cT, u8, u7, tok_C_vv, tok_attn_ev, tok_B_kv_v, tok_B_twe)

        if stop_after in ("A", "B", "C"):
            S.emit("sync", lambda e: None,
                   waits=((GD, S.cnt[GD]), (SD, S.cnt[SD]), (VV, S.cnt[VV]),
                          (SS, S.cnt[SS]), (TM, S.cnt[TM])))
            S.build(blk)
            es_gfm.close()
            return nc, sorted(dbg_tensors.keys())

        with nc.sbuf_tensor("wp_sb", [128, 16 * C], BF16) as wp_sb, \
             nc.sbuf_tensor("wtd", [128, NH * TS], BF16) as wtd, \
             nc.sbuf_tensor("xt0d", [128, C], F32) as xt0d, \
             nc.sbuf_tensor("xt1d", [128, C], F32) as xt1d, \
             nc.sbuf_tensor("y1a", [128, C], F32) as y1a, \
             nc.sbuf_tensor("y1b", [128, C], F32) as y1b:
            tok_wp = None
            for c in range(16):
                tok_wp = g_dma(wp_sb[:, c * C:(c + 1) * C],
                               d["wp"][c * 128:(c + 1) * 128, :],
                               waits=(bC + ((GD, S.cnt[GD]),)) if c == 0 else ())
            tok_xd = [None, None]
            tok_y1 = [None, None]
            wtd_ev = {0: None, 1: None}
            tok_wtd_last = None
            tok_D_pBC = None
            for sd in range(NSUB):
                tok_wtd_ev = None
                for h in range(NH):
                    hb = h % 2
                    sl = slice(hb * 512, (hb + 1) * 512)
                    pr, sub = h // 2, h % 2
                    tw = S.emit("tensor", lambda e, pr=pr, sub=sub, sd=sd, sl=sl: e.matmul(
                        pA[:, sl],
                        attn_t[sub * 64:(sub + 1) * 64, pr * 128:(pr + 1) * 128],
                        Gfm[sub * 64:(sub + 1) * 64, pr * T + sd * TS: pr * T + (sd + 1) * TS],
                        start=True, stop=True, skip_group_check=True),
                        waits=bC + (tok_A_gev, wtd_ev[hb],
                                    tok_wtd_last if h <= 1 else None),
                        inc=TM)
                    ev = S.emit("scalar", lambda e, h=h, sl=sl: e.copy(
                        wtd[:, h * TS:(h + 1) * TS], pA[:, sl]), waits=(tw,), inc=SS)
                    wtd_ev[hb] = ev
                    tok_wtd_ev = ev
                if "wtd" in dbg and sd == 0:
                    s_dma(dbg_tensors["wtd"][:, :], wtd[:, :], waits=(tok_wtd_ev,))
                for tc in range(4):
                    it = sd * 4 + tc
                    pbx = it % 2
                    xt = (xt0d, xt1d)[pbx]
                    y1 = (y1a, y1b)[pbx]
                    tok_xd[pbx] = s_dma(xt[:, :], d["x"][it * 128:(it + 1) * 128, :],
                                        waits=(tok_y1[pbx],) + (bC if it < 2 else ()))
                    tmm = None
                    for c in range(16):
                        for dc in range(4):
                            ps = (pB, pB, pC, pC)[dc]
                            sl = (slice(0, 512), slice(512, 1024))[dc % 2]
                            tmm = S.emit("tensor", lambda e, c=c, dc=dc, tc=tc, ps=ps, sl=sl: e.matmul(
                                ps[:, sl],
                                wtd[:, c * TS + tc * 128: c * TS + tc * 128 + 128],
                                wp_sb[:, c * C + dc * 512: c * C + (dc + 1) * 512],
                                start=(c == 0), stop=(c == 15), skip_group_check=True),
                                waits=(tok_wtd_ev, tok_wp, tok_B_kv_v, tok_attn_ev,
                                       tok_D_pBC) if (c == 0 and dc == 0) else (),
                                inc=TM if (c == 15 and dc == 3) else None)
                    if tc == 3:
                        tok_wtd_last = tmm
                    S.emit("vector", lambda e, y1=y1, xt=xt: e.tensor_add(
                        y1[:, 0:1024], pB[:, :], xt[:, 0:1024]),
                        waits=(tmm, tok_xd[pbx]))
                    tok_D_pBC = S.emit("vector", lambda e, y1=y1, xt=xt: e.tensor_add(
                        y1[:, 1024:2048], pC[:, :], xt[:, 1024:2048]), inc=VV)
                    tok_y1[pbx] = s_dma(out_d[it * 128:(it + 1) * 128, :], y1[:, :],
                                        waits=(tok_D_pBC,))
            tok_D_y1_last = tok_y1[1] if tok_y1[1] else tok_y1[0]
            tok_D_pA = wtd_ev
        es_gfm.close()
        bD = (tok_D_y1_last, tok_D_pBC, tok_wtd_last, tok_D_pA[0], tok_D_pA[1],
              tok_wp)

        if stop_after == "D":
            S.emit("sync", lambda e: None,
                   waits=((GD, S.cnt[GD]), (SD, S.cnt[SD]), (VV, S.cnt[VV]),
                          (SS, S.cnt[SS]), (TM, S.cnt[TM])))
            S.build(blk)
            return nc, sorted(dbg_tensors.keys())

        XsE = E(nc.sbuf_tensor("XsE", [128, 16 * TS], BF16))
        mixed = E(nc.sbuf_tensor("mixed", [128, NHC * TS], BF16))
        cbc = E(nc.sbuf_tensor("cbc", [128, NE * TS], BF16))
        xa_sb = [E(nc.sbuf_tensor(f"xa{j}", [128, TS], BF16)) for j in range(4)]
        A_sb = [E(nc.sbuf_tensor(f"A_sb{j}", [128, 16 * 128], BF16)) for j in range(4)]
        wup_sl = [E(nc.sbuf_tensor(f"wup{j}", [128, 16 * 128], BF16)) for j in range(2)]
        wg_sl = [E(nc.sbuf_tensor(f"wgs{j}", [128, 16 * 128], BF16)) for j in range(2)]
        b_sl = [E(nc.sbuf_tensor(f"bsl{j}", [128, 4 * 128], BF16)) for j in range(2)]
        dA_sl = [E(nc.sbuf_tensor(f"dAs{j}", [128, 128], BF16)) for j in range(2)]
        wdn_sl = [E(nc.sbuf_tensor(f"wdn{j}", [128, NHC * 128], BF16)) for j in range(2)]
        su_bf = E(nc.sbuf_tensor("su_bf", [128, TS], BF16))
        h_bf = E(nc.sbuf_tensor("h_bf", [128, TS], BF16))
        ch_bf = [E(nc.sbuf_tensor(f"ch{j}", [128, TS], BF16)) for j in range(2)]
        macc = [E(nc.sbuf_tensor(f"macc{j}", [128, TS], F32)) for j in range(2)]
        ya_sb = E(nc.sbuf_tensor("ya_sb", [128, 2 * TS], BF16))
        dB_sb = [E(nc.sbuf_tensor(f"dB{j}", [128, C], BF16)) for j in range(2)]
        dout = E(nc.sbuf_tensor("dout", [128, 16 * TS], BF16))
        y1e = [E(nc.sbuf_tensor(f"y1e{j}", [128, C], F32)) for j in range(2)]
        oste = [E(nc.sbuf_tensor(f"ost{j}", [128, C], F32)) for j in range(2)]

        tok_A_ld = []
        for j, nm in enumerate(("aup1", "aup2", "ag1", "ag2")):
            ta = None
            for c in range(16):
                ta = g_dma(A_sb[j][:, c * 128:(c + 1) * 128],
                           d[nm][c * 128:(c + 1) * 128, :],
                           waits=bD if (j == 0 and c == 0) else ())
            tok_A_ld.append(ta)
        tok_dB = [g_dma(dB_sb[0][:, :], d["dBA"][:, :]),
                  g_dma(dB_sb[1][:, :], d["dBB"][:, :])]

        TRANS = [(0, 0, 16), (0, 32, 32), (0, 64, 32), (0, 96, 32),
                 (1, 0, 32), (1, 32, 32), (1, 64, 32), (1, 96, 32)]

        tok_Xs = None
        tok_base_last = {0: None, 1: None}
        tok_xa_rd = None
        tok_cb_rd = None
        tok_ya_ev = None
        tok_mix_rd = None
        tok_out = []
        tok_w_used = {0: None, 1: None}
        tok_b_used = {0: None, 1: None}
        tok_dA_used = {0: None, 1: None}
        tok_wdn_used = {0: None, 1: None}
        tok_final_add = {0: None, 1: None}
        tok_tfa_prev = None
        prev_dn_ev = {0: None, 1: None}

        for s in range(nsub):
            tok_Xs = None
            for c in range(16):
                tok_Xs = s_dma(XsE[:, c * TS:(c + 1) * TS],
                               xsp_d[:, c, s * TS:(s + 1) * TS],
                               waits=((tok_xspill, tok_base_last[0],
                                       tok_base_last[1], tok_xa_rd) +
                                      (bD if s == 0 else ())) if c == 0 else ())
            tcb = None
            for e8 in range(NE):
                for q in range(4):
                    row = (4 * s + q) * NE + e8
                    tcb = s_dma(cbc[:, e8 * TS + q * 128: e8 * TS + (q + 1) * 128],
                                cT_d.partition_broadcast(128)[:, row, :],
                                waits=((tok_cT, tok_cb_rd) + (bD if s == 0 else ()))
                                if (e8 == 0 and q == 0) else ())
            tok_cbc = tcb
            tok_xa_ev = []
            for j in range(4):
                half = j % 2
                tj = None
                for c in range(16):
                    tj = S.emit("tensor", lambda e, j=j, c=c, half=half: e.matmul(
                        pA[:, half * 512:(half + 1) * 512],
                        A_sb[j][:, c * 128:(c + 1) * 128],
                        XsE[:, c * TS:(c + 1) * TS],
                        start=(c == 0), stop=(c == 15), skip_group_check=True),
                        waits=(tok_Xs, tok_A_ld[j],
                               tok_xa_ev[j - 2] if j >= 2 else
                               (tok_D_pA[half] if s == 0 else None)) if c == 0 else (),
                        inc=TM if c == 15 else None)
                ev = S.emit("scalar", lambda e, j=j, half=half: e.activation(
                    xa_sb[j][:, :], pA[:, half * 512:(half + 1) * 512],
                    AF.Copy, scale=ALPHA), waits=(tj,), inc=SS)
                tok_xa_ev.append(ev)
            if "xa0" in dbg and s == 0:
                s_dma(dbg_tensors["xa0"][:, :], xa_sb[0][:, :], waits=(tok_xa_ev[0],))


            if stop_e == "xa":
                S.emit("sync", lambda e: None,
                       waits=((GD, S.cnt[GD]), (SD, S.cnt[SD]), (VV, S.cnt[VV]),
                              (SS, S.cnt[SS]), (TM, S.cnt[TM])))
                S.build(blk)
                return nc, sorted(dbg_tensors.keys())
            S.emit("tensor", lambda e: e.matmul(
                pC[:, 0:512], z16_t[:, :], xa_sb[0][0:16, :],
                start=True, stop=True, skip_group_check=True),
                waits=(tok_xa_ev[0], tk_z16, tok_ya_ev,
                       tok_D_pBC if s == 0 else None, tok_attn_ev if s == 0 else None))
            tok_zya = S.emit("tensor", lambda e: e.matmul(
                pC[:, 512:1024], z16_t[:, :], xa_sb[0][0:16, :],
                start=True, stop=True, skip_group_check=True), inc=TM)


            if stop_e == "zero":
                S.emit("sync", lambda e: None,
                       waits=((GD, S.cnt[GD]), (SD, S.cnt[SD]), (VV, S.cnt[VV]),
                              (SS, S.cnt[SS]), (TM, S.cnt[TM])))
                S.build(blk)
                return nc, sorted(dbg_tensors.keys())
            tok_silu = {}
            tok_h = {}
            tok_ch = {}
            tok_ya_par = {0: None, 1: None}
            tok_ya = None
            tok_trans_g_last = None
            tok_mix_last = None
            tok_base_done = {}
            tok_wslice = {}
            tok_bslice = {}
            tok_dAs = {}
            tok_tch_last = None

            def emit_wdma(hc):
                pb = hc % 2
                tw = g_dma(
                    wup_sl[pb].ap().rearrange("p (c m) -> p c m", m=128),
                    d["wupT"].rearrange("(c p) h -> p c h", p=128)[:, :, hc * 128:(hc + 1) * 128],
                    waits=(tok_w_used[pb],) + (bD if (s == 0 and hc < 2) else ()))
                tw2 = g_dma(
                    wg_sl[pb].ap().rearrange("p (c m) -> p c m", m=128),
                    d["wgT"].rearrange("(c p) h -> p c h", p=128)[:, :, hc * 128:(hc + 1) * 128])
                tok_wslice[hc] = (tw, tw2)

            def emit_bdma(hc):
                pb = hc % 2
                tb = None
                for i in range(4):
                    tb = s_dma(b_sl[pb][:, i * 128:(i + 1) * 128],
                               bscr_d[i][:, hc * 128:(hc + 1) * 128],
                               waits=((tok_bscr, tok_b_used[pb]) +
                                      (bD if (s == 0 and hc < 2) else ()))
                               if i == 0 else ())
                tok_bslice[hc] = tb
                tok_dAs[hc] = g_dma(dA_sl[pb][:, :],
                                    d["dAT"][hc * 128:(hc + 1) * 128, :],
                                    waits=(tok_dA_used[pb],) +
                                    (bD if (s == 0 and hc < 2) else ()))

            def emit_base(hc, c0, c1):
                pb = hc % 2
                usl = slice(pb * 512, (pb + 1) * 512)
                tw, tw2 = tok_wslice[hc]
                for c in range(c0, c1):
                    S.emit("tensor", lambda e, c=c, usl=usl, pb=pb: e.matmul(
                        pA[:, usl], wup_sl[pb][:, c * 128:(c + 1) * 128],
                        XsE[:, c * TS:(c + 1) * TS],
                        start=(c == 0), stop=False, skip_group_check=True),
                        waits=(tw, tok_Xs,
                               tok_silu.get((hc - 2, 7)) if hc >= 2 else
                               tok_xa_ev[2 + pb]) if c == 0 else ())
                    tlast = S.emit("tensor", lambda e, c=c, usl=usl, pb=pb: e.matmul(
                        pB[:, usl], wg_sl[pb][:, c * 128:(c + 1) * 128],
                        XsE[:, c * TS:(c + 1) * TS],
                        start=(c == 0), stop=False, skip_group_check=True),
                        waits=(tw2,
                               tok_h.get((hc - 2, 7)) if hc >= 2 else None,
                               prev_dn_ev[pb] if (s > 0 and hc < 2) else None,
                               tok_D_pBC if (s == 0 and hc < 2) else None)
                        if c == 0 else (),
                        inc=TM if c == c1 - 1 else None)
                    if c == 15:
                        tok_base_done[hc] = tlast
                        tok_w_used[pb] = tlast

            def emit_ya(hc, e8):
                nonlocal tok_ya
                pb = hc % 2
                grp, off = e8 // 4, (e8 % 4) * 32
                kw2 = {"tile_position": (0, 96)} if off == 96 else {}
                tok_ya = S.emit("tensor", lambda e, grp=grp, off=off, e8=e8, pb=pb,
                                kw2=dict(kw2): e.matmul(
                    pC[:, grp * 512:(grp + 1) * 512][off:off + 16, :],
                    dA_sl[pb][:, e8 * 16:(e8 + 1) * 16],
                    ch_bf[e8 % 2][:, :],
                    start=False, stop=(hc == nhc - 1 and e8 == 7),
                    skip_group_check=True, **kw2),
                    waits=(tok_ch[(hc, e8)], tok_dAs[hc], tok_zya), inc=TM)
                tok_ya_par[e8 % 2] = tok_ya

            emit_wdma(0)
            emit_base(0, 0, 16)
            for hc in range(nhc):
                pb = hc % 2
                usl = slice(pb * 512, (pb + 1) * 512)
                if hc + 1 < nhc:
                    emit_wdma(hc + 1)
                emit_bdma(hc)
                tok_b = tok_bslice[hc]
                for e8 in range(NE):
                    ti, rlo, nr = TRANS[e8]
                    kw = {"tile_position": (96, 0)} if rlo == 96 else {}
                    tup = S.emit("tensor", lambda e, ti=ti, rlo=rlo, nr=nr, pb=pb,
                                 usl=usl, kw=dict(kw): e.matmul(
                        pA[:, usl],
                        b_sl[pb][rlo:rlo + nr, ti * 128:(ti + 1) * 128],
                        xa_sb[ti][rlo:rlo + nr, :],
                        start=False, stop=False, skip_group_check=True, **kw),
                        waits=(tok_b, tok_xa_ev[ti],
                               tok_silu.get((hc, e8 - 1)) if e8 > 0 else None),
                        inc=TM)
                    tgt = S.emit("tensor", lambda e, ti=ti, rlo=rlo, nr=nr, pb=pb,
                                 usl=usl, kw=dict(kw): e.matmul(
                        pB[:, usl],
                        b_sl[pb][rlo:rlo + nr, (2 + ti) * 128:(3 + ti) * 128],
                        xa_sb[2 + ti][rlo:rlo + nr, :],
                        start=False, stop=False, skip_group_check=True, **kw),
                        waits=(tok_xa_ev[2 + ti],
                               tok_h.get((hc, e8 - 1)) if e8 > 0 else None),
                        inc=TM)
                    if e8 == 7:
                        tok_trans_g_last = tgt
                    if hc + 1 < nhc:
                        emit_base(hc + 1, 2 * e8, 2 * e8 + 2)
                    if e8 > 0:
                        emit_ya(hc, e8 - 1)
                    ts1 = S.emit("scalar", lambda e, usl=usl: e.activation(
                        su_bf[:, :], pA[:, usl], AF.Silu),
                        waits=(tup, tok_h.get((hc, e8 - 1)) if e8 > 0 else
                               tok_h.get((hc - 1, 7))), inc=SS)
                    tok_silu[(hc, e8)] = ts1
                    th1 = S.emit("vector", lambda e, usl=usl: e.tensor_mul(
                        h_bf[:, :], su_bf[:, :], pB[:, usl]),
                        waits=(ts1, tgt), inc=VV)
                    tok_h[(hc, e8)] = th1
                    chb = ch_bf[e8 % 2]
                    tch = S.emit("vector", lambda e, e8=e8, chb=chb: e.tensor_mul(
                        chb[:, :], h_bf[:, :], cbc[:, e8 * TS:(e8 + 1) * TS]),
                        waits=(tok_cbc, tok_ya_par[e8 % 2]), inc=VV)
                    tok_ch[(hc, e8)] = tch
                    tok_tch_last = tch
                    if e8 == 0:
                        S.emit("vector", lambda e, chb=chb: e.tensor_copy(
                            macc[0][:, :], chb[:, :]))
                    elif e8 < 7:
                        S.emit("vector", lambda e, e8=e8, chb=chb: e.tensor_add(
                            macc[e8 % 2][:, :], macc[(e8 - 1) % 2][:, :], chb[:, :]))
                    else:
                        tok_mix_last = S.emit(
                            "vector", lambda e, hc=hc, chb=chb: e.tensor_add(
                                mixed[:, hc * TS:(hc + 1) * TS], macc[0][:, :], chb[:, :]),
                            waits=(tok_mix_rd if (s > 0 and hc == 0) else None,),
                            inc=VV)
                emit_ya(hc, 7)
                tok_b_used[pb] = tok_trans_g_last
                tok_dA_used[pb] = tok_ya
            tok_xa_rd = tok_trans_g_last
            tok_cb_rd = tok_tch_last
            tok_base_last[0] = tok_base_done.get(nhc - 1 - (nhc - 1) % 2)
            tok_base_last[1] = tok_base_done.get(nhc - 1 - (nhc) % 2)
            if "mix0" in dbg and s == 0:
                s_dma(dbg_tensors["mix0"][:, :], mixed[:, :], waits=(tok_mix_last,))

            S.emit("scalar", lambda e: e.activation(
                ya_sb[:, 0:TS], pC[:, 0:512], AF.Copy, scale=ALPHA),
                waits=(tok_ya,))
            tok_ya_ev = S.emit("scalar", lambda e: e.activation(
                ya_sb[:, TS:2 * TS], pC[:, 512:1024], AF.Copy, scale=ALPHA), inc=SS)
            if "ya0" in dbg and s == 0:
                s_dma(dbg_tensors["ya0"][:, :], ya_sb[:, :], waits=(tok_ya_ev,))


            if stop_e == "hloop":
                S.emit("sync", lambda e: None,
                       waits=((GD, S.cnt[GD]), (SD, S.cnt[SD]), (VV, S.cnt[VV]),
                              (SS, S.cnt[SS]), (TM, S.cnt[TM])))
                S.build(blk)
                return nc, sorted(dbg_tensors.keys())
            tok_dn_ev = {}
            for dc in range(16):
                dpb = dc % 2
                dsl = slice(dpb * 512, (dpb + 1) * 512)
                tok_wdn = g_dma(
                    wdn_sl[dpb].ap().rearrange("p (c m) -> p c m", m=128),
                    d["wdT"].rearrange("(c p) n -> p c n", p=128)[:, :, dc * 128:(dc + 1) * 128],
                    waits=(tok_wdn_used[dpb],) +
                    (bD if (s == 0 and dc < 2) else ()))
                for hc in range(nhc):
                    S.emit("tensor", lambda e, hc=hc, dpb=dpb, dsl=dsl: e.matmul(
                        pB[:, dsl], wdn_sl[dpb][:, hc * 128:(hc + 1) * 128],
                        mixed[:, hc * TS:(hc + 1) * TS],
                        start=(hc == 0), stop=False, skip_group_check=True),
                        waits=(tok_wdn, tok_mix_last,
                               tok_h.get((nhc - 1, 7)) if dc == 0 else None,
                               tok_dn_ev.get(dc - 2)) if hc == 0 else ())
                tdn = None
                for gi in range(2):
                    tdn = S.emit("tensor", lambda e, gi=gi, dc=dc, dsl=dsl: e.matmul(
                        pB[:, dsl], dB_sb[gi][:, dc * 128:(dc + 1) * 128],
                        ya_sb[:, gi * TS:(gi + 1) * TS],
                        start=False, stop=(gi == 1), skip_group_check=True),
                        waits=(tok_ya_ev, tok_dB[gi]) if gi == 0 else (),
                        inc=TM if gi == 1 else None)
                tok_wdn_used[dpb] = tdn
                tok_dn_ev[dc] = S.emit("scalar", lambda e, dc=dc, dsl=dsl: e.copy(
                    dout[:, dc * TS:(dc + 1) * TS], pB[:, dsl]), waits=(tdn,), inc=SS)
            tok_mix_rd = tdn
            prev_dn_ev = {0: tok_dn_ev[14], 1: tok_dn_ev[15]}


            if stop_e == "down":
                S.emit("sync", lambda e: None,
                       waits=((GD, S.cnt[GD]), (SD, S.cnt[SD]), (VV, S.cnt[VV]),
                              (SS, S.cnt[SS]), (TM, S.cnt[TM])))
                S.build(blk)
                return nc, sorted(dbg_tensors.keys())
            pTd = pT.ap().rearrange("p (c t) -> p c t", t=128)
            for tc in range(4):
                fp = tc % 2
                it = s * 4 + tc
                ty = s_dma(y1e[fp][:, :], out_d[it * 128:(it + 1) * 128, :],
                           waits=(tok_D_y1_last,
                                  tok_out[-2] if len(tok_out) >= 2 else None) +
                           (bD if (s == 0 and tc < 2) else ()))
                ttr = None
                for dc in range(16):
                    ttr = S.emit("tensor", lambda e, dc=dc, tc=tc: e.transpose(
                        pTd[:, dc, :], dout[:, dc * TS + tc * 128: dc * TS + tc * 128 + 128],
                        id_bf[:, :]),
                        waits=(tok_dn_ev[15], tok_tfa_prev,
                               tok_C_pT if (s == 0 and tc == 0) else None)
                        if dc == 0 else (),
                        inc=TM if dc == 15 else None)
                if stop_e == "fin1":
                    S.emit("sync", lambda e: None,
                           waits=((GD, S.cnt[GD]), (SD, S.cnt[SD]), (VV, S.cnt[VV]),
                                  (SS, S.cnt[SS]), (TM, S.cnt[TM])))
                    S.build(blk)
                    return nc, sorted(dbg_tensors.keys())
                tfa = S.emit("vector", lambda e, fp=fp: e.tensor_add(
                    oste[fp][:, :], y1e[fp][:, :], pT[:, :]),
                    waits=(ttr, ty), inc=VV)
                tok_tfa_prev = tfa
                if stop_e == "fin2":
                    S.emit("sync", lambda e: None,
                           waits=((GD, S.cnt[GD]), (SD, S.cnt[SD]), (VV, S.cnt[VV]),
                                  (SS, S.cnt[SS]), (TM, S.cnt[TM])))
                    S.build(blk)
                    return nc, sorted(dbg_tensors.keys())
                tok_final_add[fp] = tfa
                tok_out.append(s_dma(out_d[it * 128:(it + 1) * 128, :], oste[fp][:, :],
                                     waits=(tfa,)))

        S.emit("sync", lambda e: None,
               waits=tuple(tok_out[-2:]) + ((GD, S.cnt[GD]), (VV, S.cnt[VV]),
                                            (SS, S.cnt[SS]), (TM, S.cnt[TM])))

        S.build(blk)

    return nc, sorted(dbg_tensors.keys())


def _prep_inputs(inp):
    f32 = np.float32
    Wk = np.asarray(inp["L0_attn_Wk"], f32)
    Wv = np.asarray(inp["L0_attn_Wv"], f32)
    Wg = np.asarray(inp["L0_attn_Wg"], f32)
    Wp = np.asarray(inp["L0_attn_Wp"], f32)
    lq = np.asarray(inp["L0_attn_latent_queries"], f32)
    gw = np.asarray(inp["L0_ffn_gate_w"], f32)
    w_up = np.asarray(inp["L0_ffn_w_up"], f32)
    w_gate = np.asarray(inp["L0_ffn_w_gate"], f32)
    w_down = np.asarray(inp["L0_ffn_w_down"], f32)
    up_A = np.asarray(inp["L0_ffn_up_A"], f32)
    up_B = np.asarray(inp["L0_ffn_up_B"], f32)
    gate_A = np.asarray(inp["L0_ffn_gate_A"], f32)
    gate_B = np.asarray(inp["L0_ffn_gate_B"], f32)
    down_A = np.asarray(inp["L0_ffn_down_A"], f32)
    down_B = np.asarray(inp["L0_ffn_down_B"], f32)

    permd = np.concatenate([np.arange(0, HD, 2), np.arange(1, HD, 2)])
    perm = np.concatenate([h * HD + permd for h in range(NH)])
    wk_p = np.ascontiguousarray(Wk[:, perm])
    lq_p = np.ascontiguousarray(lq[0].transpose(1, 2, 0)[:, permd, :])

    def a_dup(A):
        T1 = np.zeros((C, 128), f32)
        T2 = np.zeros((C, 128), f32)
        T1[:, 0:16] = A[0].T
        T1[:, 32:48] = A[0].T; T1[:, 48:64] = A[1].T
        T1[:, 64:80] = A[1].T; T1[:, 80:96] = A[2].T
        T1[:, 96:112] = A[2].T; T1[:, 112:128] = A[3].T
        T2[:, 0:16] = A[3].T; T2[:, 16:32] = A[4].T
        T2[:, 32:48] = A[4].T; T2[:, 48:64] = A[5].T
        T2[:, 64:80] = A[5].T; T2[:, 80:96] = A[6].T
        T2[:, 96:112] = A[6].T; T2[:, 112:128] = A[7].T
        return np.ascontiguousarray(T1), np.ascontiguousarray(T2)

    def b_dup(B):
        Bt = B.transpose(0, 2, 1)
        T1 = np.zeros((128, DH), f32)
        T2 = np.zeros((128, DH), f32)
        T1[0:16] = Bt[0]
        T1[32:48] = Bt[0]; T1[48:64] = Bt[1]
        T1[64:80] = Bt[1]; T1[80:96] = Bt[2]
        T1[96:112] = Bt[2]; T1[112:128] = Bt[3]
        T2[0:16] = Bt[3]; T2[16:32] = Bt[4]
        T2[32:48] = Bt[4]; T2[48:64] = Bt[5]
        T2[64:80] = Bt[5]; T2[80:96] = Bt[6]
        T2[96:112] = Bt[6]; T2[112:128] = Bt[7]
        return np.ascontiguousarray(T1), np.ascontiguousarray(T2)

    aup1, aup2 = a_dup(up_A)
    ag1, ag2 = a_dup(gate_A)
    bup1, bup2 = b_dup(up_B)
    bg1, bg2 = b_dup(gate_B)
    dAT = np.ascontiguousarray(down_A.transpose(2, 0, 1).reshape(DH, 128))
    dBA = np.zeros((128, C), f32)
    dBB = np.zeros((128, C), f32)
    for e in range(4):
        dBA[32 * e:32 * e + 16] = down_B[e].T
        dBB[32 * e:32 * e + 16] = down_B[4 + e].T

    idn = np.eye(128, dtype=f32)
    lrow = np.arange(128, dtype=np.int64) % 64
    trow = np.arange(128, dtype=np.int64)
    maskp = np.where(trow[None, :] <= lrow[:, None], 0.0, NEGM).astype(f32)
    inv_freq = (1.0 / (10000.0 ** (np.arange(0, NL).astype(f32) * 2.0 / HD))).astype(f32)
    ang = (np.arange(128).astype(f32)[:, None] * inv_freq[None, :]).astype(f32)
    cosT = np.cos(ang).astype(f32)
    sinT = np.sin(ang).astype(f32)
    iota8 = np.ascontiguousarray(
        np.tile(np.arange(NE, dtype=f32), (128, 16)).reshape(128, 128))
    iotaB = (iota8 + BIGI).astype(f32)
    zeros16 = np.zeros((RK, 128), f32)

    return dict(
        wk=wk_p, wv=np.ascontiguousarray(Wv), wg=np.ascontiguousarray(Wg),
        wp=np.ascontiguousarray(Wp), lq=lq_p,
        gwT=np.ascontiguousarray(gw.T),
        wupT=np.ascontiguousarray(w_up.T), wgT=np.ascontiguousarray(w_gate.T),
        wdT=np.ascontiguousarray(w_down.T),
        aup1=aup1, aup2=aup2, ag1=ag1, ag2=ag2,
        bup1=bup1, bup2=bup2, bg1=bg1, bg2=bg2,
        dAT=dAT, dBA=dBA, dBB=dBB,
        idn=idn, maskp=maskp, cosT=cosT, sinT=sinT,
        iota8=iota8, iotaB=iotaB, zeros16=zeros16,
    )


def kernel(**inputs):
    x = np.asarray(inputs["L0_attn_x"], np.float32)
    shared = _prep_inputs(inputs)
    nc, dbg_names = build_program(debug=_DEBUG)
    in_maps = []
    for i in range(NCORE):
        m = dict(shared)
        m["x"] = np.ascontiguousarray(x[i])
        in_maps.append(m)
    res = run_bass_kernel_spmd(nc, in_maps, core_ids=list(range(NCORE)))
    out = np.stack([res.results[i]["out"] for i in range(NCORE)], axis=0)
    if _DEBUG:
        kernel._dbg = [{n: res.results[i]["dbg_" + n] for n in dbg_names}
                       for i in range(NCORE)]
    return out


# version 03
# speedup vs baseline: 1.4359x; 1.0976x over previous
import numpy as np
import ml_dtypes
from contextlib import ExitStack
import concourse.bass as bass
import concourse.mybir as mybir
from concourse.bass_utils import run_bass_kernel_spmd

F32 = mybir.dt.float32
BF16 = mybir.dt.bfloat16
AF = mybir.ActivationFunctionType
ALU = mybir.AluOpType
AX = mybir.AxisListType

T = 2048
C = 2048
NH = 16
HD = 128
NL = 64
NE = 8
RK = 16
DH = 5632
NHC = DH // 128
NT = T // 128
TS = 512
NSUB = T // TS
NCORE = 8

SCALE = float(1.0 / np.sqrt(128.0))
NEGM = float(-1e9 / SCALE)
ALPHA = 2.0
EPS = 1e-5
BIGI = 1.0e6

_DEBUG = []


class Sched:
    ENGINES = ("sync", "gpsimd", "tensor", "vector", "scalar")

    def __init__(self, nc, ctx):
        self.nc = nc
        self.ctx = ctx
        self.ops = {e: [] for e in self.ENGINES}
        self.sems = {}
        self.cnt = {}

    def _sem(self, name):
        if name not in self.sems:
            self.sems[name] = self.ctx.enter_context(self.nc.semaphore(name))
            self.cnt[name] = 0
        return self.sems[name]

    def emit(self, engine, fn, waits=(), inc=None, inc_by=1):
        import inspect
        tok = None
        if inc is not None:
            self._sem(inc)
            self.cnt[inc] += inc_by
            tok = (inc, self.cnt[inc])
        ws = tuple(w for w in waits if w is not None)
        for s, _ in ws:
            self._sem(s)
        lineno = inspect.currentframe().f_back.f_lineno
        self.ops[engine].append((ws, fn, inc, inc_by, lineno))
        return tok

    def build(self, block):
        def mk(engine):
            def body(eng):
                last_wait = {}
                for idx, (ws, fn, inc, inc_by, lineno) in enumerate(self.ops[engine]):
                    for (s, v) in ws:
                        if last_wait.get(s, -1) >= v:
                            continue
                        eng.wait_ge(self.sems[s], v)
                        last_wait[s] = v
                    try:
                        inst = fn(eng)
                    except Exception as ex:
                        raise RuntimeError(
                            f"op failed on {engine} idx {idx} (line {lineno}): {ex}"
                        ) from ex
                    if inc is not None:
                        assert inst is not None
                        inst.then_inc(self.sems[inc], inc_by)
            return body
        block.sync(mk("sync"))
        block.gpsimd(mk("gpsimd"))
        block.tensor(mk("tensor"))
        block.vector(mk("vector"))
        block.scalar(mk("scalar"))


def build_program(debug=(), stop_after="E", nsub=NSUB, nhc=NHC, stop_e=None):
    nc = bass.Bass("TRN2", target_bir_lowering=False, debug=False)
    dbg = set(debug)

    d = {}
    def din(name, shape):
        d[name] = nc.dram_tensor(name, list(shape), F32, kind="ExternalInput").ap()
    din("x", (T, C))
    din("wk", (C, C))
    din("wv", (C, C))
    din("wg", (C, NH * NL))
    din("wp", (C, C))
    din("lq", (NH, HD, NL))
    din("gwT", (C, NE))
    din("wupT", (C, DH))
    din("wgT", (C, DH))
    din("wdT", (DH, C))
    din("aup1", (C, 128)); din("aup2", (C, 128))
    din("ag1", (C, 128)); din("ag2", (C, 128))
    din("bup1", (128, DH)); din("bup2", (128, DH))
    din("bg1", (128, DH)); din("bg2", (128, DH))
    din("dAT", (DH, 128))
    din("dBA", (128, C)); din("dBB", (128, C))
    din("idn", (128, 128))
    din("maskp", (128, 128))
    din("cosT", (128, NL)); din("sinT", (128, NL))
    din("iota8", (128, 128)); din("iotaB", (128, 128))
    din("zeros16", (RK, 128))

    out_d = nc.dram_tensor("out", [T, C], F32, kind="ExternalOutput").ap()

    xsp_d = nc.dram_tensor("xspill", [128, NT, C], BF16, kind="Internal").ap()
    bscr_d = [nc.dram_tensor(f"bscr{i}", [128, DH], BF16, kind="Internal").ap()
              for i in range(4)]
    cT_d = nc.dram_tensor("cTd", [128, 128], BF16, kind="Internal").ap()
    st_d = nc.dram_tensor("std", [1, 8], F32, kind="Internal").ap()

    dbg_tensors = {}
    def ddbg(name, shape, dtype=F32):
        if name in dbg:
            dbg_tensors[name] = nc.dram_tensor(
                "dbg_" + name, list(shape), dtype, kind="ExternalOutput").ap()
    ddbg("xn0", (128, C))
    ddbg("glog", (128, 128))
    ddbg("csel", (128, 128))
    ddbg("kf", (128, C))
    ddbg("krope", (128, C), BF16)
    ddbg("vbf", (128, C), BF16)
    ddbg("attn", (128, 8 * 128), BF16)
    ddbg("gfm", (128, 8 * T), BF16)
    ddbg("wtd", (128, NH * TS), BF16)
    ddbg("eg15", (128, NH * NL), BF16)
    ddbg("wgsb", (128, 2048), BF16)
    ddbg("xa0", (128, TS), BF16)
    ddbg("mix0", (128, NHC * TS), BF16)
    ddbg("ya0", (128, 2 * TS), BF16)

    with ExitStack() as ctx:
        E = ctx.enter_context
        S = Sched(nc, ctx)
        GD, SD, TM, VV, SS = "gd", "sd", "tm", "vv", "ss"

        id_bf = E(nc.sbuf_tensor("id_bf", [128, 128], BF16))
        mask_t = E(nc.sbuf_tensor("mask_t", [128, 128], F32))
        cos_t = E(nc.sbuf_tensor("cos_t", [128, NL], F32))
        sin_t = E(nc.sbuf_tensor("sin_t", [128, NL], F32))
        iota_t = E(nc.sbuf_tensor("iota_t", [128, 128], F32))
        iotaB_t = E(nc.sbuf_tensor("iotaB_t", [128, 128], F32))
        z16_t = E(nc.sbuf_tensor("z16_t", [RK, 128], BF16))
        lq_t = E(nc.sbuf_tensor("lq_t", [128, NH * NL], BF16))
        gwT_t = E(nc.sbuf_tensor("gwT_t", [128, 16 * NE], BF16))
        attn_t = E(nc.sbuf_tensor("attn_t", [128, 8 * 128], BF16))
        glog_t = E(nc.sbuf_tensor("glog_t", [128, 128], F32))
        csel_t = E(nc.sbuf_tensor("csel_t", [128, 128], F32))
        cm_t = E(nc.sbuf_tensor("cm_t", [128, 96], F32))
        stt_t = E(nc.sbuf_tensor("stt_t", [128, 16], F32))
        mxb_t = E(nc.sbuf_tensor("mxb_t", [128, 8], F32))

        pA = E(nc.psum_tensor("pA", [128, 1024], F32))
        pB = E(nc.psum_tensor("pB", [128, 1024], F32))
        pC = E(nc.psum_tensor("pC", [128, 1024], F32))
        pT = E(nc.psum_tensor("pT", [128, 2048], BF16))

        blk = E(nc.Block())

        def g_dma(out, in_, waits=()):
            return S.emit("gpsimd", lambda e, o=out, i=in_: e.dma_start(out=o, in_=i),
                          waits=waits, inc=GD, inc_by=16)

        def s_dma(out, in_, waits=()):
            return S.emit("sync", lambda e, o=out, i=in_: e.dma_start(out=o, in_=i),
                          waits=waits, inc=SD, inc_by=16)

        tk_id = g_dma(id_bf[:, :], d["idn"][:, :])
        tk_mask = s_dma(mask_t[:, :], d["maskp"][:, :])
        tk_cos = s_dma(cos_t[:, :], d["cosT"][:, :])
        tk_sin = s_dma(sin_t[:, :], d["sinT"][:, :])
        tk_iota = s_dma(iota_t[:, :], d["iota8"][:, :])
        tk_iotaB = s_dma(iotaB_t[:, :], d["iotaB"][:, :])
        tk_z16 = g_dma(z16_t[:, :], d["zeros16"][:, :])
        tk_lq = g_dma(lq_t.ap().rearrange("p (h l) -> p h l", l=NL),
                      d["lq"].rearrange("h p l -> p h l"))
        tk_gwT = g_dma(gwT_t.ap().rearrange("p (c e) -> p c e", e=NE),
                       d["gwT"].rearrange("(c p) e -> p c e", p=128))

        NEG1 = (32, 64, 96)
        NEG2 = (0, 32, 64, 96)
        with nc.sbuf_tensor("bstg", [128, DH], BF16) as bstg, \
             nc.sbuf_tensor("bneg", [128, DH], BF16) as bneg:
            tok_bscr = None
            tok_bneg = None
            for i, name in enumerate(("bup1", "bup2", "bg1", "bg2")):
                tld = g_dma(bstg[:, :], d[name][:, :],
                            waits=(tok_bscr,) if tok_bscr else ())
                tok_bneg = S.emit("vector",
                                  lambda e: e.tensor_scalar_mul(bneg[:, :], bstg[:, :], -1.0),
                                  waits=(tld,), inc=VV)
                negs = NEG1 if (i % 2 == 0) else NEG2
                for lo in range(0, 128, 16):
                    src = bneg if lo in negs else bstg
                    tok_bscr = s_dma(bscr_d[i][lo:lo + 16, :], src[lo:lo + 16, :],
                                     waits=(tok_bneg,))

        es_gfm = ExitStack()
        Gfm = es_gfm.enter_context(nc.sbuf_tensor("Gfm", [128, 8 * T], BF16))
        es_x = ExitStack()
        X_sb = es_x.enter_context(nc.sbuf_tensor("X_sb", [128, NT * C], BF16))
        with nc.sbuf_tensor("wg_sb", [128, 16 * NH * NL], BF16) as wg_sb, \
             nc.sbuf_tensor("xt0", [128, C], F32) as xt0, \
             nc.sbuf_tensor("xt1", [128, C], F32) as xt1, \
             nc.sbuf_tensor("xn_bf", [128, C], BF16) as xn_bf, \
             nc.sbuf_tensor("E_bf", [128, NH * NL], BF16) as E_bf, \
             nc.sbuf_tensor("Gn_bf", [128, NH * NL], BF16) as Gn_bf, \
             nc.sbuf_tensor("lnsc", [128, 8], F32) as lnsc, \
             nc.sbuf_tensor("sqd", [128, C], F32) as sqd, \
             nc.sbuf_tensor("grec", [128, NH], F32) as grec:

            tok_wg = None
            for c in range(16):
                tok_wg = g_dma(wg_sb[:, c * NH * NL:(c + 1) * NH * NL],
                               d["wg"][c * 128:(c + 1) * 128, :],
                               waits=(tok_bscr, tok_bneg) if c == 0 else ())

            X3 = X_sb.ap().rearrange("p (c t) -> p c t", t=T)
            pT3 = pT.ap().rearrange("p (c t) -> p c t", t=128)
            Gfm3 = Gfm.ap().rearrange("p (h t) -> p h t", t=T)

            tok_x = [None, None]
            tok_xn = [None, None]
            tok_xev = None
            tok_gev = None
            tok_exp = None
            tok_glev = None
            tok_gtr = None
            tok_gn = None

            for it in range(NT):
                pb = it % 2
                xt = (xt0, xt1)[pb]
                tok_x[pb] = s_dma(xt[:, :], d["x"][it * 128:(it + 1) * 128, :],
                                  waits=(tok_xn[pb],))
                S.emit("vector", lambda e, x=xt: e.reduce_sum(
                    lnsc[:, 0:1], x[:, :], axis=AX.X), waits=(tok_x[pb],))
                S.emit("vector", lambda e: e.drain())
                t2 = S.emit("vector", lambda e: e.tensor_scalar_mul(
                    lnsc[:, 1:2], lnsc[:, 0:1], -1.0 / C), inc=VV)
                t3 = S.emit("scalar", lambda e, x=xt: e.activation(
                    sqd[:, :], x[:, :], AF.Square, bias=lnsc[:, 1:2],
                    accum_out=lnsc[:, 2:3]), waits=(t2,), inc=SS)
                S.emit("vector", lambda e: e.tensor_scalar(
                    lnsc[:, 3:4], lnsc[:, 2:3], 1.0 / C, EPS,
                    op0=ALU.mult, op1=ALU.add), waits=(t3,))
                S.emit("vector", lambda e: e.drain())
                t5 = S.emit("vector", lambda e: e.reciprocal(
                    lnsc[:, 4:5], lnsc[:, 3:4]), inc=VV)
                t6 = S.emit("scalar", lambda e: e.activation(
                    lnsc[:, 5:6], lnsc[:, 4:5], AF.Sqrt), waits=(t5,), inc=SS)
                tok_xn[pb] = S.emit("vector", lambda e, x=xt: e.tensor_scalar(
                    xn_bf[:, :], x[:, :], lnsc[:, 1:2], lnsc[:, 5:6],
                    op0=ALU.add, op1=ALU.mult), waits=(t6,), inc=VV)
                if it == 0 and "xn0" in dbg:
                    tdbg = S.emit("vector", lambda e: e.tensor_copy(
                        sqd[:, :], xn_bf[:, :]), waits=(tok_xn[pb],), inc=VV)
                    s_dma(dbg_tensors["xn0"][:, :], sqd[:, :], waits=(tdbg,))
                tt = None
                for c in range(16):
                    tt = S.emit("tensor", lambda e, c=c: e.transpose(
                        pT3[:, c, :], xn_bf[:, c * 128:(c + 1) * 128], id_bf[:, :]),
                        waits=(tok_xn[pb], tk_id, tok_gev) if c == 0 else (),
                        inc=TM if c == 15 else None)
                tok_xev = S.emit("scalar", lambda e, it=it: e.copy(
                    X3[:, :, it * 128:(it + 1) * 128], pT3[:, :, :]),
                    waits=(tt,), inc=SS)
                tg = None
                for c in range(16):
                    tg = S.emit("tensor", lambda e, c=c, it=it: e.matmul(
                        pB[:, 512:520],
                        X_sb[:, c * C + it * 128: c * C + it * 128 + 128],
                        gwT_t[:, c * NE:(c + 1) * NE],
                        start=(c == 0), stop=(c == 15), skip_group_check=True),
                        waits=(tok_xev, tok_glev, tk_gwT) if c == 0 else (),
                        inc=TM if c == 15 else None)
                tok_glev = S.emit("vector", lambda e, it=it: e.tensor_copy(
                    glog_t[:, it * NE:(it + 1) * NE], pB[:, 512:520]),
                    waits=(tg,), inc=VV)
                tG = None
                for c in range(16):
                    for nh in range(2):
                        tG = S.emit("tensor", lambda e, c=c, it=it, nh=nh: e.matmul(
                            pA[:, nh * 512:(nh + 1) * 512],
                            X_sb[:, c * C + it * 128: c * C + it * 128 + 128],
                            wg_sb[:, c * NH * NL + nh * 512: c * NH * NL + (nh + 1) * 512],
                            start=(c == 0), stop=(c == 15), skip_group_check=True),
                            waits=(tok_xev, tok_exp, tok_wg) if (c == 0 and nh == 0) else (),
                            inc=TM if (c == 15 and nh == 1) else None)
                tok_exp = S.emit("scalar", lambda e: e.activation(
                    E_bf[:, :], pA[:, :], AF.Exp, scale=SCALE),
                    waits=(tG, tok_gn), inc=SS)
                S.emit("vector", lambda e: e.reduce_sum(
                    grec.ap().rearrange("p (h o) -> p h o", o=1),
                    E_bf.ap().rearrange("p (h l) -> p h l", l=NL), axis=AX.X),
                    waits=(tok_exp,))
                S.emit("vector", lambda e: e.drain())
                S.emit("vector", lambda e: e.reciprocal(grec[:, :], grec[:, :]), inc=VV)
                S.emit("vector", lambda e: e.drain())
                tok_gn = S.emit("vector", lambda e: e.tensor_tensor(
                    Gn_bf.ap().rearrange("p (h l) -> p h l", l=NL),
                    E_bf.ap().rearrange("p (h l) -> p h l", l=NL),
                    grec.ap().rearrange("p (h o) -> p h o", o=1).to_broadcast([128, NH, NL]),
                    ALU.mult), waits=(tok_gtr,), inc=VV)
                tok_gtr = None
                for pr in range(8):
                    tok_gtr = S.emit("tensor", lambda e, pr=pr: e.transpose(
                        pT3[:, pr, :], Gn_bf[:, pr * 128:(pr + 1) * 128], id_bf[:, :]),
                        waits=(tok_gn, tok_xev) if pr == 0 else (),
                        inc=TM if pr == 7 else None)
                tok_gev = S.emit("scalar", lambda e, it=it: e.copy(
                    Gfm3[:, :, it * 128:(it + 1) * 128], pT3[:, 0:8, :]),
                    waits=(tok_gtr,), inc=SS)

            if "wgsb" in dbg:
                s_dma(dbg_tensors["wgsb"][:, :], wg_sb[:, 0:2048], waits=(tok_wg,))
            if "eg15" in dbg:
                s_dma(dbg_tensors["eg15"][:, :], E_bf[:, :], waits=(tok_gn,))
            tok_A_gev = tok_gev
            tok_A_glev = tok_glev
            tok_A_exp = tok_exp
            tok_A_xev = tok_xev
            tok_A_gtr = tok_gtr
            tok_A_gn = tok_gn
            bA = (tok_A_gev, tok_A_glev, tok_A_exp, tok_A_xev, tok_A_gtr, tok_A_gn)
            if "glog" in dbg:
                s_dma(dbg_tensors["glog"][:, :], glog_t[:, :], waits=(tok_glev,))

        with nc.sbuf_tensor("wkv0", [128, C], BF16) as wkv0, \
             nc.sbuf_tensor("wkv1", [128, C], BF16) as wkv1, \
             nc.sbuf_tensor("kf", [128, C], F32) as kf, \
             nc.sbuf_tensor("k_bf", [128, C], BF16) as k_bf, \
             nc.sbuf_tensor("v_bf", [128, C], BF16) as v_bf, \
             nc.sbuf_tensor("kfm", [128, C], BF16) as kfm, \
             nc.sbuf_tensor("rta", [128, NH * NL], F32) as rta, \
             nc.sbuf_tensor("rtb", [128, NH * NL], F32) as rtb, \
             nc.sbuf_tensor("wsc", [128, 128], F32) as wsc, \
             nc.sbuf_tensor("we", [128, 128], F32) as we, \
             nc.sbuf_tensor("wn", [128, 128], BF16) as wn, \
             nc.sbuf_tensor("wT", [128, 128], BF16) as wT, \
             nc.sbuf_tensor("scs", [128, 8], F32) as scs:

            tok_kv = {}
            for which, wname in (("k", "wk"), ("v", "wv")):
                tok_w = [None, None]
                tchunk = {}
                tmm = None
                for c in range(16):
                    wb = (wkv0, wkv1)[c % 2]
                    extra = (tok_kv["k"],) if (which == "v" and c < 2) else ()
                    tok_w[c % 2] = g_dma(
                        wb[:, :], d[wname][c * 128:(c + 1) * 128, :],
                        waits=(tchunk.get(c - 2),) + extra +
                        (bA if (which == "k" and c < 2) else ()))
                    for nq in range(4):
                        ps = (pA, pA, pB, pB)[nq]
                        sl = (slice(0, 512), slice(512, 1024))[nq % 2]
                        tmm = S.emit("tensor", lambda e, c=c, wb=wb, ps=ps, sl=sl, nq=nq: e.matmul(
                            ps[:, sl], X_sb[:, c * C: c * C + 128],
                            wb[:, nq * 512:(nq + 1) * 512],
                            start=(c == 0), stop=(c == 15), skip_group_check=True),
                            waits=(tok_w[c % 2], tok_A_exp, tok_A_glev,
                                   tok_kv.get(which)) if nq == 0 else (),
                            inc=TM if nq == 3 else None)
                    tchunk[c] = tmm
                if which == "k":
                    S.emit("vector", lambda e: e.tensor_copy(kf[:, 0:1024], pA[:, :]),
                           waits=(tmm,))
                    tok_kv["k"] = S.emit("vector", lambda e: e.tensor_copy(
                        kf[:, 1024:2048], pB[:, :]), inc=VV)
                else:
                    S.emit("vector", lambda e: e.tensor_copy(v_bf[:, 0:1024], pA[:, :]),
                           waits=(tmm,))
                    tok_kv["v"] = S.emit("vector", lambda e: e.tensor_copy(
                        v_bf[:, 1024:2048], pB[:, :]), inc=VV)

            kf3 = kf.ap().rearrange("p (h d) -> p h d", d=HD)
            kb3 = k_bf.ap().rearrange("p (h d) -> p h d", d=HD)
            cosb = cos_t.ap().rearrange("p (o j) -> p o j", o=1).to_broadcast([128, NH, NL])
            sinb = sin_t.ap().rearrange("p (o j) -> p o j", o=1).to_broadcast([128, NH, NL])
            rta3 = rta.ap().rearrange("p (h j) -> p h j", j=NL)
            rtb3 = rtb.ap().rearrange("p (h j) -> p h j", j=NL)
            S.emit("vector", lambda e: e.tensor_tensor(
                rta3[:, :, :], kf3[:, :, 0:NL], cosb, ALU.mult),
                waits=(tok_kv["k"], tk_cos, tk_sin))
            S.emit("vector", lambda e: e.tensor_tensor(
                rtb3[:, :, :], kf3[:, :, NL:HD], sinb, ALU.mult))
            S.emit("vector", lambda e: e.tensor_tensor(
                kb3[:, :, 0:NL], rta3[:, :, :], rtb3[:, :, :], ALU.subtract), inc=VV)
            S.emit("vector", lambda e: e.tensor_tensor(
                rta3[:, :, :], kf3[:, :, 0:NL], sinb, ALU.mult))
            S.emit("vector", lambda e: e.tensor_tensor(
                rtb3[:, :, :], kf3[:, :, NL:HD], cosb, ALU.mult))
            tok_rope = S.emit("vector", lambda e: e.tensor_tensor(
                kb3[:, :, NL:HD], rta3[:, :, :], rtb3[:, :, :], ALU.add), inc=VV)
            if "kf" in dbg:
                s_dma(dbg_tensors["kf"][:, :], kf[:, :], waits=(tok_kv["k"],))
            if "krope" in dbg:
                s_dma(dbg_tensors["krope"][:, :], k_bf[:, :], waits=(tok_rope,))
            if "vbf" in dbg:
                s_dma(dbg_tensors["vbf"][:, :], v_bf[:, :], waits=(tok_kv["v"],))

            pT3b = pT.ap().rearrange("p (h t) -> p h t", t=128)
            tt = None
            for h in range(NH):
                tt = S.emit("tensor", lambda e, h=h: e.transpose(
                    pT3b[:, h, :], k_bf[:, h * 128:(h + 1) * 128], id_bf[:, :]),
                    waits=(tok_rope, tok_A_gev) if h == 0 else (),
                    inc=TM if h == 15 else None)
            tok_kfm = S.emit("scalar", lambda e: e.copy(kfm[:, :], pT[:, :]),
                             waits=(tt,), inc=SS)

            tok_attn_ev = None
            tok_twe = None
            tok_tm1 = None
            for pr in range(8):
                tsc = None
                for sub in range(2):
                    h = pr * 2 + sub
                    tsc = S.emit("tensor", lambda e, h=h, sub=sub: e.matmul(
                        pC[sub * 64:(sub + 1) * 64, 0:128],
                        lq_t[:, h * NL:(h + 1) * NL],
                        kfm[:, h * 128:(h + 1) * 128],
                        start=True, stop=True, skip_group_check=True),
                        waits=(tok_kfm, tk_lq, tok_tm1) if sub == 0 else (),
                        inc=TM if sub == 1 else None)
                tok_tm1 = S.emit("vector", lambda e: e.tensor_add(
                    wsc[:, :], pC[:, 0:128], mask_t[:, :]),
                    waits=(tsc, tk_mask), inc=VV)
                S.emit("vector", lambda e: e.drain())
                tm2 = S.emit("vector", lambda e: e.reduce_max(
                    scs[:, 0:1], wsc[:, :], axis=AX.X, negate=True), inc=VV)
                S.emit("scalar", lambda e: e.mul(scs[:, 1:2], scs[:, 0:1], SCALE),
                       waits=(tm2,))
                S.emit("scalar", lambda e: e.drain())
                tm4 = S.emit("scalar", lambda e: e.activation(
                    we[:, :], wsc[:, :], AF.Exp, bias=scs[:, 1:2], scale=SCALE,
                    accum_out=scs[:, 2:3]), inc=SS)
                S.emit("vector", lambda e: e.reciprocal(scs[:, 3:4], scs[:, 2:3]),
                       waits=(tm4,))
                S.emit("vector", lambda e: e.drain())
                tm6 = S.emit("vector", lambda e: e.tensor_scalar_mul(
                    wn[:, :], we[:, :], scs[:, 3:4]), inc=VV)
                tw = S.emit("tensor", lambda e: e.transpose(
                    pT3b[:, 0, :], wn[:, :], id_bf[:, :]),
                    waits=(tm6, tok_kfm, tok_twe), inc=TM)
                tok_twe = S.emit("scalar", lambda e: e.copy(wT[:, :], pT3b[:, 0, :]),
                                 waits=(tw,), inc=SS)
                ta = None
                for sub in range(2):
                    h = pr * 2 + sub
                    ta = S.emit("tensor", lambda e, h=h, sub=sub: e.matmul(
                        pC[sub * 64:(sub + 1) * 64, 512:640],
                        wT[:, sub * 64:(sub + 1) * 64],
                        v_bf[:, h * 128:(h + 1) * 128],
                        start=True, stop=True, skip_group_check=True),
                        waits=(tok_twe, tok_kv["v"], tok_attn_ev) if sub == 0 else (),
                        inc=TM if sub == 1 else None)
                tok_attn_ev = S.emit("scalar", lambda e, pr=pr: e.copy(
                    attn_t[:, pr * 128:(pr + 1) * 128], pC[:, 512:640]),
                    waits=(ta,), inc=SS)
            if "attn" in dbg:
                s_dma(dbg_tensors["attn"][:, :], attn_t[:, :], waits=(tok_attn_ev,))
            if "gfm" in dbg:
                s_dma(dbg_tensors["gfm"][:, :], Gfm[:, :], waits=(tok_A_gev,))
            tok_B_kv_v = tok_kv["v"]
            tok_B_kfm = tok_kfm
            tok_B_twe = tok_twe

        tok_xspill = s_dma(xsp_d[:, :, :],
                           X_sb.ap().rearrange("p (c t) -> p c t", t=T),
                           waits=(tok_A_xev,))

        g3 = glog_t.ap().rearrange("p (a b) -> p a b", b=NE)
        i3 = iota_t.ap().rearrange("p (a b) -> p a b", b=NE)
        def cmb(lo, hi):
            return cm_t.ap()[:, lo:hi].rearrange(
                "p (a b) -> p a b", b=1).to_broadcast([128, 16, NE])
        def cm3(lo, hi):
            return cm_t.ap()[:, lo:hi].rearrange("p (a b) -> p a b", b=1)

        with nc.sbuf_tensor("eqw", [128, 128], F32) as eqw, \
             nc.sbuf_tensor("ind0", [128, 128], F32) as ind0, \
             nc.sbuf_tensor("ind1", [128, 128], F32) as ind1, \
             nc.sbuf_tensor("mskg", [128, 128], F32) as mskg, \
             nc.sbuf_tensor("sttb", [128, 16], BF16) as sttb, \
             nc.sbuf_tensor("rowsc", [16, 128], F32) as rowsc, \
             nc.sbuf_tensor("cselb", [128, 128], BF16) as cselb:
            eq3 = eqw.ap().rearrange("p (a b) -> p a b", b=NE)
            i03 = ind0.ap().rearrange("p (a b) -> p a b", b=NE)
            i13 = ind1.ap().rearrange("p (a b) -> p a b", b=NE)
            mg3 = mskg.ap().rearrange("p (a b) -> p a b", b=NE)

            S.emit("vector", lambda e: e.tensor_reduce(
                cm3(0, 16), g3[:, :, :], axis=AX.X, op=ALU.max),
                waits=(tok_A_glev, tok_attn_ev))
            S.emit("vector", lambda e: e.drain())
            S.emit("vector", lambda e: e.tensor_tensor(
                eq3[:, :, :], g3[:, :, :], cmb(0, 16), ALU.is_equal))
            S.emit("vector", lambda e: e.scalar_tensor_tensor(
                mskg[:, :], eqw[:, :], -BIGI, iotaB_t[:, :], ALU.mult, ALU.add),
                waits=(tk_iotaB,))
            S.emit("vector", lambda e: e.drain())
            S.emit("vector", lambda e: e.tensor_reduce(
                cm3(64, 80), mg3[:, :, :], axis=AX.X, op=ALU.min))
            S.emit("vector", lambda e: e.drain())
            S.emit("vector", lambda e: e.tensor_tensor(
                i03[:, :, :], i3[:, :, :], cmb(64, 80), ALU.is_equal),
                waits=(tk_iota,))
            S.emit("vector", lambda e: e.scalar_tensor_tensor(
                mskg[:, :], ind0[:, :], -1.0e5, glog_t[:, :], ALU.mult, ALU.add))
            S.emit("vector", lambda e: e.drain())
            S.emit("vector", lambda e: e.tensor_reduce(
                cm3(16, 32), mg3[:, :, :], axis=AX.X, op=ALU.max))
            S.emit("vector", lambda e: e.drain())
            S.emit("vector", lambda e: e.tensor_tensor(
                eq3[:, :, :], mg3[:, :, :], cmb(16, 32), ALU.is_equal))
            S.emit("vector", lambda e: e.scalar_tensor_tensor(
                mskg[:, :], eqw[:, :], -BIGI, iotaB_t[:, :], ALU.mult, ALU.add))
            S.emit("vector", lambda e: e.drain())
            S.emit("vector", lambda e: e.tensor_reduce(
                cm3(64, 80), mg3[:, :, :], axis=AX.X, op=ALU.min))
            S.emit("vector", lambda e: e.drain())
            tok_ind1 = S.emit("vector", lambda e: e.tensor_tensor(
                i13[:, :, :], i3[:, :, :], cmb(64, 80), ALU.is_equal), inc=VV)

            S.emit("vector", lambda e: e.reduce_max(
                stt_t[:, 0:1], cm_t[:, 0:16], axis=AX.X))
            tok_pm = S.emit("vector", lambda e: e.reduce_max(
                stt_t[:, 1:2], cm_t[:, 16:32], axis=AX.X), inc=VV)
            S.emit("vector", lambda e: e.drain())
            tc1 = S.emit("vector", lambda e: e.tensor_copy(
                sttb[:, :], stt_t[:, 0:16]), inc=VV)
            tc2 = S.emit("tensor", lambda e: e.transpose(
                pT.ap()[0:16, 0:128], sttb[:, :], id_bf[:, :]),
                waits=(tc1, tok_B_twe), inc=TM)
            tc3 = S.emit("scalar", lambda e: e.copy(
                rowsc[:, :], pT.ap()[0:16, 0:128]), waits=(tc2,), inc=SS)
            tc4 = S.emit("vector", lambda e: e.reduce_max(
                stt_t[0:2, 2:3], rowsc[0:2, :], axis=AX.X, negate=True),
                waits=(tc3,), inc=VV)
            tc5 = s_dma(st_d[0:1, 0:2].rearrange("o p -> p o"),
                        stt_t.ap()[0:2, 2:3], waits=(tc4,))
            tc6 = s_dma(mxb_t[:, 0:2], st_d.partition_broadcast(128)[:, 0, 0:2],
                        waits=(tc5,))
            tc7 = S.emit("scalar", lambda e: e.activation(
                cm_t[:, 32:48], cm_t[:, 0:16], AF.Exp, bias=mxb_t[:, 0:1]),
                waits=(tc6,))
            tc8 = S.emit("scalar", lambda e: e.activation(
                cm_t[:, 48:64], cm_t[:, 16:32], AF.Exp, bias=mxb_t[:, 1:2]), inc=SS)
            S.emit("vector", lambda e: e.reduce_sum(
                stt_t[:, 4:5], cm_t[:, 32:48], axis=AX.X), waits=(tc8,))
            tok_ps = S.emit("vector", lambda e: e.reduce_sum(
                stt_t[:, 5:6], cm_t[:, 48:64], axis=AX.X), inc=VV)
            S.emit("vector", lambda e: e.drain())
            tc11 = S.emit("vector", lambda e: e.tensor_copy(
                sttb[:, 0:2], stt_t[:, 4:6]), inc=VV)
            tc12 = S.emit("tensor", lambda e: e.transpose(
                pT.ap()[0:16, 128:256], sttb[:, :], id_bf[:, :]),
                waits=(tc11, tc3), inc=TM)
            tc13 = S.emit("scalar", lambda e: e.copy(
                rowsc[:, :], pT.ap()[0:16, 128:256]), waits=(tc12,), inc=SS)
            S.emit("vector", lambda e: e.reduce_sum(
                stt_t[0:2, 6:7], rowsc[0:2, :], axis=AX.X), waits=(tc13,))
            S.emit("vector", lambda e: e.drain())
            tc15 = S.emit("vector", lambda e: e.reciprocal(
                stt_t[0:2, 7:8], stt_t[0:2, 6:7]), inc=VV)
            tc16 = s_dma(st_d[0:1, 2:4].rearrange("o p -> p o"),
                         stt_t.ap()[0:2, 7:8], waits=(tc15,))
            tc17 = s_dma(mxb_t[:, 2:4], st_d.partition_broadcast(128)[:, 0, 2:4],
                         waits=(tc16,))
            S.emit("vector", lambda e: e.tensor_tensor(
                mg3[:, :, :], i03[:, :, :], cmb(32, 48), ALU.mult),
                waits=(tc17, tok_ind1))
            S.emit("vector", lambda e: e.tensor_scalar_mul(
                ind0[:, :], mskg[:, :], mxb_t[:, 2:3]))
            S.emit("vector", lambda e: e.tensor_tensor(
                eq3[:, :, :], i13[:, :, :], cmb(48, 64), ALU.mult))
            S.emit("vector", lambda e: e.tensor_scalar_mul(
                eqw[:, :], eqw[:, :], mxb_t[:, 3:4]))
            u5 = S.emit("vector", lambda e: e.tensor_add(
                csel_t[:, :], ind0[:, :], eqw[:, :]), inc=VV)
            if "csel" in dbg:
                s_dma(dbg_tensors["csel"][:, :], csel_t[:, :], waits=(u5,))
            u6 = S.emit("vector", lambda e: e.tensor_copy(
                cselb[:, :], csel_t[:, :]), waits=(u5,), inc=VV)
            u7 = S.emit("tensor", lambda e: e.transpose(
                pT.ap()[:, 256:384], cselb[:, :], id_bf[:, :]),
                waits=(u6, tc12), inc=TM)
            u8 = S.emit("scalar", lambda e: e.copy(
                cselb[:, :], pT.ap()[:, 256:384]), waits=(u7,), inc=SS)
            tok_cT = s_dma(cT_d[:, :], cselb[:, :], waits=(u8,))
            tok_C_pT = u8
            tok_C_vv = u6
        es_x.close()
        bC = (tok_cT, u8, u7, tok_C_vv, tok_attn_ev, tok_B_kv_v, tok_B_twe)

        if stop_after in ("A", "B", "C"):
            S.emit("sync", lambda e: None,
                   waits=((GD, S.cnt[GD]), (SD, S.cnt[SD]), (VV, S.cnt[VV]),
                          (SS, S.cnt[SS]), (TM, S.cnt[TM])))
            S.build(blk)
            es_gfm.close()
            return nc, sorted(dbg_tensors.keys())

        with nc.sbuf_tensor("wp_sb", [128, 16 * C], BF16) as wp_sb, \
             nc.sbuf_tensor("wtd", [128, NH * TS], BF16) as wtd, \
             nc.sbuf_tensor("xt0d", [128, C], F32) as xt0d, \
             nc.sbuf_tensor("xt1d", [128, C], F32) as xt1d, \
             nc.sbuf_tensor("y1a", [128, C], F32) as y1a, \
             nc.sbuf_tensor("y1b", [128, C], F32) as y1b:
            tok_wp = None
            for c in range(16):
                tok_wp = g_dma(wp_sb[:, c * C:(c + 1) * C],
                               d["wp"][c * 128:(c + 1) * 128, :],
                               waits=(bC + ((GD, S.cnt[GD]),)) if c == 0 else ())
            tok_xd = [None, None]
            tok_y1 = [None, None]
            wtd_ev = {0: None, 1: None}
            tok_wtd_last = None
            tok_D_pBC = None
            for sd in range(NSUB):
                tok_wtd_ev = None
                for h in range(NH):
                    hb = h % 2
                    sl = slice(hb * 512, (hb + 1) * 512)
                    pr, sub = h // 2, h % 2
                    tw = S.emit("tensor", lambda e, pr=pr, sub=sub, sd=sd, sl=sl: e.matmul(
                        pA[:, sl],
                        attn_t[sub * 64:(sub + 1) * 64, pr * 128:(pr + 1) * 128],
                        Gfm[sub * 64:(sub + 1) * 64, pr * T + sd * TS: pr * T + (sd + 1) * TS],
                        start=True, stop=True, skip_group_check=True),
                        waits=bC + (tok_A_gev, wtd_ev[hb],
                                    tok_wtd_last if h <= 1 else None),
                        inc=TM)
                    ev = S.emit("scalar", lambda e, h=h, sl=sl: e.copy(
                        wtd[:, h * TS:(h + 1) * TS], pA[:, sl]), waits=(tw,), inc=SS)
                    wtd_ev[hb] = ev
                    tok_wtd_ev = ev
                if "wtd" in dbg and sd == 0:
                    s_dma(dbg_tensors["wtd"][:, :], wtd[:, :], waits=(tok_wtd_ev,))
                for tc in range(4):
                    it = sd * 4 + tc
                    pbx = it % 2
                    xt = (xt0d, xt1d)[pbx]
                    y1 = (y1a, y1b)[pbx]
                    tok_xd[pbx] = s_dma(xt[:, :], d["x"][it * 128:(it + 1) * 128, :],
                                        waits=(tok_y1[pbx],) + (bC if it < 2 else ()))
                    tmm = None
                    for c in range(16):
                        for dc in range(4):
                            ps = (pB, pB, pC, pC)[dc]
                            sl = (slice(0, 512), slice(512, 1024))[dc % 2]
                            tmm = S.emit("tensor", lambda e, c=c, dc=dc, tc=tc, ps=ps, sl=sl: e.matmul(
                                ps[:, sl],
                                wtd[:, c * TS + tc * 128: c * TS + tc * 128 + 128],
                                wp_sb[:, c * C + dc * 512: c * C + (dc + 1) * 512],
                                start=(c == 0), stop=(c == 15), skip_group_check=True),
                                waits=(tok_wtd_ev, tok_wp, tok_B_kv_v, tok_attn_ev,
                                       tok_D_pBC) if (c == 0 and dc == 0) else (),
                                inc=TM if (c == 15 and dc == 3) else None)
                    if tc == 3:
                        tok_wtd_last = tmm
                    S.emit("vector", lambda e, y1=y1, xt=xt: e.tensor_add(
                        y1[:, 0:1024], pB[:, :], xt[:, 0:1024]),
                        waits=(tmm, tok_xd[pbx]))
                    tok_D_pBC = S.emit("vector", lambda e, y1=y1, xt=xt: e.tensor_add(
                        y1[:, 1024:2048], pC[:, :], xt[:, 1024:2048]), inc=VV)
                    tok_y1[pbx] = s_dma(out_d[it * 128:(it + 1) * 128, :], y1[:, :],
                                        waits=(tok_D_pBC,))
            tok_D_y1_last = tok_y1[1] if tok_y1[1] else tok_y1[0]
            tok_D_pA = wtd_ev
        es_gfm.close()
        bD = (tok_D_y1_last, tok_D_pBC, tok_wtd_last, tok_D_pA[0], tok_D_pA[1],
              tok_wp)

        if stop_after == "D":
            S.emit("sync", lambda e: None,
                   waits=((GD, S.cnt[GD]), (SD, S.cnt[SD]), (VV, S.cnt[VV]),
                          (SS, S.cnt[SS]), (TM, S.cnt[TM])))
            S.build(blk)
            return nc, sorted(dbg_tensors.keys())

        XsE = E(nc.sbuf_tensor("XsE", [128, 16 * TS], BF16))
        mixed = E(nc.sbuf_tensor("mixed", [128, NHC * TS], BF16))
        cbc = E(nc.sbuf_tensor("cbc", [128, NE * TS], BF16))
        xa_sb = [E(nc.sbuf_tensor(f"xa{j}", [128, TS], BF16)) for j in range(4)]
        A_sb = [E(nc.sbuf_tensor(f"A_sb{j}", [128, 16 * 128], BF16)) for j in range(4)]
        wup_sl = [E(nc.sbuf_tensor(f"wup{j}", [128, 16 * 128], BF16)) for j in range(2)]
        wg_sl = [E(nc.sbuf_tensor(f"wgs{j}", [128, 16 * 128], BF16)) for j in range(2)]
        b_sl = [E(nc.sbuf_tensor(f"bsl{j}", [128, 4 * 128], BF16)) for j in range(2)]
        dA_sl = [E(nc.sbuf_tensor(f"dAs{j}", [128, 128], BF16)) for j in range(2)]
        wdn_sl = [E(nc.sbuf_tensor(f"wdn{j}", [128, NHC * 128], BF16)) for j in range(2)]
        su_bf = [E(nc.sbuf_tensor(f"su_bf{j}", [128, TS], BF16)) for j in range(2)]
        g_bfb = [E(nc.sbuf_tensor(f"g_bfb{j}", [128, TS], BF16)) for j in range(2)]
        h_bf = E(nc.sbuf_tensor("h_bf", [128, TS], BF16))
        ch_bf = [E(nc.sbuf_tensor(f"ch{j}", [128, TS], BF16)) for j in range(2)]
        macc = [E(nc.sbuf_tensor(f"macc{j}", [128, TS], F32)) for j in range(2)]
        ya_sb = E(nc.sbuf_tensor("ya_sb", [128, 2 * TS], BF16))
        dB_sb = [E(nc.sbuf_tensor(f"dB{j}", [128, C], BF16)) for j in range(2)]
        dout = E(nc.sbuf_tensor("dout", [128, 16 * TS], BF16))
        y1e = [E(nc.sbuf_tensor(f"y1e{j}", [128, C], F32)) for j in range(2)]
        oste = [E(nc.sbuf_tensor(f"ost{j}", [128, C], F32)) for j in range(2)]

        tok_A_ld = []
        for j, nm in enumerate(("aup1", "aup2", "ag1", "ag2")):
            ta = None
            for c in range(16):
                ta = g_dma(A_sb[j][:, c * 128:(c + 1) * 128],
                           d[nm][c * 128:(c + 1) * 128, :],
                           waits=bD if (j == 0 and c == 0) else ())
            tok_A_ld.append(ta)
        tok_dB = [g_dma(dB_sb[0][:, :], d["dBA"][:, :]),
                  g_dma(dB_sb[1][:, :], d["dBB"][:, :])]

        TRANS = [(0, 0, 16), (0, 32, 32), (0, 64, 32), (0, 96, 32),
                 (1, 0, 32), (1, 32, 32), (1, 64, 32), (1, 96, 32)]

        tok_Xs = None
        tok_base_last = {0: None, 1: None}
        tok_xa_rd = None
        tok_cb_rd = None
        tok_ya_ev = None
        tok_mix_rd = None
        tok_out = []
        tok_w_used = {0: None, 1: None}
        tok_b_used = {0: None, 1: None}
        tok_dA_used = {0: None, 1: None}
        tok_wdn_used = {0: None, 1: None}
        tok_final_add = {0: None, 1: None}
        tok_tfa_prev = None
        prev_dn_ev = {0: None, 1: None}

        for s in range(nsub):
            tok_Xs = None
            for c in range(16):
                tok_Xs = s_dma(XsE[:, c * TS:(c + 1) * TS],
                               xsp_d[:, c, s * TS:(s + 1) * TS],
                               waits=((tok_xspill, tok_base_last[0],
                                       tok_base_last[1], tok_xa_rd) +
                                      (bD if s == 0 else ())) if c == 0 else ())
            tcb = None
            for e8 in range(NE):
                for q in range(4):
                    row = (4 * s + q) * NE + e8
                    tcb = s_dma(cbc[:, e8 * TS + q * 128: e8 * TS + (q + 1) * 128],
                                cT_d.partition_broadcast(128)[:, row, :],
                                waits=((tok_cT, tok_cb_rd) + (bD if s == 0 else ()))
                                if (e8 == 0 and q == 0) else ())
            tok_cbc = tcb
            tok_xa_ev = []
            for j in range(4):
                half = j % 2
                tj = None
                for c in range(16):
                    tj = S.emit("tensor", lambda e, j=j, c=c, half=half: e.matmul(
                        pA[:, half * 512:(half + 1) * 512],
                        A_sb[j][:, c * 128:(c + 1) * 128],
                        XsE[:, c * TS:(c + 1) * TS],
                        start=(c == 0), stop=(c == 15), skip_group_check=True),
                        waits=(tok_Xs, tok_A_ld[j],
                               tok_xa_ev[j - 2] if j >= 2 else
                               (tok_D_pA[half] if s == 0 else None)) if c == 0 else (),
                        inc=TM if c == 15 else None)
                ev = S.emit("scalar", lambda e, j=j, half=half: e.activation(
                    xa_sb[j][:, :], pA[:, half * 512:(half + 1) * 512],
                    AF.Copy, scale=ALPHA), waits=(tj,), inc=SS)
                tok_xa_ev.append(ev)
            if "xa0" in dbg and s == 0:
                s_dma(dbg_tensors["xa0"][:, :], xa_sb[0][:, :], waits=(tok_xa_ev[0],))


            if stop_e == "xa":
                S.emit("sync", lambda e: None,
                       waits=((GD, S.cnt[GD]), (SD, S.cnt[SD]), (VV, S.cnt[VV]),
                              (SS, S.cnt[SS]), (TM, S.cnt[TM])))
                S.build(blk)
                return nc, sorted(dbg_tensors.keys())
            S.emit("tensor", lambda e: e.matmul(
                pC[:, 0:512], z16_t[:, :], xa_sb[0][0:16, :],
                start=True, stop=True, skip_group_check=True),
                waits=(tok_xa_ev[0], tk_z16, tok_ya_ev,
                       tok_D_pBC if s == 0 else None, tok_attn_ev if s == 0 else None))
            tok_zya = S.emit("tensor", lambda e: e.matmul(
                pC[:, 512:1024], z16_t[:, :], xa_sb[0][0:16, :],
                start=True, stop=True, skip_group_check=True), inc=TM)


            if stop_e == "zero":
                S.emit("sync", lambda e: None,
                       waits=((GD, S.cnt[GD]), (SD, S.cnt[SD]), (VV, S.cnt[VV]),
                              (SS, S.cnt[SS]), (TM, S.cnt[TM])))
                S.build(blk)
                return nc, sorted(dbg_tensors.keys())
            tok_silu = {}
            tok_gev = {}
            tok_h = {}
            tok_ch = {}
            tok_ya_par = {0: None, 1: None}
            tok_ya = None
            tok_trans_g_last = None
            tok_mix_last = None
            tok_base_done = {}
            tok_wslice = {}
            tok_bslice = {}
            tok_dAs = {}
            tok_tch_last = None

            def emit_wdma(hc):
                pb = hc % 2
                tw = g_dma(
                    wup_sl[pb].ap().rearrange("p (c m) -> p c m", m=128),
                    d["wupT"].rearrange("(c p) h -> p c h", p=128)[:, :, hc * 128:(hc + 1) * 128],
                    waits=(tok_w_used[pb],) + (bD if (s == 0 and hc < 2) else ()))
                tw2 = g_dma(
                    wg_sl[pb].ap().rearrange("p (c m) -> p c m", m=128),
                    d["wgT"].rearrange("(c p) h -> p c h", p=128)[:, :, hc * 128:(hc + 1) * 128])
                tok_wslice[hc] = (tw, tw2)

            def emit_bdma(hc):
                pb = hc % 2
                tb = None
                for i in range(4):
                    tb = s_dma(b_sl[pb][:, i * 128:(i + 1) * 128],
                               bscr_d[i][:, hc * 128:(hc + 1) * 128],
                               waits=((tok_bscr, tok_b_used[pb]) +
                                      (bD if (s == 0 and hc < 2) else ()))
                               if i == 0 else ())
                tok_bslice[hc] = tb
                tok_dAs[hc] = g_dma(dA_sl[pb][:, :],
                                    d["dAT"][hc * 128:(hc + 1) * 128, :],
                                    waits=(tok_dA_used[pb],) +
                                    (bD if (s == 0 and hc < 2) else ()))

            def emit_base(hc, c0, c1):
                pb = hc % 2
                usl = slice(pb * 512, (pb + 1) * 512)
                tw, tw2 = tok_wslice[hc]
                for c in range(c0, c1):
                    S.emit("tensor", lambda e, c=c, usl=usl, pb=pb: e.matmul(
                        pA[:, usl], wup_sl[pb][:, c * 128:(c + 1) * 128],
                        XsE[:, c * TS:(c + 1) * TS],
                        start=(c == 0), stop=False, skip_group_check=True),
                        waits=(tw, tok_Xs,
                               tok_silu.get((hc - 2, 7)) if hc >= 2 else
                               tok_xa_ev[2 + pb]) if c == 0 else ())
                    tlast = S.emit("tensor", lambda e, c=c, usl=usl, pb=pb: e.matmul(
                        pB[:, usl], wg_sl[pb][:, c * 128:(c + 1) * 128],
                        XsE[:, c * TS:(c + 1) * TS],
                        start=(c == 0), stop=False, skip_group_check=True),
                        waits=(tw2,
                               tok_gev.get((hc - 2, 7)) if hc >= 2 else None,
                               prev_dn_ev[pb] if (s > 0 and hc < 2) else None,
                               tok_D_pBC if (s == 0 and hc < 2) else None)
                        if c == 0 else (),
                        inc=TM if c == c1 - 1 else None)
                    if c == 15:
                        tok_base_done[hc] = tlast
                        tok_w_used[pb] = tlast

            def emit_ya(hc, e8):
                nonlocal tok_ya
                pb = hc % 2
                grp, off = e8 // 4, (e8 % 4) * 32
                kw2 = {"tile_position": (0, 96)} if off == 96 else {}
                tok_ya = S.emit("tensor", lambda e, grp=grp, off=off, e8=e8, pb=pb,
                                kw2=dict(kw2): e.matmul(
                    pC[:, grp * 512:(grp + 1) * 512][off:off + 16, :],
                    dA_sl[pb][:, e8 * 16:(e8 + 1) * 16],
                    ch_bf[e8 % 2][:, :],
                    start=False, stop=(hc == nhc - 1 and e8 == 7),
                    skip_group_check=True, **kw2),
                    waits=(tok_ch[(hc, e8)], tok_dAs[hc], tok_zya), inc=TM)
                tok_ya_par[e8 % 2] = tok_ya

            emit_wdma(0)
            emit_base(0, 0, 16)
            for hc in range(nhc):
                pb = hc % 2
                usl = slice(pb * 512, (pb + 1) * 512)
                if hc + 1 < nhc:
                    emit_wdma(hc + 1)
                emit_bdma(hc)
                tok_b = tok_bslice[hc]
                for e8 in range(NE):
                    ti, rlo, nr = TRANS[e8]
                    kw = {"tile_position": (96, 0)} if rlo == 96 else {}
                    tup = S.emit("tensor", lambda e, ti=ti, rlo=rlo, nr=nr, pb=pb,
                                 usl=usl, kw=dict(kw): e.matmul(
                        pA[:, usl],
                        b_sl[pb][rlo:rlo + nr, ti * 128:(ti + 1) * 128],
                        xa_sb[ti][rlo:rlo + nr, :],
                        start=False, stop=False, skip_group_check=True, **kw),
                        waits=(tok_b, tok_xa_ev[ti],
                               tok_silu.get((hc, e8 - 1)) if e8 > 0 else None),
                        inc=TM)
                    tgt = S.emit("tensor", lambda e, ti=ti, rlo=rlo, nr=nr, pb=pb,
                                 usl=usl, kw=dict(kw): e.matmul(
                        pB[:, usl],
                        b_sl[pb][rlo:rlo + nr, (2 + ti) * 128:(3 + ti) * 128],
                        xa_sb[2 + ti][rlo:rlo + nr, :],
                        start=False, stop=False, skip_group_check=True, **kw),
                        waits=(tok_xa_ev[2 + ti],
                               tok_gev.get((hc, e8 - 1)) if e8 > 0 else None),
                        inc=TM)
                    if e8 == 7:
                        tok_trans_g_last = tgt
                    if hc + 1 < nhc:
                        emit_base(hc + 1, 2 * e8, 2 * e8 + 2)
                    if e8 > 0:
                        emit_ya(hc, e8 - 1)
                    par = e8 % 2
                    prev_h = (tok_h.get((hc, e8 - 2)) if e8 >= 2 else
                              tok_h.get((hc - 1, 6 + e8)))
                    ts1 = S.emit("scalar", lambda e, usl=usl, par=par: e.activation(
                        su_bf[par][:, :], pA[:, usl], AF.Silu),
                        waits=(tup, prev_h), inc=SS)
                    tok_silu[(hc, e8)] = ts1
                    tg1 = S.emit("scalar", lambda e, usl=usl, par=par: e.copy(
                        g_bfb[par][:, :], pB[:, usl]),
                        waits=(tgt,), inc=SS)
                    tok_gev[(hc, e8)] = tg1
                    th1 = S.emit("vector", lambda e, par=par: e.tensor_mul(
                        h_bf[:, :], su_bf[par][:, :], g_bfb[par][:, :]),
                        waits=(ts1, tg1), inc=VV)
                    tok_h[(hc, e8)] = th1
                    chb = ch_bf[e8 % 2]
                    tch = S.emit("vector", lambda e, e8=e8, chb=chb: e.tensor_mul(
                        chb[:, :], h_bf[:, :], cbc[:, e8 * TS:(e8 + 1) * TS]),
                        waits=(tok_cbc, tok_ya_par[e8 % 2]), inc=VV)
                    tok_ch[(hc, e8)] = tch
                    tok_tch_last = tch
                    if e8 == 0:
                        S.emit("vector", lambda e, chb=chb: e.tensor_copy(
                            macc[0][:, :], chb[:, :]))
                    elif e8 < 7:
                        S.emit("vector", lambda e, e8=e8, chb=chb: e.tensor_add(
                            macc[e8 % 2][:, :], macc[(e8 - 1) % 2][:, :], chb[:, :]))
                    else:
                        tok_mix_last = S.emit(
                            "vector", lambda e, hc=hc, chb=chb: e.tensor_add(
                                mixed[:, hc * TS:(hc + 1) * TS], macc[0][:, :], chb[:, :]),
                            waits=(tok_mix_rd if (s > 0 and hc == 0) else None,),
                            inc=VV)
                emit_ya(hc, 7)
                tok_b_used[pb] = tok_trans_g_last
                tok_dA_used[pb] = tok_ya
            tok_xa_rd = tok_trans_g_last
            tok_cb_rd = tok_tch_last
            tok_base_last[0] = tok_base_done.get(nhc - 1 - (nhc - 1) % 2)
            tok_base_last[1] = tok_base_done.get(nhc - 1 - (nhc) % 2)
            if "mix0" in dbg and s == 0:
                s_dma(dbg_tensors["mix0"][:, :], mixed[:, :], waits=(tok_mix_last,))

            S.emit("scalar", lambda e: e.activation(
                ya_sb[:, 0:TS], pC[:, 0:512], AF.Copy, scale=ALPHA),
                waits=(tok_ya,))
            tok_ya_ev = S.emit("scalar", lambda e: e.activation(
                ya_sb[:, TS:2 * TS], pC[:, 512:1024], AF.Copy, scale=ALPHA), inc=SS)
            if "ya0" in dbg and s == 0:
                s_dma(dbg_tensors["ya0"][:, :], ya_sb[:, :], waits=(tok_ya_ev,))


            if stop_e == "hloop":
                S.emit("sync", lambda e: None,
                       waits=((GD, S.cnt[GD]), (SD, S.cnt[SD]), (VV, S.cnt[VV]),
                              (SS, S.cnt[SS]), (TM, S.cnt[TM])))
                S.build(blk)
                return nc, sorted(dbg_tensors.keys())
            tok_dn_ev = {}
            for dc in range(16):
                dpb = dc % 2
                dsl = slice(dpb * 512, (dpb + 1) * 512)
                tok_wdn = g_dma(
                    wdn_sl[dpb].ap().rearrange("p (c m) -> p c m", m=128),
                    d["wdT"].rearrange("(c p) n -> p c n", p=128)[:, :, dc * 128:(dc + 1) * 128],
                    waits=(tok_wdn_used[dpb],) +
                    (bD if (s == 0 and dc < 2) else ()))
                for hc in range(nhc):
                    S.emit("tensor", lambda e, hc=hc, dpb=dpb, dsl=dsl: e.matmul(
                        pB[:, dsl], wdn_sl[dpb][:, hc * 128:(hc + 1) * 128],
                        mixed[:, hc * TS:(hc + 1) * TS],
                        start=(hc == 0), stop=False, skip_group_check=True),
                        waits=(tok_wdn, tok_mix_last,
                               tok_gev.get((nhc - 1, 7)) if dc == 0 else None,
                               tok_dn_ev.get(dc - 2)) if hc == 0 else ())
                tdn = None
                for gi in range(2):
                    tdn = S.emit("tensor", lambda e, gi=gi, dc=dc, dsl=dsl: e.matmul(
                        pB[:, dsl], dB_sb[gi][:, dc * 128:(dc + 1) * 128],
                        ya_sb[:, gi * TS:(gi + 1) * TS],
                        start=False, stop=(gi == 1), skip_group_check=True),
                        waits=(tok_ya_ev, tok_dB[gi]) if gi == 0 else (),
                        inc=TM if gi == 1 else None)
                tok_wdn_used[dpb] = tdn
                tok_dn_ev[dc] = S.emit("scalar", lambda e, dc=dc, dsl=dsl: e.copy(
                    dout[:, dc * TS:(dc + 1) * TS], pB[:, dsl]), waits=(tdn,), inc=SS)
            tok_mix_rd = tdn
            prev_dn_ev = {0: tok_dn_ev[14], 1: tok_dn_ev[15]}


            if stop_e == "down":
                S.emit("sync", lambda e: None,
                       waits=((GD, S.cnt[GD]), (SD, S.cnt[SD]), (VV, S.cnt[VV]),
                              (SS, S.cnt[SS]), (TM, S.cnt[TM])))
                S.build(blk)
                return nc, sorted(dbg_tensors.keys())
            pTd = pT.ap().rearrange("p (c t) -> p c t", t=128)
            for tc in range(4):
                fp = tc % 2
                it = s * 4 + tc
                ty = s_dma(y1e[fp][:, :], out_d[it * 128:(it + 1) * 128, :],
                           waits=(tok_D_y1_last,
                                  tok_out[-2] if len(tok_out) >= 2 else None) +
                           (bD if (s == 0 and tc < 2) else ()))
                ttr = None
                for dc in range(16):
                    ttr = S.emit("tensor", lambda e, dc=dc, tc=tc: e.transpose(
                        pTd[:, dc, :], dout[:, dc * TS + tc * 128: dc * TS + tc * 128 + 128],
                        id_bf[:, :]),
                        waits=(tok_dn_ev[15], tok_tfa_prev,
                               tok_C_pT if (s == 0 and tc == 0) else None)
                        if dc == 0 else (),
                        inc=TM if dc == 15 else None)
                if stop_e == "fin1":
                    S.emit("sync", lambda e: None,
                           waits=((GD, S.cnt[GD]), (SD, S.cnt[SD]), (VV, S.cnt[VV]),
                                  (SS, S.cnt[SS]), (TM, S.cnt[TM])))
                    S.build(blk)
                    return nc, sorted(dbg_tensors.keys())
                tfa = S.emit("vector", lambda e, fp=fp: e.tensor_add(
                    oste[fp][:, :], y1e[fp][:, :], pT[:, :]),
                    waits=(ttr, ty), inc=VV)
                tok_tfa_prev = tfa
                if stop_e == "fin2":
                    S.emit("sync", lambda e: None,
                           waits=((GD, S.cnt[GD]), (SD, S.cnt[SD]), (VV, S.cnt[VV]),
                                  (SS, S.cnt[SS]), (TM, S.cnt[TM])))
                    S.build(blk)
                    return nc, sorted(dbg_tensors.keys())
                tok_final_add[fp] = tfa
                tok_out.append(s_dma(out_d[it * 128:(it + 1) * 128, :], oste[fp][:, :],
                                     waits=(tfa,)))

        S.emit("sync", lambda e: None,
               waits=tuple(tok_out[-2:]) + ((GD, S.cnt[GD]), (VV, S.cnt[VV]),
                                            (SS, S.cnt[SS]), (TM, S.cnt[TM])))

        S.build(blk)

    return nc, sorted(dbg_tensors.keys())


def _prep_inputs(inp):
    f32 = np.float32
    Wk = np.asarray(inp["L0_attn_Wk"], f32)
    Wv = np.asarray(inp["L0_attn_Wv"], f32)
    Wg = np.asarray(inp["L0_attn_Wg"], f32)
    Wp = np.asarray(inp["L0_attn_Wp"], f32)
    lq = np.asarray(inp["L0_attn_latent_queries"], f32)
    gw = np.asarray(inp["L0_ffn_gate_w"], f32)
    w_up = np.asarray(inp["L0_ffn_w_up"], f32)
    w_gate = np.asarray(inp["L0_ffn_w_gate"], f32)
    w_down = np.asarray(inp["L0_ffn_w_down"], f32)
    up_A = np.asarray(inp["L0_ffn_up_A"], f32)
    up_B = np.asarray(inp["L0_ffn_up_B"], f32)
    gate_A = np.asarray(inp["L0_ffn_gate_A"], f32)
    gate_B = np.asarray(inp["L0_ffn_gate_B"], f32)
    down_A = np.asarray(inp["L0_ffn_down_A"], f32)
    down_B = np.asarray(inp["L0_ffn_down_B"], f32)

    permd = np.concatenate([np.arange(0, HD, 2), np.arange(1, HD, 2)])
    perm = np.concatenate([h * HD + permd for h in range(NH)])
    wk_p = np.ascontiguousarray(Wk[:, perm])
    lq_p = np.ascontiguousarray(lq[0].transpose(1, 2, 0)[:, permd, :])

    def a_dup(A):
        T1 = np.zeros((C, 128), f32)
        T2 = np.zeros((C, 128), f32)
        T1[:, 0:16] = A[0].T
        T1[:, 32:48] = A[0].T; T1[:, 48:64] = A[1].T
        T1[:, 64:80] = A[1].T; T1[:, 80:96] = A[2].T
        T1[:, 96:112] = A[2].T; T1[:, 112:128] = A[3].T
        T2[:, 0:16] = A[3].T; T2[:, 16:32] = A[4].T
        T2[:, 32:48] = A[4].T; T2[:, 48:64] = A[5].T
        T2[:, 64:80] = A[5].T; T2[:, 80:96] = A[6].T
        T2[:, 96:112] = A[6].T; T2[:, 112:128] = A[7].T
        return np.ascontiguousarray(T1), np.ascontiguousarray(T2)

    def b_dup(B):
        Bt = B.transpose(0, 2, 1)
        T1 = np.zeros((128, DH), f32)
        T2 = np.zeros((128, DH), f32)
        T1[0:16] = Bt[0]
        T1[32:48] = Bt[0]; T1[48:64] = Bt[1]
        T1[64:80] = Bt[1]; T1[80:96] = Bt[2]
        T1[96:112] = Bt[2]; T1[112:128] = Bt[3]
        T2[0:16] = Bt[3]; T2[16:32] = Bt[4]
        T2[32:48] = Bt[4]; T2[48:64] = Bt[5]
        T2[64:80] = Bt[5]; T2[80:96] = Bt[6]
        T2[96:112] = Bt[6]; T2[112:128] = Bt[7]
        return np.ascontiguousarray(T1), np.ascontiguousarray(T2)

    aup1, aup2 = a_dup(up_A)
    ag1, ag2 = a_dup(gate_A)
    bup1, bup2 = b_dup(up_B)
    bg1, bg2 = b_dup(gate_B)
    dAT = np.ascontiguousarray(down_A.transpose(2, 0, 1).reshape(DH, 128))
    dBA = np.zeros((128, C), f32)
    dBB = np.zeros((128, C), f32)
    for e in range(4):
        dBA[32 * e:32 * e + 16] = down_B[e].T
        dBB[32 * e:32 * e + 16] = down_B[4 + e].T

    idn = np.eye(128, dtype=f32)
    lrow = np.arange(128, dtype=np.int64) % 64
    trow = np.arange(128, dtype=np.int64)
    maskp = np.where(trow[None, :] <= lrow[:, None], 0.0, NEGM).astype(f32)
    inv_freq = (1.0 / (10000.0 ** (np.arange(0, NL).astype(f32) * 2.0 / HD))).astype(f32)
    ang = (np.arange(128).astype(f32)[:, None] * inv_freq[None, :]).astype(f32)
    cosT = np.cos(ang).astype(f32)
    sinT = np.sin(ang).astype(f32)
    iota8 = np.ascontiguousarray(
        np.tile(np.arange(NE, dtype=f32), (128, 16)).reshape(128, 128))
    iotaB = (iota8 + BIGI).astype(f32)
    zeros16 = np.zeros((RK, 128), f32)

    return dict(
        wk=wk_p, wv=np.ascontiguousarray(Wv), wg=np.ascontiguousarray(Wg),
        wp=np.ascontiguousarray(Wp), lq=lq_p,
        gwT=np.ascontiguousarray(gw.T),
        wupT=np.ascontiguousarray(w_up.T), wgT=np.ascontiguousarray(w_gate.T),
        wdT=np.ascontiguousarray(w_down.T),
        aup1=aup1, aup2=aup2, ag1=ag1, ag2=ag2,
        bup1=bup1, bup2=bup2, bg1=bg1, bg2=bg2,
        dAT=dAT, dBA=dBA, dBB=dBB,
        idn=idn, maskp=maskp, cosT=cosT, sinT=sinT,
        iota8=iota8, iotaB=iotaB, zeros16=zeros16,
    )


def kernel(**inputs):
    x = np.asarray(inputs["L0_attn_x"], np.float32)
    shared = _prep_inputs(inputs)
    nc, dbg_names = build_program(debug=_DEBUG)
    in_maps = []
    for i in range(NCORE):
        m = dict(shared)
        m["x"] = np.ascontiguousarray(x[i])
        in_maps.append(m)
    res = run_bass_kernel_spmd(nc, in_maps, core_ids=list(range(NCORE)))
    out = np.stack([res.results[i]["out"] for i in range(NCORE)], axis=0)
    if _DEBUG:
        kernel._dbg = [{n: res.results[i]["dbg_" + n] for n in dbg_names}
                       for i in range(NCORE)]
    return out


# version 04 (final)
# speedup vs baseline: 1.7757x; 1.2367x over previous
import numpy as np
import ml_dtypes
from contextlib import ExitStack
import concourse.bass as bass
import concourse.mybir as mybir
from concourse.bass_utils import run_bass_kernel_spmd

F32 = mybir.dt.float32
BF16 = mybir.dt.bfloat16
AF = mybir.ActivationFunctionType
ALU = mybir.AluOpType
AX = mybir.AxisListType

T = 2048
C = 2048
NH = 16
HD = 128
NL = 64
NE = 8
RK = 16
DH = 5632
NHC = DH // 128
NT = T // 128
TS = 512
NSUB = T // TS
NCORE = 8

SCALE = float(1.0 / np.sqrt(128.0))
NEGM = float(-1e9 / SCALE)
ALPHA = 2.0
EPS = 1e-5
BIGI = 1.0e6

_DEBUG = []


class Sched:
    ENGINES = ("sync", "gpsimd", "tensor", "vector", "scalar")

    def __init__(self, nc, ctx):
        self.nc = nc
        self.ctx = ctx
        self.ops = {e: [] for e in self.ENGINES}
        self.sems = {}
        self.cnt = {}

    def _sem(self, name):
        if name not in self.sems:
            self.sems[name] = self.ctx.enter_context(self.nc.semaphore(name))
            self.cnt[name] = 0
        return self.sems[name]

    def emit(self, engine, fn, waits=(), inc=None, inc_by=1):
        import inspect
        tok = None
        if inc is not None:
            self._sem(inc)
            self.cnt[inc] += inc_by
            tok = (inc, self.cnt[inc])
        ws = tuple(w for w in waits if w is not None)
        for s, _ in ws:
            self._sem(s)
        lineno = inspect.currentframe().f_back.f_lineno
        self.ops[engine].append((ws, fn, inc, inc_by, lineno))
        return tok

    def build(self, block):
        def mk(engine):
            def body(eng):
                last_wait = {}
                for idx, (ws, fn, inc, inc_by, lineno) in enumerate(self.ops[engine]):
                    for (s, v) in ws:
                        if last_wait.get(s, -1) >= v:
                            continue
                        eng.wait_ge(self.sems[s], v)
                        last_wait[s] = v
                    try:
                        inst = fn(eng)
                    except Exception as ex:
                        raise RuntimeError(
                            f"op failed on {engine} idx {idx} (line {lineno}): {ex}"
                        ) from ex
                    if inc is not None:
                        assert inst is not None
                        inst.then_inc(self.sems[inc], inc_by)
            return body
        block.sync(mk("sync"))
        block.gpsimd(mk("gpsimd"))
        block.tensor(mk("tensor"))
        block.vector(mk("vector"))
        block.scalar(mk("scalar"))


def build_program(debug=(), stop_after="E", nsub=NSUB, nhc=NHC, stop_e=None):
    nc = bass.Bass("TRN2", target_bir_lowering=False, debug=False)
    dbg = set(debug)

    d = {}
    def din(name, shape):
        d[name] = nc.dram_tensor(name, list(shape), F32, kind="ExternalInput").ap()
    din("x", (T, C))
    din("wk", (C, C))
    din("wv", (C, C))
    din("wg", (C, NH * NL))
    din("wp", (C, C))
    din("lq", (NH, HD, NL))
    din("gwT", (C, NE))
    din("wupT", (C, DH))
    din("wgT", (C, DH))
    din("wdT", (DH, C))
    din("aup1", (C, 128)); din("aup2", (C, 128))
    din("ag1", (C, 128)); din("ag2", (C, 128))
    din("bup1", (128, DH)); din("bup2", (128, DH))
    din("bg1", (128, DH)); din("bg2", (128, DH))
    din("dAT", (DH, 128))
    din("dBA", (128, C)); din("dBB", (128, C))
    din("idn", (128, 128))
    din("maskp", (128, 128))
    din("cosT", (128, NL)); din("sinT", (128, NL))
    din("iota8", (128, 128)); din("iotaB", (128, 128))
    din("zeros16", (RK, 128))

    out_d = nc.dram_tensor("out", [T, C], F32, kind="ExternalOutput").ap()

    xsp_d = nc.dram_tensor("xspill", [128, NT, C], BF16, kind="Internal").ap()
    bscr_d = [nc.dram_tensor(f"bscr{i}", [128, DH], BF16, kind="Internal").ap()
              for i in range(4)]
    cT_d = nc.dram_tensor("cTd", [128, 128], BF16, kind="Internal").ap()
    st_d = nc.dram_tensor("std", [1, 8], F32, kind="Internal").ap()

    dbg_tensors = {}
    def ddbg(name, shape, dtype=F32):
        if name in dbg:
            dbg_tensors[name] = nc.dram_tensor(
                "dbg_" + name, list(shape), dtype, kind="ExternalOutput").ap()
    ddbg("xn0", (128, C))
    ddbg("glog", (128, 128))
    ddbg("csel", (128, 128))
    ddbg("kf", (128, C))
    ddbg("krope", (128, C), BF16)
    ddbg("vbf", (128, C), BF16)
    ddbg("attn", (128, 8 * 128), BF16)
    ddbg("gfm", (128, 8 * T), BF16)
    ddbg("wtd", (128, NH * TS), BF16)
    ddbg("eg15", (128, NH * NL), BF16)
    ddbg("wgsb", (128, 2048), BF16)
    ddbg("xa0", (128, TS), BF16)
    ddbg("mix0", (128, NHC * TS), BF16)
    ddbg("ya0", (128, 2 * TS), BF16)

    with ExitStack() as ctx:
        E = ctx.enter_context
        S = Sched(nc, ctx)
        GD, SD, TM, VV, SS = "gd", "sd", "tm", "vv", "ss"

        id_bf = E(nc.sbuf_tensor("id_bf", [128, 128], BF16))
        mask_t = E(nc.sbuf_tensor("mask_t", [128, 128], F32))
        cos_t = E(nc.sbuf_tensor("cos_t", [128, NL], F32))
        sin_t = E(nc.sbuf_tensor("sin_t", [128, NL], F32))
        iota_t = E(nc.sbuf_tensor("iota_t", [128, 128], F32))
        iotaB_t = E(nc.sbuf_tensor("iotaB_t", [128, 128], F32))
        z16_t = E(nc.sbuf_tensor("z16_t", [RK, 128], BF16))
        lq_t = E(nc.sbuf_tensor("lq_t", [128, NH * NL], BF16))
        gwT_t = E(nc.sbuf_tensor("gwT_t", [128, 16 * NE], BF16))
        attn_t = E(nc.sbuf_tensor("attn_t", [128, 8 * 128], BF16))
        glog_t = E(nc.sbuf_tensor("glog_t", [128, 128], F32))
        csel_t = E(nc.sbuf_tensor("csel_t", [128, 128], F32))
        cm_t = E(nc.sbuf_tensor("cm_t", [128, 96], F32))
        stt_t = E(nc.sbuf_tensor("stt_t", [128, 16], F32))
        mxb_t = E(nc.sbuf_tensor("mxb_t", [128, 8], F32))

        pA = E(nc.psum_tensor("pA", [128, 1024], F32))
        pB = E(nc.psum_tensor("pB", [128, 1024], F32))
        pC = E(nc.psum_tensor("pC", [128, 1024], F32))
        pT = E(nc.psum_tensor("pT", [128, 2048], BF16))

        blk = E(nc.Block())

        def g_dma(out, in_, waits=()):
            return S.emit("gpsimd", lambda e, o=out, i=in_: e.dma_start(out=o, in_=i),
                          waits=waits, inc=GD, inc_by=16)

        def s_dma(out, in_, waits=()):
            return S.emit("sync", lambda e, o=out, i=in_: e.dma_start(out=o, in_=i),
                          waits=waits, inc=SD, inc_by=16)

        tk_id = g_dma(id_bf[:, :], d["idn"][:, :])
        tk_mask = s_dma(mask_t[:, :], d["maskp"][:, :])
        tk_cos = s_dma(cos_t[:, :], d["cosT"][:, :])
        tk_sin = s_dma(sin_t[:, :], d["sinT"][:, :])
        tk_iota = s_dma(iota_t[:, :], d["iota8"][:, :])
        tk_iotaB = s_dma(iotaB_t[:, :], d["iotaB"][:, :])
        tk_z16 = g_dma(z16_t[:, :], d["zeros16"][:, :])
        tk_lq = g_dma(lq_t.ap().rearrange("p (h l) -> p h l", l=NL),
                      d["lq"].rearrange("h p l -> p h l"))
        tk_gwT = g_dma(gwT_t.ap().rearrange("p (c e) -> p c e", e=NE),
                       d["gwT"].rearrange("(c p) e -> p c e", p=128))

        NEG1 = (32, 64, 96)
        NEG2 = (0, 32, 64, 96)
        with nc.sbuf_tensor("bstg", [128, DH], BF16) as bstg, \
             nc.sbuf_tensor("bneg", [128, DH], BF16) as bneg:
            tok_bscr = None
            tok_bneg = None
            for i, name in enumerate(("bup1", "bup2", "bg1", "bg2")):
                tld = g_dma(bstg[:, :], d[name][:, :],
                            waits=(tok_bscr,) if tok_bscr else ())
                tok_bneg = S.emit("vector",
                                  lambda e: e.tensor_scalar_mul(bneg[:, :], bstg[:, :], -1.0),
                                  waits=(tld,), inc=VV)
                negs = NEG1 if (i % 2 == 0) else NEG2
                for lo in range(0, 128, 16):
                    src = bneg if lo in negs else bstg
                    tok_bscr = s_dma(bscr_d[i][lo:lo + 16, :], src[lo:lo + 16, :],
                                     waits=(tok_bneg,))

        es_gfm = ExitStack()
        Gfm = es_gfm.enter_context(nc.sbuf_tensor("Gfm", [128, 8 * T], BF16))
        es_x = ExitStack()
        X_sb = es_x.enter_context(nc.sbuf_tensor("X_sb", [128, NT * C], BF16))
        with nc.sbuf_tensor("wg_sb", [128, 16 * NH * NL], BF16) as wg_sb, \
             nc.sbuf_tensor("xt0", [128, C], F32) as xt0, \
             nc.sbuf_tensor("xt1", [128, C], F32) as xt1, \
             nc.sbuf_tensor("xn_bf", [128, C], BF16) as xn_bf, \
             nc.sbuf_tensor("E_bf", [128, NH * NL], BF16) as E_bf, \
             nc.sbuf_tensor("Gn_bf", [128, NH * NL], BF16) as Gn_bf, \
             nc.sbuf_tensor("lnsc", [128, 8], F32) as lnsc, \
             nc.sbuf_tensor("sqd", [128, C], F32) as sqd, \
             nc.sbuf_tensor("grec", [128, NH], F32) as grec:

            tok_wg = None
            for c in range(16):
                tok_wg = g_dma(wg_sb[:, c * NH * NL:(c + 1) * NH * NL],
                               d["wg"][c * 128:(c + 1) * 128, :],
                               waits=(tok_bscr, tok_bneg) if c == 0 else ())

            X3 = X_sb.ap().rearrange("p (c t) -> p c t", t=T)
            pT3 = pT.ap().rearrange("p (c t) -> p c t", t=128)
            Gfm3 = Gfm.ap().rearrange("p (h t) -> p h t", t=T)

            tok_x = [None, None]
            tok_xn = [None, None]
            tok_xev = None
            tok_gev = None
            tok_exp = None
            tok_glev = None
            tok_gtr = None
            tok_gn = None

            for it in range(NT):
                pb = it % 2
                xt = (xt0, xt1)[pb]
                tok_x[pb] = s_dma(xt[:, :], d["x"][it * 128:(it + 1) * 128, :],
                                  waits=(tok_xn[pb],))
                S.emit("vector", lambda e, x=xt: e.reduce_sum(
                    lnsc[:, 0:1], x[:, :], axis=AX.X), waits=(tok_x[pb],))
                S.emit("vector", lambda e: e.drain())
                t2 = S.emit("vector", lambda e: e.tensor_scalar_mul(
                    lnsc[:, 1:2], lnsc[:, 0:1], -1.0 / C), inc=VV)
                t3 = S.emit("scalar", lambda e, x=xt: e.activation(
                    sqd[:, :], x[:, :], AF.Square, bias=lnsc[:, 1:2],
                    accum_out=lnsc[:, 2:3]), waits=(t2,), inc=SS)
                S.emit("vector", lambda e: e.tensor_scalar(
                    lnsc[:, 3:4], lnsc[:, 2:3], 1.0 / C, EPS,
                    op0=ALU.mult, op1=ALU.add), waits=(t3,))
                S.emit("vector", lambda e: e.drain())
                t5 = S.emit("vector", lambda e: e.reciprocal(
                    lnsc[:, 4:5], lnsc[:, 3:4]), inc=VV)
                t6 = S.emit("scalar", lambda e: e.activation(
                    lnsc[:, 5:6], lnsc[:, 4:5], AF.Sqrt), waits=(t5,), inc=SS)
                tok_xn[pb] = S.emit("vector", lambda e, x=xt: e.tensor_scalar(
                    xn_bf[:, :], x[:, :], lnsc[:, 1:2], lnsc[:, 5:6],
                    op0=ALU.add, op1=ALU.mult), waits=(t6,), inc=VV)
                if it == 0 and "xn0" in dbg:
                    tdbg = S.emit("vector", lambda e: e.tensor_copy(
                        sqd[:, :], xn_bf[:, :]), waits=(tok_xn[pb],), inc=VV)
                    s_dma(dbg_tensors["xn0"][:, :], sqd[:, :], waits=(tdbg,))
                tt = None
                for c in range(16):
                    tt = S.emit("tensor", lambda e, c=c: e.transpose(
                        pT3[:, c, :], xn_bf[:, c * 128:(c + 1) * 128], id_bf[:, :]),
                        waits=(tok_xn[pb], tk_id, tok_gev) if c == 0 else (),
                        inc=TM if c == 15 else None)
                tok_xev = S.emit("scalar", lambda e, it=it: e.copy(
                    X3[:, :, it * 128:(it + 1) * 128], pT3[:, :, :]),
                    waits=(tt,), inc=SS)
                tg = None
                for c in range(16):
                    tg = S.emit("tensor", lambda e, c=c, it=it: e.matmul(
                        pB[:, 512:520],
                        X_sb[:, c * C + it * 128: c * C + it * 128 + 128],
                        gwT_t[:, c * NE:(c + 1) * NE],
                        start=(c == 0), stop=(c == 15), skip_group_check=True),
                        waits=(tok_xev, tok_glev, tk_gwT) if c == 0 else (),
                        inc=TM if c == 15 else None)
                tok_glev = S.emit("vector", lambda e, it=it: e.tensor_copy(
                    glog_t[:, it * NE:(it + 1) * NE], pB[:, 512:520]),
                    waits=(tg,), inc=VV)
                tG = None
                for c in range(16):
                    for nh in range(2):
                        tG = S.emit("tensor", lambda e, c=c, it=it, nh=nh: e.matmul(
                            pA[:, nh * 512:(nh + 1) * 512],
                            X_sb[:, c * C + it * 128: c * C + it * 128 + 128],
                            wg_sb[:, c * NH * NL + nh * 512: c * NH * NL + (nh + 1) * 512],
                            start=(c == 0), stop=(c == 15), skip_group_check=True),
                            waits=(tok_xev, tok_exp, tok_wg) if (c == 0 and nh == 0) else (),
                            inc=TM if (c == 15 and nh == 1) else None)
                tok_exp = S.emit("scalar", lambda e: e.activation(
                    E_bf[:, :], pA[:, :], AF.Exp, scale=SCALE),
                    waits=(tG, tok_gn), inc=SS)
                S.emit("vector", lambda e: e.reduce_sum(
                    grec.ap().rearrange("p (h o) -> p h o", o=1),
                    E_bf.ap().rearrange("p (h l) -> p h l", l=NL), axis=AX.X),
                    waits=(tok_exp,))
                S.emit("vector", lambda e: e.drain())
                S.emit("vector", lambda e: e.reciprocal(grec[:, :], grec[:, :]), inc=VV)
                S.emit("vector", lambda e: e.drain())
                tok_gn = S.emit("vector", lambda e: e.tensor_tensor(
                    Gn_bf.ap().rearrange("p (h l) -> p h l", l=NL),
                    E_bf.ap().rearrange("p (h l) -> p h l", l=NL),
                    grec.ap().rearrange("p (h o) -> p h o", o=1).to_broadcast([128, NH, NL]),
                    ALU.mult), waits=(tok_gtr,), inc=VV)
                tok_gtr = None
                for pr in range(8):
                    tok_gtr = S.emit("tensor", lambda e, pr=pr: e.transpose(
                        pT3[:, pr, :], Gn_bf[:, pr * 128:(pr + 1) * 128], id_bf[:, :]),
                        waits=(tok_gn, tok_xev) if pr == 0 else (),
                        inc=TM if pr == 7 else None)
                tok_gev = S.emit("scalar", lambda e, it=it: e.copy(
                    Gfm3[:, :, it * 128:(it + 1) * 128], pT3[:, 0:8, :]),
                    waits=(tok_gtr,), inc=SS)

            if "wgsb" in dbg:
                s_dma(dbg_tensors["wgsb"][:, :], wg_sb[:, 0:2048], waits=(tok_wg,))
            if "eg15" in dbg:
                s_dma(dbg_tensors["eg15"][:, :], E_bf[:, :], waits=(tok_gn,))
            tok_A_gev = tok_gev
            tok_A_glev = tok_glev
            tok_A_exp = tok_exp
            tok_A_xev = tok_xev
            tok_A_gtr = tok_gtr
            tok_A_gn = tok_gn
            bA = (tok_A_gev, tok_A_glev, tok_A_exp, tok_A_xev, tok_A_gtr, tok_A_gn)
            if "glog" in dbg:
                s_dma(dbg_tensors["glog"][:, :], glog_t[:, :], waits=(tok_glev,))

        with nc.sbuf_tensor("wkv0", [128, C], BF16) as wkv0, \
             nc.sbuf_tensor("wkv1", [128, C], BF16) as wkv1, \
             nc.sbuf_tensor("kf", [128, C], F32) as kf, \
             nc.sbuf_tensor("k_bf", [128, C], BF16) as k_bf, \
             nc.sbuf_tensor("v_bf", [128, C], BF16) as v_bf, \
             nc.sbuf_tensor("kfm", [128, C], BF16) as kfm, \
             nc.sbuf_tensor("rta", [128, NH * NL], F32) as rta, \
             nc.sbuf_tensor("rtb", [128, NH * NL], F32) as rtb, \
             nc.sbuf_tensor("wsc", [128, 128], F32) as wsc, \
             nc.sbuf_tensor("we", [128, 128], F32) as we, \
             nc.sbuf_tensor("wn", [128, 128], BF16) as wn, \
             nc.sbuf_tensor("wT", [128, 128], BF16) as wT, \
             nc.sbuf_tensor("scs", [128, 8], F32) as scs:

            tok_kv = {}
            for which, wname in (("k", "wk"), ("v", "wv")):
                tok_w = [None, None]
                tchunk = {}
                tmm = None
                for c in range(16):
                    wb = (wkv0, wkv1)[c % 2]
                    extra = (tok_kv["k"],) if (which == "v" and c < 2) else ()
                    tok_w[c % 2] = g_dma(
                        wb[:, :], d[wname][c * 128:(c + 1) * 128, :],
                        waits=(tchunk.get(c - 2),) + extra +
                        (bA if (which == "k" and c < 2) else ()))
                    for nq in range(4):
                        ps = (pA, pA, pB, pB)[nq]
                        sl = (slice(0, 512), slice(512, 1024))[nq % 2]
                        tmm = S.emit("tensor", lambda e, c=c, wb=wb, ps=ps, sl=sl, nq=nq: e.matmul(
                            ps[:, sl], X_sb[:, c * C: c * C + 128],
                            wb[:, nq * 512:(nq + 1) * 512],
                            start=(c == 0), stop=(c == 15), skip_group_check=True),
                            waits=(tok_w[c % 2], tok_A_exp, tok_A_glev,
                                   tok_kv.get(which)) if nq == 0 else (),
                            inc=TM if nq == 3 else None)
                    tchunk[c] = tmm
                if which == "k":
                    S.emit("vector", lambda e: e.tensor_copy(kf[:, 0:1024], pA[:, :]),
                           waits=(tmm,))
                    tok_kv["k"] = S.emit("vector", lambda e: e.tensor_copy(
                        kf[:, 1024:2048], pB[:, :]), inc=VV)
                else:
                    S.emit("vector", lambda e: e.tensor_copy(v_bf[:, 0:1024], pA[:, :]),
                           waits=(tmm,))
                    tok_kv["v"] = S.emit("vector", lambda e: e.tensor_copy(
                        v_bf[:, 1024:2048], pB[:, :]), inc=VV)

            kf3 = kf.ap().rearrange("p (h d) -> p h d", d=HD)
            kb3 = k_bf.ap().rearrange("p (h d) -> p h d", d=HD)
            cosb = cos_t.ap().rearrange("p (o j) -> p o j", o=1).to_broadcast([128, NH, NL])
            sinb = sin_t.ap().rearrange("p (o j) -> p o j", o=1).to_broadcast([128, NH, NL])
            rta3 = rta.ap().rearrange("p (h j) -> p h j", j=NL)
            rtb3 = rtb.ap().rearrange("p (h j) -> p h j", j=NL)
            S.emit("vector", lambda e: e.tensor_tensor(
                rta3[:, :, :], kf3[:, :, 0:NL], cosb, ALU.mult),
                waits=(tok_kv["k"], tk_cos, tk_sin))
            S.emit("vector", lambda e: e.tensor_tensor(
                rtb3[:, :, :], kf3[:, :, NL:HD], sinb, ALU.mult))
            S.emit("vector", lambda e: e.tensor_tensor(
                kb3[:, :, 0:NL], rta3[:, :, :], rtb3[:, :, :], ALU.subtract), inc=VV)
            S.emit("vector", lambda e: e.tensor_tensor(
                rta3[:, :, :], kf3[:, :, 0:NL], sinb, ALU.mult))
            S.emit("vector", lambda e: e.tensor_tensor(
                rtb3[:, :, :], kf3[:, :, NL:HD], cosb, ALU.mult))
            tok_rope = S.emit("vector", lambda e: e.tensor_tensor(
                kb3[:, :, NL:HD], rta3[:, :, :], rtb3[:, :, :], ALU.add), inc=VV)
            if "kf" in dbg:
                s_dma(dbg_tensors["kf"][:, :], kf[:, :], waits=(tok_kv["k"],))
            if "krope" in dbg:
                s_dma(dbg_tensors["krope"][:, :], k_bf[:, :], waits=(tok_rope,))
            if "vbf" in dbg:
                s_dma(dbg_tensors["vbf"][:, :], v_bf[:, :], waits=(tok_kv["v"],))

            pT3b = pT.ap().rearrange("p (h t) -> p h t", t=128)
            tt = None
            for h in range(NH):
                tt = S.emit("tensor", lambda e, h=h: e.transpose(
                    pT3b[:, h, :], k_bf[:, h * 128:(h + 1) * 128], id_bf[:, :]),
                    waits=(tok_rope, tok_A_gev) if h == 0 else (),
                    inc=TM if h == 15 else None)
            tok_kfm = S.emit("scalar", lambda e: e.copy(kfm[:, :], pT[:, :]),
                             waits=(tt,), inc=SS)

            tok_attn_ev = None
            tok_twe = None
            tok_tm1 = None
            for pr in range(8):
                tsc = None
                for sub in range(2):
                    h = pr * 2 + sub
                    tsc = S.emit("tensor", lambda e, h=h, sub=sub: e.matmul(
                        pC[sub * 64:(sub + 1) * 64, 0:128],
                        lq_t[:, h * NL:(h + 1) * NL],
                        kfm[:, h * 128:(h + 1) * 128],
                        start=True, stop=True, skip_group_check=True),
                        waits=(tok_kfm, tk_lq, tok_tm1) if sub == 0 else (),
                        inc=TM if sub == 1 else None)
                tok_tm1 = S.emit("vector", lambda e: e.tensor_add(
                    wsc[:, :], pC[:, 0:128], mask_t[:, :]),
                    waits=(tsc, tk_mask), inc=VV)
                S.emit("vector", lambda e: e.drain())
                tm2 = S.emit("vector", lambda e: e.reduce_max(
                    scs[:, 0:1], wsc[:, :], axis=AX.X, negate=True), inc=VV)
                S.emit("scalar", lambda e: e.mul(scs[:, 1:2], scs[:, 0:1], SCALE),
                       waits=(tm2,))
                S.emit("scalar", lambda e: e.drain())
                tm4 = S.emit("scalar", lambda e: e.activation(
                    we[:, :], wsc[:, :], AF.Exp, bias=scs[:, 1:2], scale=SCALE,
                    accum_out=scs[:, 2:3]), inc=SS)
                S.emit("vector", lambda e: e.reciprocal(scs[:, 3:4], scs[:, 2:3]),
                       waits=(tm4,))
                S.emit("vector", lambda e: e.drain())
                tm6 = S.emit("vector", lambda e: e.tensor_scalar_mul(
                    wn[:, :], we[:, :], scs[:, 3:4]), inc=VV)
                tw = S.emit("tensor", lambda e: e.transpose(
                    pT3b[:, 0, :], wn[:, :], id_bf[:, :]),
                    waits=(tm6, tok_kfm, tok_twe), inc=TM)
                tok_twe = S.emit("scalar", lambda e: e.copy(wT[:, :], pT3b[:, 0, :]),
                                 waits=(tw,), inc=SS)
                ta = None
                for sub in range(2):
                    h = pr * 2 + sub
                    ta = S.emit("tensor", lambda e, h=h, sub=sub: e.matmul(
                        pC[sub * 64:(sub + 1) * 64, 512:640],
                        wT[:, sub * 64:(sub + 1) * 64],
                        v_bf[:, h * 128:(h + 1) * 128],
                        start=True, stop=True, skip_group_check=True),
                        waits=(tok_twe, tok_kv["v"], tok_attn_ev) if sub == 0 else (),
                        inc=TM if sub == 1 else None)
                tok_attn_ev = S.emit("scalar", lambda e, pr=pr: e.copy(
                    attn_t[:, pr * 128:(pr + 1) * 128], pC[:, 512:640]),
                    waits=(ta,), inc=SS)
            if "attn" in dbg:
                s_dma(dbg_tensors["attn"][:, :], attn_t[:, :], waits=(tok_attn_ev,))
            if "gfm" in dbg:
                s_dma(dbg_tensors["gfm"][:, :], Gfm[:, :], waits=(tok_A_gev,))
            tok_B_kv_v = tok_kv["v"]
            tok_B_kfm = tok_kfm
            tok_B_twe = tok_twe

        tok_xspill = s_dma(xsp_d[:, :, :],
                           X_sb.ap().rearrange("p (c t) -> p c t", t=T),
                           waits=(tok_A_xev,))

        g3 = glog_t.ap().rearrange("p (a b) -> p a b", b=NE)
        i3 = iota_t.ap().rearrange("p (a b) -> p a b", b=NE)
        def cmb(lo, hi):
            return cm_t.ap()[:, lo:hi].rearrange(
                "p (a b) -> p a b", b=1).to_broadcast([128, 16, NE])
        def cm3(lo, hi):
            return cm_t.ap()[:, lo:hi].rearrange("p (a b) -> p a b", b=1)

        with nc.sbuf_tensor("eqw", [128, 128], F32) as eqw, \
             nc.sbuf_tensor("ind0", [128, 128], F32) as ind0, \
             nc.sbuf_tensor("ind1", [128, 128], F32) as ind1, \
             nc.sbuf_tensor("mskg", [128, 128], F32) as mskg, \
             nc.sbuf_tensor("sttb", [128, 16], BF16) as sttb, \
             nc.sbuf_tensor("rowsc", [16, 128], F32) as rowsc, \
             nc.sbuf_tensor("cselb", [128, 128], BF16) as cselb:
            eq3 = eqw.ap().rearrange("p (a b) -> p a b", b=NE)
            i03 = ind0.ap().rearrange("p (a b) -> p a b", b=NE)
            i13 = ind1.ap().rearrange("p (a b) -> p a b", b=NE)
            mg3 = mskg.ap().rearrange("p (a b) -> p a b", b=NE)

            S.emit("vector", lambda e: e.tensor_reduce(
                cm3(0, 16), g3[:, :, :], axis=AX.X, op=ALU.max),
                waits=(tok_A_glev, tok_attn_ev))
            S.emit("vector", lambda e: e.drain())
            S.emit("vector", lambda e: e.tensor_tensor(
                eq3[:, :, :], g3[:, :, :], cmb(0, 16), ALU.is_equal))
            S.emit("vector", lambda e: e.scalar_tensor_tensor(
                mskg[:, :], eqw[:, :], -BIGI, iotaB_t[:, :], ALU.mult, ALU.add),
                waits=(tk_iotaB,))
            S.emit("vector", lambda e: e.drain())
            S.emit("vector", lambda e: e.tensor_reduce(
                cm3(64, 80), mg3[:, :, :], axis=AX.X, op=ALU.min))
            S.emit("vector", lambda e: e.drain())
            S.emit("vector", lambda e: e.tensor_tensor(
                i03[:, :, :], i3[:, :, :], cmb(64, 80), ALU.is_equal),
                waits=(tk_iota,))
            S.emit("vector", lambda e: e.scalar_tensor_tensor(
                mskg[:, :], ind0[:, :], -1.0e5, glog_t[:, :], ALU.mult, ALU.add))
            S.emit("vector", lambda e: e.drain())
            S.emit("vector", lambda e: e.tensor_reduce(
                cm3(16, 32), mg3[:, :, :], axis=AX.X, op=ALU.max))
            S.emit("vector", lambda e: e.drain())
            S.emit("vector", lambda e: e.tensor_tensor(
                eq3[:, :, :], mg3[:, :, :], cmb(16, 32), ALU.is_equal))
            S.emit("vector", lambda e: e.scalar_tensor_tensor(
                mskg[:, :], eqw[:, :], -BIGI, iotaB_t[:, :], ALU.mult, ALU.add))
            S.emit("vector", lambda e: e.drain())
            S.emit("vector", lambda e: e.tensor_reduce(
                cm3(64, 80), mg3[:, :, :], axis=AX.X, op=ALU.min))
            S.emit("vector", lambda e: e.drain())
            tok_ind1 = S.emit("vector", lambda e: e.tensor_tensor(
                i13[:, :, :], i3[:, :, :], cmb(64, 80), ALU.is_equal), inc=VV)

            S.emit("vector", lambda e: e.reduce_max(
                stt_t[:, 0:1], cm_t[:, 0:16], axis=AX.X))
            tok_pm = S.emit("vector", lambda e: e.reduce_max(
                stt_t[:, 1:2], cm_t[:, 16:32], axis=AX.X), inc=VV)
            S.emit("vector", lambda e: e.drain())
            tc1 = S.emit("vector", lambda e: e.tensor_copy(
                sttb[:, :], stt_t[:, 0:16]), inc=VV)
            tc2 = S.emit("tensor", lambda e: e.transpose(
                pT.ap()[0:16, 0:128], sttb[:, :], id_bf[:, :]),
                waits=(tc1, tok_B_twe), inc=TM)
            tc3 = S.emit("scalar", lambda e: e.copy(
                rowsc[:, :], pT.ap()[0:16, 0:128]), waits=(tc2,), inc=SS)
            tc4 = S.emit("vector", lambda e: e.reduce_max(
                stt_t[0:2, 2:3], rowsc[0:2, :], axis=AX.X, negate=True),
                waits=(tc3,), inc=VV)
            tc5 = s_dma(st_d[0:1, 0:2].rearrange("o p -> p o"),
                        stt_t.ap()[0:2, 2:3], waits=(tc4,))
            tc6 = s_dma(mxb_t[:, 0:2], st_d.partition_broadcast(128)[:, 0, 0:2],
                        waits=(tc5,))
            tc7 = S.emit("scalar", lambda e: e.activation(
                cm_t[:, 32:48], cm_t[:, 0:16], AF.Exp, bias=mxb_t[:, 0:1]),
                waits=(tc6,))
            tc8 = S.emit("scalar", lambda e: e.activation(
                cm_t[:, 48:64], cm_t[:, 16:32], AF.Exp, bias=mxb_t[:, 1:2]), inc=SS)
            S.emit("vector", lambda e: e.reduce_sum(
                stt_t[:, 4:5], cm_t[:, 32:48], axis=AX.X), waits=(tc8,))
            tok_ps = S.emit("vector", lambda e: e.reduce_sum(
                stt_t[:, 5:6], cm_t[:, 48:64], axis=AX.X), inc=VV)
            S.emit("vector", lambda e: e.drain())
            tc11 = S.emit("vector", lambda e: e.tensor_copy(
                sttb[:, 0:2], stt_t[:, 4:6]), inc=VV)
            tc12 = S.emit("tensor", lambda e: e.transpose(
                pT.ap()[0:16, 128:256], sttb[:, :], id_bf[:, :]),
                waits=(tc11, tc3), inc=TM)
            tc13 = S.emit("scalar", lambda e: e.copy(
                rowsc[:, :], pT.ap()[0:16, 128:256]), waits=(tc12,), inc=SS)
            S.emit("vector", lambda e: e.reduce_sum(
                stt_t[0:2, 6:7], rowsc[0:2, :], axis=AX.X), waits=(tc13,))
            S.emit("vector", lambda e: e.drain())
            tc15 = S.emit("vector", lambda e: e.reciprocal(
                stt_t[0:2, 7:8], stt_t[0:2, 6:7]), inc=VV)
            tc16 = s_dma(st_d[0:1, 2:4].rearrange("o p -> p o"),
                         stt_t.ap()[0:2, 7:8], waits=(tc15,))
            tc17 = s_dma(mxb_t[:, 2:4], st_d.partition_broadcast(128)[:, 0, 2:4],
                         waits=(tc16,))
            S.emit("vector", lambda e: e.tensor_tensor(
                mg3[:, :, :], i03[:, :, :], cmb(32, 48), ALU.mult),
                waits=(tc17, tok_ind1))
            S.emit("vector", lambda e: e.tensor_scalar_mul(
                ind0[:, :], mskg[:, :], mxb_t[:, 2:3]))
            S.emit("vector", lambda e: e.tensor_tensor(
                eq3[:, :, :], i13[:, :, :], cmb(48, 64), ALU.mult))
            S.emit("vector", lambda e: e.tensor_scalar_mul(
                eqw[:, :], eqw[:, :], mxb_t[:, 3:4]))
            u5 = S.emit("vector", lambda e: e.tensor_add(
                csel_t[:, :], ind0[:, :], eqw[:, :]), inc=VV)
            if "csel" in dbg:
                s_dma(dbg_tensors["csel"][:, :], csel_t[:, :], waits=(u5,))
            u6 = S.emit("vector", lambda e: e.tensor_copy(
                cselb[:, :], csel_t[:, :]), waits=(u5,), inc=VV)
            u7 = S.emit("tensor", lambda e: e.transpose(
                pT.ap()[:, 256:384], cselb[:, :], id_bf[:, :]),
                waits=(u6, tc12), inc=TM)
            u8 = S.emit("scalar", lambda e: e.copy(
                cselb[:, :], pT.ap()[:, 256:384]), waits=(u7,), inc=SS)
            tok_cT = s_dma(cT_d[:, :], cselb[:, :], waits=(u8,))
            tok_C_pT = u8
            tok_C_vv = u6
        es_x.close()
        bC = (tok_cT, u8, u7, tok_C_vv, tok_attn_ev, tok_B_kv_v, tok_B_twe)

        if stop_after in ("A", "B", "C"):
            S.emit("sync", lambda e: None,
                   waits=((GD, S.cnt[GD]), (SD, S.cnt[SD]), (VV, S.cnt[VV]),
                          (SS, S.cnt[SS]), (TM, S.cnt[TM])))
            S.build(blk)
            es_gfm.close()
            return nc, sorted(dbg_tensors.keys())

        with nc.sbuf_tensor("wp_sb", [128, 16 * C], BF16) as wp_sb, \
             nc.sbuf_tensor("wtd", [128, NH * TS], BF16) as wtd, \
             nc.sbuf_tensor("xt0d", [128, C], F32) as xt0d, \
             nc.sbuf_tensor("xt1d", [128, C], F32) as xt1d, \
             nc.sbuf_tensor("y1a", [128, C], F32) as y1a, \
             nc.sbuf_tensor("y1b", [128, C], F32) as y1b:
            tok_wp = None
            for c in range(16):
                tok_wp = g_dma(wp_sb[:, c * C:(c + 1) * C],
                               d["wp"][c * 128:(c + 1) * 128, :],
                               waits=(bC + ((GD, S.cnt[GD]),)) if c == 0 else ())
            tok_xd = [None, None]
            tok_y1 = [None, None]
            wtd_ev = {0: None, 1: None}
            tok_wtd_last = None
            tok_D_pBC = None
            for sd in range(NSUB):
                tok_wtd_ev = None
                for h in range(NH):
                    hb = h % 2
                    sl = slice(hb * 512, (hb + 1) * 512)
                    pr, sub = h // 2, h % 2
                    tw = S.emit("tensor", lambda e, pr=pr, sub=sub, sd=sd, sl=sl: e.matmul(
                        pA[:, sl],
                        attn_t[sub * 64:(sub + 1) * 64, pr * 128:(pr + 1) * 128],
                        Gfm[sub * 64:(sub + 1) * 64, pr * T + sd * TS: pr * T + (sd + 1) * TS],
                        start=True, stop=True, skip_group_check=True),
                        waits=bC + (tok_A_gev, wtd_ev[hb],
                                    tok_wtd_last if h <= 1 else None),
                        inc=TM)
                    ev = S.emit("scalar", lambda e, h=h, sl=sl: e.copy(
                        wtd[:, h * TS:(h + 1) * TS], pA[:, sl]), waits=(tw,), inc=SS)
                    wtd_ev[hb] = ev
                    tok_wtd_ev = ev
                if "wtd" in dbg and sd == 0:
                    s_dma(dbg_tensors["wtd"][:, :], wtd[:, :], waits=(tok_wtd_ev,))
                for tc in range(4):
                    it = sd * 4 + tc
                    pbx = it % 2
                    xt = (xt0d, xt1d)[pbx]
                    y1 = (y1a, y1b)[pbx]
                    tok_xd[pbx] = s_dma(xt[:, :], d["x"][it * 128:(it + 1) * 128, :],
                                        waits=(tok_y1[pbx],) + (bC if it < 2 else ()))
                    tmm = None
                    for c in range(16):
                        for dc in range(4):
                            ps = (pB, pB, pC, pC)[dc]
                            sl = (slice(0, 512), slice(512, 1024))[dc % 2]
                            tmm = S.emit("tensor", lambda e, c=c, dc=dc, tc=tc, ps=ps, sl=sl: e.matmul(
                                ps[:, sl],
                                wtd[:, c * TS + tc * 128: c * TS + tc * 128 + 128],
                                wp_sb[:, c * C + dc * 512: c * C + (dc + 1) * 512],
                                start=(c == 0), stop=(c == 15), skip_group_check=True),
                                waits=(tok_wtd_ev, tok_wp, tok_B_kv_v, tok_attn_ev,
                                       tok_D_pBC) if (c == 0 and dc == 0) else (),
                                inc=TM if (c == 15 and dc == 3) else None)
                    if tc == 3:
                        tok_wtd_last = tmm
                    S.emit("vector", lambda e, y1=y1, xt=xt: e.tensor_add(
                        y1[:, 0:1024], pB[:, :], xt[:, 0:1024]),
                        waits=(tmm, tok_xd[pbx]))
                    tok_D_pBC = S.emit("vector", lambda e, y1=y1, xt=xt: e.tensor_add(
                        y1[:, 1024:2048], pC[:, :], xt[:, 1024:2048]), inc=VV)
                    tok_y1[pbx] = s_dma(out_d[it * 128:(it + 1) * 128, :], y1[:, :],
                                        waits=(tok_D_pBC,))
            tok_D_y1_last = tok_y1[1] if tok_y1[1] else tok_y1[0]
            tok_D_pA = wtd_ev
        es_gfm.close()
        bD = (tok_D_y1_last, tok_D_pBC, tok_wtd_last, tok_D_pA[0], tok_D_pA[1],
              tok_wp)

        if stop_after == "D":
            S.emit("sync", lambda e: None,
                   waits=((GD, S.cnt[GD]), (SD, S.cnt[SD]), (VV, S.cnt[VV]),
                          (SS, S.cnt[SS]), (TM, S.cnt[TM])))
            S.build(blk)
            return nc, sorted(dbg_tensors.keys())

        XsE = E(nc.sbuf_tensor("XsE", [128, 16 * TS], BF16))
        mixed = E(nc.sbuf_tensor("mixed", [128, NHC * TS], BF16))
        cbc = E(nc.sbuf_tensor("cbc", [128, NE * TS], BF16))
        xa_sb = [E(nc.sbuf_tensor(f"xa{j}", [128, TS], BF16)) for j in range(4)]
        A_sb = [E(nc.sbuf_tensor(f"A_sb{j}", [128, 16 * 128], BF16)) for j in range(4)]
        wup_sl = [E(nc.sbuf_tensor(f"wup{j}", [128, 16 * 128], BF16)) for j in range(2)]
        wg_sl = [E(nc.sbuf_tensor(f"wgs{j}", [128, 16 * 128], BF16)) for j in range(2)]
        b_sl = [E(nc.sbuf_tensor(f"bsl{j}", [128, 4 * 128], BF16)) for j in range(2)]
        dA_sl = [E(nc.sbuf_tensor(f"dAs{j}", [128, 128], BF16)) for j in range(2)]
        wdn_sl = [E(nc.sbuf_tensor(f"wdn{j}", [128, NHC * 128], BF16)) for j in range(2)]
        su_bf = [E(nc.sbuf_tensor(f"su_bf{j}", [128, TS], BF16)) for j in range(2)]
        g_bfb = [E(nc.sbuf_tensor(f"g_bfb{j}", [128, TS], BF16)) for j in range(2)]
        h_bf = E(nc.sbuf_tensor("h_bf", [128, TS], BF16))
        ch_bf = [E(nc.sbuf_tensor(f"ch{j}", [128, TS], BF16)) for j in range(8)]
        macc = [E(nc.sbuf_tensor(f"macc{j}", [128, TS], F32)) for j in range(2)]
        ya_sb = E(nc.sbuf_tensor("ya_sb", [128, 2 * TS], BF16))
        dB_sb = [E(nc.sbuf_tensor(f"dB{j}", [128, C], BF16)) for j in range(2)]
        dout = E(nc.sbuf_tensor("dout", [128, 16 * TS], BF16))
        y1e = [E(nc.sbuf_tensor(f"y1e{j}", [128, C], F32)) for j in range(2)]
        oste = [E(nc.sbuf_tensor("ost0", [128, C], F32))] * 2

        tok_A_ld = []
        for j, nm in enumerate(("aup1", "aup2", "ag1", "ag2")):
            ta = None
            for c in range(16):
                ta = g_dma(A_sb[j][:, c * 128:(c + 1) * 128],
                           d[nm][c * 128:(c + 1) * 128, :],
                           waits=bD if (j == 0 and c == 0) else ())
            tok_A_ld.append(ta)
        tok_dB = [g_dma(dB_sb[0][:, :], d["dBA"][:, :]),
                  g_dma(dB_sb[1][:, :], d["dBB"][:, :])]

        TRANS = [(0, 0, 16), (0, 32, 32), (0, 64, 32), (0, 96, 32),
                 (1, 0, 32), (1, 32, 32), (1, 64, 32), (1, 96, 32)]

        tok_Xs = None
        tok_base_last = {0: None, 1: None}
        tok_xa_rd = None
        tok_cb_rd = None
        tok_ya_ev = None
        tok_mix_rd = None
        tok_out = []
        tok_w_used = {0: None, 1: None}
        tok_b_used = {0: None, 1: None}
        tok_dA_used = {0: None, 1: None}
        tok_wdn_used = {0: None, 1: None}
        tok_final_add = {0: None, 1: None}
        tok_tfa_prev = None
        prev_dn_ev = {0: None, 1: None}

        for s in range(nsub):
            tok_Xs = None
            for c in range(16):
                tok_Xs = s_dma(XsE[:, c * TS:(c + 1) * TS],
                               xsp_d[:, c, s * TS:(s + 1) * TS],
                               waits=((tok_xspill, tok_base_last[0],
                                       tok_base_last[1], tok_xa_rd) +
                                      (bD if s == 0 else ())) if c == 0 else ())
            tcb = None
            for e8 in range(NE):
                for q in range(4):
                    row = (4 * s + q) * NE + e8
                    tcb = s_dma(cbc[:, e8 * TS + q * 128: e8 * TS + (q + 1) * 128],
                                cT_d.partition_broadcast(128)[:, row, :],
                                waits=((tok_cT, tok_cb_rd) + (bD if s == 0 else ()))
                                if (e8 == 0 and q == 0) else ())
            tok_cbc = tcb
            tok_xa_ev = []
            for j in range(4):
                half = j % 2
                tj = None
                for c in range(16):
                    tj = S.emit("tensor", lambda e, j=j, c=c, half=half: e.matmul(
                        pA[:, half * 512:(half + 1) * 512],
                        A_sb[j][:, c * 128:(c + 1) * 128],
                        XsE[:, c * TS:(c + 1) * TS],
                        start=(c == 0), stop=(c == 15), skip_group_check=True),
                        waits=(tok_Xs, tok_A_ld[j],
                               tok_xa_ev[j - 2] if j >= 2 else
                               (tok_D_pA[half] if s == 0 else None)) if c == 0 else (),
                        inc=TM if c == 15 else None)
                ev = S.emit("scalar", lambda e, j=j, half=half: e.activation(
                    xa_sb[j][:, :], pA[:, half * 512:(half + 1) * 512],
                    AF.Copy, scale=ALPHA), waits=(tj,), inc=SS)
                tok_xa_ev.append(ev)
            if "xa0" in dbg and s == 0:
                s_dma(dbg_tensors["xa0"][:, :], xa_sb[0][:, :], waits=(tok_xa_ev[0],))


            if stop_e == "xa":
                S.emit("sync", lambda e: None,
                       waits=((GD, S.cnt[GD]), (SD, S.cnt[SD]), (VV, S.cnt[VV]),
                              (SS, S.cnt[SS]), (TM, S.cnt[TM])))
                S.build(blk)
                return nc, sorted(dbg_tensors.keys())
            S.emit("tensor", lambda e: e.matmul(
                pC[:, 0:512], z16_t[:, :], xa_sb[0][0:16, :],
                start=True, stop=True, skip_group_check=True),
                waits=(tok_xa_ev[0], tk_z16, tok_ya_ev,
                       tok_D_pBC if s == 0 else None, tok_attn_ev if s == 0 else None))
            tok_zya = S.emit("tensor", lambda e: e.matmul(
                pC[:, 512:1024], z16_t[:, :], xa_sb[0][0:16, :],
                start=True, stop=True, skip_group_check=True), inc=TM)


            if stop_e == "zero":
                S.emit("sync", lambda e: None,
                       waits=((GD, S.cnt[GD]), (SD, S.cnt[SD]), (VV, S.cnt[VV]),
                              (SS, S.cnt[SS]), (TM, S.cnt[TM])))
                S.build(blk)
                return nc, sorted(dbg_tensors.keys())
            tok_silu = {}
            tok_gev = {}
            tok_h = {}
            tok_ch = {}
            tok_ya_par = {}
            tok_ya = None
            tok_trans_g_last = None
            tok_mix_last = None
            tok_base_done = {}
            tok_wslice = {}
            tok_bslice = {}
            tok_dAs = {}
            tok_tch_last = None

            def emit_wdma(hc):
                pb = hc % 2
                tw = g_dma(
                    wup_sl[pb].ap().rearrange("p (c m) -> p c m", m=128),
                    d["wupT"].rearrange("(c p) h -> p c h", p=128)[:, :, hc * 128:(hc + 1) * 128],
                    waits=(tok_w_used[pb],) + (bD if (s == 0 and hc < 2) else ()))
                tw2 = g_dma(
                    wg_sl[pb].ap().rearrange("p (c m) -> p c m", m=128),
                    d["wgT"].rearrange("(c p) h -> p c h", p=128)[:, :, hc * 128:(hc + 1) * 128])
                tok_wslice[hc] = (tw, tw2)

            def emit_bdma(hc):
                pb = hc % 2
                tb = None
                for i in range(4):
                    tb = s_dma(b_sl[pb][:, i * 128:(i + 1) * 128],
                               bscr_d[i][:, hc * 128:(hc + 1) * 128],
                               waits=((tok_bscr, tok_b_used[pb]) +
                                      (bD if (s == 0 and hc < 2) else ()))
                               if i == 0 else ())
                tok_bslice[hc] = tb
                tok_dAs[hc] = g_dma(dA_sl[pb][:, :],
                                    d["dAT"][hc * 128:(hc + 1) * 128, :],
                                    waits=(tok_dA_used[pb],) +
                                    (bD if (s == 0 and hc < 2) else ()))

            def emit_base(hc, c0, c1):
                pb = hc % 2
                usl = slice(pb * 512, (pb + 1) * 512)
                tw, tw2 = tok_wslice[hc]
                for c in range(c0, c1):
                    S.emit("tensor", lambda e, c=c, usl=usl, pb=pb: e.matmul(
                        pA[:, usl], wup_sl[pb][:, c * 128:(c + 1) * 128],
                        XsE[:, c * TS:(c + 1) * TS],
                        start=(c == 0), stop=False, skip_group_check=True),
                        waits=(tw, tok_Xs,
                               tok_silu.get((hc - 2, 7)) if hc >= 2 else
                               tok_xa_ev[2 + pb]) if c == 0 else ())
                    tlast = S.emit("tensor", lambda e, c=c, usl=usl, pb=pb: e.matmul(
                        pB[:, usl], wg_sl[pb][:, c * 128:(c + 1) * 128],
                        XsE[:, c * TS:(c + 1) * TS],
                        start=(c == 0), stop=False, skip_group_check=True),
                        waits=(tw2,
                               tok_gev.get((hc - 2, 7)) if hc >= 2 else None,
                               prev_dn_ev[pb] if (s > 0 and hc < 2) else None,
                               tok_D_pBC if (s == 0 and hc < 2) else None)
                        if c == 0 else (),
                        inc=TM if c == c1 - 1 else None)
                    if c == 15:
                        tok_base_done[hc] = tlast
                        tok_w_used[pb] = tlast

            def emit_ya(hc, e8):
                nonlocal tok_ya
                pb = hc % 2
                grp, off = e8 // 4, (e8 % 4) * 32
                kw2 = {"tile_position": (0, 96)} if off == 96 else {}
                tok_ya = S.emit("tensor", lambda e, grp=grp, off=off, e8=e8, pb=pb,
                                kw2=dict(kw2): e.matmul(
                    pC[:, grp * 512:(grp + 1) * 512][off:off + 16, :],
                    dA_sl[pb][:, e8 * 16:(e8 + 1) * 16],
                    ch_bf[e8][:, :],
                    start=False, stop=(hc == nhc - 1 and e8 == 7),
                    skip_group_check=True, **kw2),
                    waits=(tok_ch[(hc, e8)], tok_dAs[hc], tok_zya), inc=TM)
                tok_ya_par[e8] = tok_ya

            emit_wdma(0)
            emit_base(0, 0, 16)
            for hc in range(nhc):
                pb = hc % 2
                usl = slice(pb * 512, (pb + 1) * 512)
                if hc + 1 < nhc:
                    emit_wdma(hc + 1)
                emit_bdma(hc)
                tok_b = tok_bslice[hc]
                for e8 in range(NE):
                    ti, rlo, nr = TRANS[e8]
                    kw = {"tile_position": (96, 0)} if rlo == 96 else {}
                    tup = S.emit("tensor", lambda e, ti=ti, rlo=rlo, nr=nr, pb=pb,
                                 usl=usl, kw=dict(kw): e.matmul(
                        pA[:, usl],
                        b_sl[pb][rlo:rlo + nr, ti * 128:(ti + 1) * 128],
                        xa_sb[ti][rlo:rlo + nr, :],
                        start=False, stop=False, skip_group_check=True, **kw),
                        waits=(tok_b, tok_xa_ev[ti],
                               tok_silu.get((hc, e8 - 1)) if e8 > 0 else None),
                        inc=TM)
                    tgt = S.emit("tensor", lambda e, ti=ti, rlo=rlo, nr=nr, pb=pb,
                                 usl=usl, kw=dict(kw): e.matmul(
                        pB[:, usl],
                        b_sl[pb][rlo:rlo + nr, (2 + ti) * 128:(3 + ti) * 128],
                        xa_sb[2 + ti][rlo:rlo + nr, :],
                        start=False, stop=False, skip_group_check=True, **kw),
                        waits=(tok_xa_ev[2 + ti],
                               tok_gev.get((hc, e8 - 1)) if e8 > 0 else None),
                        inc=TM)
                    if e8 == 7:
                        tok_trans_g_last = tgt
                    if hc + 1 < nhc:
                        emit_base(hc + 1, 2 * e8, 2 * e8 + 2)
                    if hc > 0 and e8 == 0:
                        for e2 in range(NE):
                            emit_ya(hc - 1, e2)
                    par = e8 % 2
                    prev_h = (tok_h.get((hc, e8 - 2)) if e8 >= 2 else
                              tok_h.get((hc - 1, 6 + e8)))
                    ts1 = S.emit("scalar", lambda e, usl=usl, par=par: e.activation(
                        su_bf[par][:, :], pA[:, usl], AF.Silu),
                        waits=(tup, prev_h), inc=SS)
                    tok_silu[(hc, e8)] = ts1
                    tg1 = S.emit("scalar", lambda e, usl=usl, par=par: e.copy(
                        g_bfb[par][:, :], pB[:, usl]),
                        waits=(tgt,), inc=SS)
                    tok_gev[(hc, e8)] = tg1
                    th1 = S.emit("vector", lambda e, par=par: e.tensor_mul(
                        h_bf[:, :], su_bf[par][:, :], g_bfb[par][:, :]),
                        waits=(ts1, tg1), inc=VV)
                    tok_h[(hc, e8)] = th1
                    chb = ch_bf[e8]
                    tch = S.emit("vector", lambda e, e8=e8, chb=chb: e.tensor_mul(
                        chb[:, :], h_bf[:, :], cbc[:, e8 * TS:(e8 + 1) * TS]),
                        waits=(tok_cbc, tok_ya_par.get(e8)), inc=VV)
                    tok_ch[(hc, e8)] = tch
                    tok_tch_last = tch
                    if e8 == 0:
                        S.emit("vector", lambda e, chb=chb: e.tensor_copy(
                            macc[0][:, :], chb[:, :]))
                    elif e8 < 7:
                        S.emit("vector", lambda e, e8=e8, chb=chb: e.tensor_add(
                            macc[e8 % 2][:, :], macc[(e8 - 1) % 2][:, :], chb[:, :]))
                    else:
                        tok_mix_last = S.emit(
                            "vector", lambda e, hc=hc, chb=chb: e.tensor_add(
                                mixed[:, hc * TS:(hc + 1) * TS], macc[0][:, :], chb[:, :]),
                            waits=(tok_mix_rd if (s > 0 and hc == 0) else None,),
                            inc=VV)
                tok_b_used[pb] = tok_trans_g_last
                tok_dA_used[pb] = tok_ya
            for e2 in range(NE):
                emit_ya(nhc - 1, e2)
            tok_dA_used[(nhc - 1) % 2] = tok_ya
            tok_xa_rd = tok_trans_g_last
            tok_cb_rd = tok_tch_last
            tok_base_last[0] = tok_base_done.get(nhc - 1 - (nhc - 1) % 2)
            tok_base_last[1] = tok_base_done.get(nhc - 1 - (nhc) % 2)
            if "mix0" in dbg and s == 0:
                s_dma(dbg_tensors["mix0"][:, :], mixed[:, :], waits=(tok_mix_last,))

            S.emit("scalar", lambda e: e.activation(
                ya_sb[:, 0:TS], pC[:, 0:512], AF.Copy, scale=ALPHA),
                waits=(tok_ya,))
            tok_ya_ev = S.emit("scalar", lambda e: e.activation(
                ya_sb[:, TS:2 * TS], pC[:, 512:1024], AF.Copy, scale=ALPHA), inc=SS)
            if "ya0" in dbg and s == 0:
                s_dma(dbg_tensors["ya0"][:, :], ya_sb[:, :], waits=(tok_ya_ev,))


            if stop_e == "hloop":
                S.emit("sync", lambda e: None,
                       waits=((GD, S.cnt[GD]), (SD, S.cnt[SD]), (VV, S.cnt[VV]),
                              (SS, S.cnt[SS]), (TM, S.cnt[TM])))
                S.build(blk)
                return nc, sorted(dbg_tensors.keys())
            tok_dn_ev = {}
            for dc in range(16):
                dpb = dc % 2
                dsl = slice(dpb * 512, (dpb + 1) * 512)
                tok_wdn = g_dma(
                    wdn_sl[dpb].ap().rearrange("p (c m) -> p c m", m=128),
                    d["wdT"].rearrange("(c p) n -> p c n", p=128)[:, :, dc * 128:(dc + 1) * 128],
                    waits=(tok_wdn_used[dpb],) +
                    (bD if (s == 0 and dc < 2) else ()))
                for hc in range(nhc):
                    S.emit("tensor", lambda e, hc=hc, dpb=dpb, dsl=dsl: e.matmul(
                        pB[:, dsl], wdn_sl[dpb][:, hc * 128:(hc + 1) * 128],
                        mixed[:, hc * TS:(hc + 1) * TS],
                        start=(hc == 0), stop=False, skip_group_check=True),
                        waits=(tok_wdn, tok_mix_last,
                               tok_gev.get((nhc - 1, 7)) if dc == 0 else None,
                               tok_dn_ev.get(dc - 2)) if hc == 0 else ())
                tdn = None
                for gi in range(2):
                    tdn = S.emit("tensor", lambda e, gi=gi, dc=dc, dsl=dsl: e.matmul(
                        pB[:, dsl], dB_sb[gi][:, dc * 128:(dc + 1) * 128],
                        ya_sb[:, gi * TS:(gi + 1) * TS],
                        start=False, stop=(gi == 1), skip_group_check=True),
                        waits=(tok_ya_ev, tok_dB[gi]) if gi == 0 else (),
                        inc=TM if gi == 1 else None)
                tok_wdn_used[dpb] = tdn
                tok_dn_ev[dc] = S.emit("scalar", lambda e, dc=dc, dsl=dsl: e.copy(
                    dout[:, dc * TS:(dc + 1) * TS], pB[:, dsl]), waits=(tdn,), inc=SS)
            tok_mix_rd = tdn
            prev_dn_ev = {0: tok_dn_ev[14], 1: tok_dn_ev[15]}


            if stop_e == "down":
                S.emit("sync", lambda e: None,
                       waits=((GD, S.cnt[GD]), (SD, S.cnt[SD]), (VV, S.cnt[VV]),
                              (SS, S.cnt[SS]), (TM, S.cnt[TM])))
                S.build(blk)
                return nc, sorted(dbg_tensors.keys())
            pTd = pT.ap().rearrange("p (c t) -> p c t", t=128)
            for tc in range(4):
                fp = tc % 2
                it = s * 4 + tc
                ty = s_dma(y1e[fp][:, :], out_d[it * 128:(it + 1) * 128, :],
                           waits=(tok_D_y1_last,
                                  tok_out[-2] if len(tok_out) >= 2 else None) +
                           (bD if (s == 0 and tc < 2) else ()))
                ttr = None
                for dc in range(16):
                    ttr = S.emit("tensor", lambda e, dc=dc, tc=tc: e.transpose(
                        pTd[:, dc, :], dout[:, dc * TS + tc * 128: dc * TS + tc * 128 + 128],
                        id_bf[:, :]),
                        waits=(tok_dn_ev[15], tok_tfa_prev,
                               tok_C_pT if (s == 0 and tc == 0) else None)
                        if dc == 0 else (),
                        inc=TM if dc == 15 else None)
                if stop_e == "fin1":
                    S.emit("sync", lambda e: None,
                           waits=((GD, S.cnt[GD]), (SD, S.cnt[SD]), (VV, S.cnt[VV]),
                                  (SS, S.cnt[SS]), (TM, S.cnt[TM])))
                    S.build(blk)
                    return nc, sorted(dbg_tensors.keys())
                tfa = S.emit("vector", lambda e, fp=fp: e.tensor_add(
                    oste[fp][:, :], y1e[fp][:, :], pT[:, :]),
                    waits=(ttr, ty, tok_out[-1] if tok_out else None), inc=VV)
                tok_tfa_prev = tfa
                if stop_e == "fin2":
                    S.emit("sync", lambda e: None,
                           waits=((GD, S.cnt[GD]), (SD, S.cnt[SD]), (VV, S.cnt[VV]),
                                  (SS, S.cnt[SS]), (TM, S.cnt[TM])))
                    S.build(blk)
                    return nc, sorted(dbg_tensors.keys())
                tok_final_add[fp] = tfa
                tok_out.append(s_dma(out_d[it * 128:(it + 1) * 128, :], oste[fp][:, :],
                                     waits=(tfa,)))

        S.emit("sync", lambda e: None,
               waits=tuple(tok_out[-2:]) + ((GD, S.cnt[GD]), (VV, S.cnt[VV]),
                                            (SS, S.cnt[SS]), (TM, S.cnt[TM])))

        S.build(blk)

    return nc, sorted(dbg_tensors.keys())


def _prep_inputs(inp):
    f32 = np.float32
    Wk = np.asarray(inp["L0_attn_Wk"], f32)
    Wv = np.asarray(inp["L0_attn_Wv"], f32)
    Wg = np.asarray(inp["L0_attn_Wg"], f32)
    Wp = np.asarray(inp["L0_attn_Wp"], f32)
    lq = np.asarray(inp["L0_attn_latent_queries"], f32)
    gw = np.asarray(inp["L0_ffn_gate_w"], f32)
    w_up = np.asarray(inp["L0_ffn_w_up"], f32)
    w_gate = np.asarray(inp["L0_ffn_w_gate"], f32)
    w_down = np.asarray(inp["L0_ffn_w_down"], f32)
    up_A = np.asarray(inp["L0_ffn_up_A"], f32)
    up_B = np.asarray(inp["L0_ffn_up_B"], f32)
    gate_A = np.asarray(inp["L0_ffn_gate_A"], f32)
    gate_B = np.asarray(inp["L0_ffn_gate_B"], f32)
    down_A = np.asarray(inp["L0_ffn_down_A"], f32)
    down_B = np.asarray(inp["L0_ffn_down_B"], f32)

    permd = np.concatenate([np.arange(0, HD, 2), np.arange(1, HD, 2)])
    perm = np.concatenate([h * HD + permd for h in range(NH)])
    wk_p = np.ascontiguousarray(Wk[:, perm])
    lq_p = np.ascontiguousarray(lq[0].transpose(1, 2, 0)[:, permd, :])

    def a_dup(A):
        T1 = np.zeros((C, 128), f32)
        T2 = np.zeros((C, 128), f32)
        T1[:, 0:16] = A[0].T
        T1[:, 32:48] = A[0].T; T1[:, 48:64] = A[1].T
        T1[:, 64:80] = A[1].T; T1[:, 80:96] = A[2].T
        T1[:, 96:112] = A[2].T; T1[:, 112:128] = A[3].T
        T2[:, 0:16] = A[3].T; T2[:, 16:32] = A[4].T
        T2[:, 32:48] = A[4].T; T2[:, 48:64] = A[5].T
        T2[:, 64:80] = A[5].T; T2[:, 80:96] = A[6].T
        T2[:, 96:112] = A[6].T; T2[:, 112:128] = A[7].T
        return np.ascontiguousarray(T1), np.ascontiguousarray(T2)

    def b_dup(B):
        Bt = B.transpose(0, 2, 1)
        T1 = np.zeros((128, DH), f32)
        T2 = np.zeros((128, DH), f32)
        T1[0:16] = Bt[0]
        T1[32:48] = Bt[0]; T1[48:64] = Bt[1]
        T1[64:80] = Bt[1]; T1[80:96] = Bt[2]
        T1[96:112] = Bt[2]; T1[112:128] = Bt[3]
        T2[0:16] = Bt[3]; T2[16:32] = Bt[4]
        T2[32:48] = Bt[4]; T2[48:64] = Bt[5]
        T2[64:80] = Bt[5]; T2[80:96] = Bt[6]
        T2[96:112] = Bt[6]; T2[112:128] = Bt[7]
        return np.ascontiguousarray(T1), np.ascontiguousarray(T2)

    aup1, aup2 = a_dup(up_A)
    ag1, ag2 = a_dup(gate_A)
    bup1, bup2 = b_dup(up_B)
    bg1, bg2 = b_dup(gate_B)
    dAT = np.ascontiguousarray(down_A.transpose(2, 0, 1).reshape(DH, 128))
    dBA = np.zeros((128, C), f32)
    dBB = np.zeros((128, C), f32)
    for e in range(4):
        dBA[32 * e:32 * e + 16] = down_B[e].T
        dBB[32 * e:32 * e + 16] = down_B[4 + e].T

    idn = np.eye(128, dtype=f32)
    lrow = np.arange(128, dtype=np.int64) % 64
    trow = np.arange(128, dtype=np.int64)
    maskp = np.where(trow[None, :] <= lrow[:, None], 0.0, NEGM).astype(f32)
    inv_freq = (1.0 / (10000.0 ** (np.arange(0, NL).astype(f32) * 2.0 / HD))).astype(f32)
    ang = (np.arange(128).astype(f32)[:, None] * inv_freq[None, :]).astype(f32)
    cosT = np.cos(ang).astype(f32)
    sinT = np.sin(ang).astype(f32)
    iota8 = np.ascontiguousarray(
        np.tile(np.arange(NE, dtype=f32), (128, 16)).reshape(128, 128))
    iotaB = (iota8 + BIGI).astype(f32)
    zeros16 = np.zeros((RK, 128), f32)

    return dict(
        wk=wk_p, wv=np.ascontiguousarray(Wv), wg=np.ascontiguousarray(Wg),
        wp=np.ascontiguousarray(Wp), lq=lq_p,
        gwT=np.ascontiguousarray(gw.T),
        wupT=np.ascontiguousarray(w_up.T), wgT=np.ascontiguousarray(w_gate.T),
        wdT=np.ascontiguousarray(w_down.T),
        aup1=aup1, aup2=aup2, ag1=ag1, ag2=ag2,
        bup1=bup1, bup2=bup2, bg1=bg1, bg2=bg2,
        dAT=dAT, dBA=dBA, dBB=dBB,
        idn=idn, maskp=maskp, cosT=cosT, sinT=sinT,
        iota8=iota8, iotaB=iotaB, zeros16=zeros16,
    )


def kernel(**inputs):
    x = np.asarray(inputs["L0_attn_x"], np.float32)
    shared = _prep_inputs(inputs)
    nc, dbg_names = build_program(debug=_DEBUG)
    in_maps = []
    for i in range(NCORE):
        m = dict(shared)
        m["x"] = np.ascontiguousarray(x[i])
        in_maps.append(m)
    res = run_bass_kernel_spmd(nc, in_maps, core_ids=list(range(NCORE)))
    out = np.stack([res.results[i]["out"] for i in range(NCORE)], axis=0)
    if _DEBUG:
        kernel._dbg = [{n: res.results[i]["dbg_" + n] for n in dbg_names}
                       for i in range(NCORE)]
    return out
